# version 22
# baseline (speedup 1.0000x reference)
import numpy as np
import ml_dtypes
import concourse.bass as bass
import concourse.mybir as mybir
from concourse.bass_utils import run_bass_kernel_spmd
from contextlib import ExitStack

F32 = mybir.dt.float32
BF16 = mybir.dt.bfloat16
ALU = mybir.AluOpType
AF = mybir.ActivationFunctionType
AX = mybir.AxisListType

D = 1024
N = 4096
NB = 32
NOWN = 16
TOWN = NOWN * 128
NH = 16
DH = 64
H2 = 128
NE = 8
EPS = 1e-5
NEG = -1e30
Q_SCALE = 0.125
Q2_SCALE = float(np.float32(1.0) / np.sqrt(np.float32(128.0)))

BUILD_PHASES = 4


class Prog:
    ENGS = ("sync", "gpsimd", "scalar", "vector", "tensor")

    def __init__(self, nc, ctx):
        self.nc = nc
        self.sems = {}
        self.cnt = {}
        self.last_wait = {}
        for name in ("LD", "LW", "OUT", "LDA", "LDB", "LWA", "LWB", "PE", "DV", "AC", "PL"):
            self.sems[name] = ctx.enter_context(nc.semaphore(name))
            self.cnt[name] = 0
        self.sched = []

    DMA_SEMS = ("LD", "LW", "OUT", "LDA", "LDB", "LWA", "LWB")

    def step(self, eng, fn, waits=(), inc=None, n=1):
        ev = None
        waits = [(w[0], self.cnt[w[0]]) if (w is not None and w[0] in self.DMA_SEMS)
                 else w for w in waits if w is not None]
        if inc is not None:
            self.cnt[inc] += n
            ev = (inc, self.cnt[inc], n)
        self.sched.append((eng, fn, tuple(waits), ev))
        return (ev[0], ev[1]) if ev else None

    def flush(self):
        sched = self.sched
        self.sched = []
        if not sched:
            return
        with self.nc.Block() as block:
            for engname in self.ENGS:
                if not any(s[0] == engname for s in sched):
                    continue

                def make(en):
                    def body(eng):
                        for (e2, fn, waits, ev) in sched:
                            if e2 != en:
                                continue
                            for (s, v) in waits:
                                key = (en, s)
                                if self.last_wait.get(key, -1) < v:
                                    eng.wait_ge(self.sems[s], v)
                                    self.last_wait[key] = v
                            ins = fn(eng)
                            if ev is not None:
                                ins.then_inc(self.sems[ev[0]], ev[2])
                    return body

                getattr(block, engname)(make(engname))


def _bucket(n):
    n = np.asarray(n)
    small = n < 16
    nf = np.maximum(n, 1).astype(np.float32)
    val = 16 + (np.log(nf / np.float32(16.0)) / np.float32(np.log(8.0))
                * np.float32(16.0)).astype(np.int32)
    val = np.minimum(val, 31)
    return np.where(small, n, val)


def _host_constants(p):
    q_idx = np.arange(128)[:, None]
    s_idx = np.arange(128)[None, :]
    ohm_T = np.zeros((3, 33, 256), dtype=np.float32)
    for slot in range(3):
        dblk = 1 + p - slot
        n = 128 * dblk - 127 + np.arange(256)
        masked = n < 0
        bkt = _bucket(np.maximum(n, 0))
        for b in range(31):
            sel = (bkt == b) & (~masked)
            ohm_T[slot, b][sel] = 1.0
            ohm_T[slot, 31][sel] -= 1.0
        ohm_T[slot, 32][masked] = 1.0
    ca = np.zeros((2, 128, 128), dtype=np.float32)
    caus = np.where(s_idx > q_idx, np.float32(NEG), np.float32(0.0)).astype(np.float32)
    if p == 0:
        ca[0] = caus
        ca[1] = NEG
    else:
        ca[0] = 0.0
        ca[1] = caus
    ct = np.ascontiguousarray(ca.transpose(0, 2, 1))
    ident = np.eye(128, dtype=np.float32)
    e_one = np.zeros((8, 8 * 128), dtype=np.float32)
    for e in range(8):
        e_one[e, e * 128:(e + 1) * 128] = 1.0
    bf = ml_dtypes.bfloat16
    return {
        "ohm_T": ohm_T.astype(bf),
        "ca": ca, "ct": ct,
        "ident_f": ident, "ident_b": ident.astype(bf),
        "e_one": e_one.astype(bf),
    }


def build_program(debug=0):
    nc = bass.Bass("TRN2", target_bir_lowering=False)
    ctx = ExitStack()

    dT = lambda name, shape, dt=F32: nc.dram_tensor(name, shape, dt, kind="ExternalInput").ap()
    x_full = dT("x_full", [N, D])
    x_own = dT("x_own", [TOWN, D])
    Wq_d = dT("Wq", [D, D])
    Wkv_d = dT("Wkv", [D, 2 * DH])
    Wout_d = dT("Wout", [D, D])
    bout_d = dT("bout", [D])
    rel_d = dT("rel_emb", [32, 16])
    ln1w_d = dT("ln1_w", [D]); ln1b_d = dT("ln1_b", [D])
    ln2w_d = dT("ln2_w", [D]); ln2b_d = dT("ln2_b", [D])
    sim1_d = dT("smha_sim", [D, NE]); g1_d = dT("smha_gates", [NE])
    qp_d = dT("q_proj", [NE, D, H2]); kp_d = dT("k_proj", [NE, D, H2])
    vp_d = dT("v_proj", [NE, D, H2]); op_d = dT("o_proj", [NE, H2, D])
    sim2_d = dT("moe_sim", [D, NE]); g2_d = dT("moe_gates", [NE])
    w1_d = dT("w1", [NE, D, D]); w2_d = dT("w2", [NE, D, D])
    ohmT_d = dT("ohm_T", [3, 33, 256], BF16)
    ca_d = dT("ca", [2, 128, 128]); ct_d = dT("ct", [2, 128, 128])
    identf_d = dT("ident_f", [128, 128])
    identb_d = dT("ident_b", [128, 128], BF16)
    eone_d = dT("e_one", [8, 8 * 128], BF16)

    y = nc.dram_tensor("y", [TOWN, D], F32, kind="ExternalOutput").ap()
    dbg = {}
    if debug:
        def dout(name, shape, dt=F32):
            dbg[name] = nc.dram_tensor(name, shape, dt, kind="ExternalOutput").ap()
        dout("d_kaug", [65, N], BF16)
        dout("d_voaug", [128, 32 * 65], BF16)
        dout("d_k2", [128, N], BF16)
        dout("d_v2aug", [128, 32 * 129], BF16)
        dout("d_q2", [128, TOWN], BF16)
        dout("d_rw", [128, NOWN * 8])
        dout("d_xn", [128, NOWN * D])
        dout("d_lgraw", [128, 12 * 32])
        dout("d_n1", [128, 4096], BF16)
        dout("d_r2bc", [128, 512], BF16)
        dout("d_rn", [128, 12 * 4])
        dout("d_stash2", [128, 32])
        if debug >= 2:
            dout("d_hs", [128, NOWN * D])
            dout("d_n2T", [128, NOWN * D], BF16)
            dout("d_rw2", [128, NOWN * 8])
        if debug >= 3:
            dout("d_partial", [128, NOWN * D])

    E = ctx.enter_context

    kaug = E(nc.sbuf_tensor("s_kaug", [65, N], BF16))
    v_o_aug = E(nc.sbuf_tensor("v_o_aug", [128, 32 * 65], BF16))
    partial = E(nc.sbuf_tensor("partial", [128, NOWN * D], F32))
    rw_own = E(nc.sbuf_tensor("rw_own", [128, NOWN * 8], F32))
    rw2_own = E(nc.sbuf_tensor("rw2_own", [128, NOWN * 8], F32))
    ca_t = E(nc.sbuf_tensor("ca_t", [128, 2 * 128], F32))
    ct_t = E(nc.sbuf_tensor("ct_t", [128, 2 * 128], F32))
    ident_f = E(nc.sbuf_tensor("s_ident_f", [128, 128], F32))
    ident_b = E(nc.sbuf_tensor("s_ident_b", [128, 128], BF16))
    e_one = E(nc.sbuf_tensor("s_e_one", [8, 8 * 128], BF16))
    onesc_b = E(nc.sbuf_tensor("onesc_b", [128, 1], BF16))
    ones1_f = E(nc.sbuf_tensor("ones1_f", [1, 128], F32))
    sig1 = E(nc.sbuf_tensor("sig1", [128, 8], F32))
    sig2 = E(nc.sbuf_tensor("sig2", [128, 8], F32))
    rcol1 = E(nc.sbuf_tensor("rcol1", [128, 8], F32))
    rcol2 = E(nc.sbuf_tensor("rcol2", [128, 8], F32))
    sim1 = E(nc.sbuf_tensor("sim1", [128, 8 * 8], BF16))
    sim2 = E(nc.sbuf_tensor("sim2", [128, 8 * 8], BF16))
    lnc = E(nc.sbuf_tensor("lnc", [128, 32], F32))
    ln1w_c = lnc[:, 0:8]; ln1b_c = lnc[:, 8:16]
    ln2w_c = lnc[:, 16:24]; ln2b_c = lnc[:, 24:32]
    bout_bc = E(nc.sbuf_tensor("bout_bc", [128, D], BF16))
    emb_ext = E(nc.sbuf_tensor("emb_ext", [33, 16], BF16))
    rel_sb = E(nc.sbuf_tensor("rel_sb", [32, 16], F32))
    row512 = E(nc.sbuf_tensor("row512", [1, 512], F32))
    row512b = E(nc.sbuf_tensor("row512b", [1, 512], F32))
    lgT = E(nc.sbuf_tensor("lgT", [8, 512], F32))
    rwT = E(nc.sbuf_tensor("rwT", [8, 512], BF16))
    lg_tm = E(nc.sbuf_tensor("lg_tm", [128, 32], F32))
    rn_cols = E(nc.sbuf_tensor("rn_cols", [128, 4], F32))
    rwtm = rw2_own[:, 0:32]
    smalls = E(nc.sbuf_tensor("smalls", [128, 208], F32))
    m_all = E(nc.sbuf_tensor("m_all", [128, 16], F32))
    qaug = E(nc.sbuf_tensor("qaug", [65, 128], BF16))
    vfm = E(nc.sbuf_tensor("vfm", [64, 512], BF16))

    d_stash = E(nc.sbuf_tensor("d_stash", [128, 432], F32)) if (debug and BUILD_PHASES < 3) else None
    AB = E(nc.sbuf_tensor("AB", [128, 24576], BF16))
    AB2 = E(nc.sbuf_tensor("AB2", [128, 26656], BF16))
    AFa = E(nc.sbuf_tensor("AFa", [128, 3072], F32))

    k2_fm = AB2[:, 0:4096]
    v2_aug = AB2[:, 4096:4096 + 32 * 129]
    q2_fm = AB2[:, 8224:8224 + 2048]
    n2T = AB2[:, 10272:10272 + 16384]
    deltaT = AB2[:, 0:6144]
    deltaA = AB2[:, 6144:12288]
    woutv = AB2[:, 12288:12288 + 8192]
    p3scr = AB2[:, 20480:26656]
    wkv = AB[:, 23040:24064]

    pall = E(nc.psum_tensor("pall", [128, 3584], F32))
    pb = [pall[:, i * 512:(i + 1) * 512] for i in range(7)]
    pbt = E(nc.psum_tensor("pbt", [128, 1024], BF16))

    P = Prog(nc, ctx)
    S = P.step

    def mm(out, lhsT, rhs, start=True, stop=True, waits=(), inc="PE"):
        return S("tensor", lambda e, o=out, l=lhsT, r=rhs, a=start, b=stop:
                 e.matmul(out=o, lhsT=l, rhs=r, start=a, stop=b), waits, inc)

    def tr(out, in_, ident, waits=(), inc="PE"):
        return S("tensor", lambda e, o=out, i=in_, d=ident:
                 e.transpose(out=o, in_=i, identity=d), waits, inc)

    def act(out, in_, func=AF.Copy, bias=0.0, scale=1.0, accum=None, waits=(), inc="AC"):
        return S("scalar", lambda e, o=out, i=in_, f=func, b=bias, s=scale, a=accum:
                 e.activation(out=o, in_=i, func=f, bias=b, scale=s, accum_out=a), waits, inc)

    def dve(fn, waits=(), inc="DV"):
        return S("vector", fn, waits, inc)

    def pool(fn, waits=(), inc="PL"):
        return S("gpsimd", fn, waits, inc)

    def dma(out, in_, waits=(), q="sync", sem=None):
        s = sem or ("LW" if q == "gpsimd" else "LD")
        return S(q, lambda e, o=out, i=in_: e.dma_start(out=o, in_=i),
                 list(waits) + [(s, P.cnt[s])], s, 16)

    def dma_out(out, in_, waits=()):
        return S("sync", lambda e, o=out, i=in_: e.dma_start(out=o, in_=i),
                 list(waits) + [("OUT", P.cnt["OUT"])], "OUT", 16)

    ev_pb = {i: None for i in range(7)}
    state = {"pbt": None}

    ev_idf = dma(ident_f[:], identf_d)
    ev_idb = dma(ident_b[:], identb_d)
    ev_eone = dma(e_one[:], eone_d)
    ev_ca = [dma(ca_t[:, a * 128:(a + 1) * 128], ca_d[a]) for a in range(2)][-1]
    ev_ct = [dma(ct_t[:, a * 128:(a + 1) * 128], ct_d[a]) for a in range(2)][-1]
    ev_rel = dma(rel_sb[:], rel_d)
    ev_sim1 = dma(sim1[:].rearrange("p (k e) -> p k e", e=8), sim1_d.rearrange("(k p) e -> p k e", p=128), q="gpsimd")
    ev_sim2 = dma(sim2[:].rearrange("p (k e) -> p k e", e=8), sim2_d.rearrange("(k p) e -> p k e", p=128), q="gpsimd")
    _lnstage = AFa[0:8, 2048:2560]
    _lnd = []
    for i, srcd in enumerate((ln1w_d, ln1b_d, ln2w_d, ln2b_d)):
        _lnd.append(dma(_lnstage.rearrange("k (i c) -> k i c", c=128)[:, i, :],
                        srcd.rearrange("(k p) -> k p", p=128)))
    _lnt = []
    for i in range(4):
        _lnt.append(tr(pb[6][:, 120 + i * 8:120 + (i + 1) * 8],
                       _lnstage.rearrange("k (i c) -> k i c", c=128)[:, i, :],
                       ident_f[0:8, 0:8], waits=[_lnd[i], ev_idf, ev_pb[6]]))
    ev_lnc = act(lnc[:], pb[6][:, 120:152], waits=[_lnt[-1]])
    ev_pb[6] = ev_lnc
    ev_ln1 = [ev_lnc]
    ev_ln2 = [ev_lnc]
    ev_g1 = dma(smalls[0:1, 0:8], g1_d.rearrange("(o e) -> o e", o=1))
    ev_g2 = dma(smalls[0:1, 8:16], g2_d.rearrange("(o e) -> o e", o=1))
    ev_bout = dma(AFa[0:1, 0:1024], bout_d.rearrange("(o d) -> o d", o=1))

    ev_m1 = pool(lambda e: e.memset(onesc_b[:], 1.0))
    ev_m2 = pool(lambda e: e.memset(ones1_f[:], 1.0))
    ev_m3 = pool(lambda e: e.memset(kaug[64:65, :], 1.0))
    ev_m4 = pool(lambda e: e.memset(
        v_o_aug[:].rearrange("p (b c) -> p b c", c=65)[:, :, 64:65], 1.0))
    ev_m5 = pool(lambda e: e.memset(
        v2_aug.rearrange("p (b c) -> p b c", c=129)[:, :, 128:129], 1.0))

    def bcast_row(dst, src_row, n, evs):
        evm = mm(pb[6][:, 0:n], ones1_f[0:1, 0:128], src_row,
                 waits=list(evs) + [ev_m2, ev_pb[6]])
        ev = act(dst, pb[6][:, 0:n], waits=[evm])
        ev_pb[6] = ev
        return ev

    ev_s1 = act(smalls[0:1, 16:24], smalls[0:1, 0:8], AF.Sigmoid, waits=[ev_g1])
    ev_s2 = act(smalls[0:1, 24:32], smalls[0:1, 8:16], AF.Sigmoid, waits=[ev_s1, ev_g2])
    ev_sig1 = bcast_row(sig1[:], smalls[0:1, 16:24], 8, [ev_s1])
    ev_sig2 = bcast_row(sig2[:], smalls[0:1, 24:32], 8, [ev_s2])
    ev_bo1 = bcast_row(bout_bc[:, 0:512], AFa[0:1, 0:512], 512, [ev_bout])
    ev_bo2 = bcast_row(bout_bc[:, 512:1024], AFa[0:1, 512:1024], 512, [ev_bout])

    def sim_colnorm(dst, sim_sb, ev_sim, sq_off):
        sq = AB[:, sq_off:sq_off + 64]
        e1 = act(sq, sim_sb[:], AF.Square, waits=[ev_sim])
        e2 = mm(pb[5][0:1, 0:64], onesc_b[:], sq, waits=[e1, ev_m1, ev_pb[5]])
        row = row512[0:1, 0:64]
        e3 = act(row, pb[5][0:1, 0:64], waits=[e2])
        ev_pb[5] = e3
        rowv = row.rearrange("o (k e) -> o e k", k=8)
        ssr = row512b[0:1, 0:8]
        e4 = dve(lambda e, o=ssr, i=rowv: e.tensor_reduce(out=o, in_=i, axis=AX.X, op=ALU.add), waits=[e3])
        e5 = dve(lambda e, o=ssr: e.tensor_scalar_max(out=o, in0=o, scalar1=1e-24), waits=[e4])
        e6 = act(ssr, ssr, AF.Ln, waits=[e5])
        e7 = act(ssr, ssr, AF.Exp, scale=-0.5, waits=[e6])
        return bcast_row(dst, ssr, 8, [e7])

    ev_rc1 = sim_colnorm(rcol1[:], sim1, ev_sim1, 0)
    ev_rc2 = sim_colnorm(rcol2[:], sim2, ev_sim2, 64)

    ev_emb = act(emb_ext[0:32, :], rel_sb[:], AF.Copy, scale=8.0, waits=[ev_rel])
    ev_embm = pool(lambda e: e.memset(emb_ext[32:33, :], NEG))

    P.flush()

    kp = AB[:, 0:8192]
    vp = AB[:, 8192:16384]
    xn1T = AB[:, 16384:20480]
    xnb = AB[:, 20480:21504]
    sqs = AB[:, 21504:22016]
    rwb = AB[:, 22016:22528]
    r2bc = AB[:, 22528:23040]

    ev_kp = dma(kp.rearrange("p (e k h) -> p e k h", k=8, h=128), kp_d.rearrange("e (k p) h -> p e k h", p=128), q="gpsimd")
    ev_vp = dma(vp.rearrange("p (e k h) -> p e k h", k=8, h=128), vp_d.rearrange("e (k p) h -> p e k h", p=128), q="gpsimd")
    ev_wkv = dma(wkv.rearrange("p (k d) -> p k d", d=128), Wkv_d.rearrange("(k p) d -> p k d", p=128), q="gpsimd")
    ev_qp = None

    xin = [AFa[:, 0:1024], AFa[:, 1024:2048]]
    accA = AFa[:, 2048:2560]
    accB = AFa[:, 2560:3072]

    def ln_stats(xblk_f, cols, waits):
        c = cols
        bn6 = smalls[:, c:c + 12]
        mv = smalls[:, c + 12:c + 14]
        rstd = smalls[:, c + 14:c + 15]
        nmr = smalls[:, c + 15:c + 16]
        rstd2 = smalls[:, c + 16:c + 17]
        tmp = smalls[:, c + 17:c + 18]
        ru = smalls[:, c + 18:c + 19]
        e0 = dve(lambda e, o=bn6, i=xblk_f: e.bn_stats(out=o[:, 0:6], in_=i[:, 0:512]), waits=waits)
        e1 = dve(lambda e, o=bn6, i=xblk_f: e.bn_stats(out=o[:, 6:12], in_=i[:, 512:1024]), waits=waits + [e0])
        e2 = dve(lambda e, o=mv, i=bn6: e.bn_aggr(out=o, in_=i), waits=[e1])
        e3 = dve(lambda e, o=tmp, i=mv: e.tensor_scalar_add(out=o, in0=i[:, 1:2], scalar1=EPS), waits=[e2])
        e4 = act(rstd, tmp, AF.Ln, waits=[e3])
        e5 = act(rstd, rstd, AF.Exp, scale=-0.5, waits=[e4])
        e6 = dve(lambda e, o=nmr, i=mv: e.tensor_scalar_mul(
            out=o, in0=i[:, 0:1], scalar1=-1.0), waits=[e5, e2])
        e7 = act(ru, rstd, AF.Square, waits=[e5])
        e8 = dve(lambda e, o=tmp, i=mv, r=ru: e.tensor_tensor(out=o, in0=i[:, 1:2], in1=r, op=ALU.mult), waits=[e7, e6])
        e9 = dve(lambda e, o=tmp: e.tensor_scalar_add(out=o, in0=o, scalar1=EPS), waits=[e8])
        e10 = act(rstd2, tmp, AF.Ln, waits=[e9])
        e11 = act(rstd2, rstd2, AF.Exp, scale=-0.5, waits=[e10])
        return e6, rstd, nmr, rstd2, e11

    def gate_block(ev_ready, g, rncol, ev_rn, sigb, rcolb, rw_dst, cols, extra=()):
        c = cols
        t1 = smalls[:, c + 20:c + 28]
        t2 = smalls[:, c + 28:c + 36]
        m8 = smalls[:, c + 36:c + 44]
        af = smalls[:, c + 44:c + 45]
        den = smalls[:, c + 45:c + 46]
        e1 = dve(lambda e, o=g, r=rncol: e.tensor_scalar_mul(out=o, in0=o, scalar1=r),
                 waits=[ev_ready, ev_rn] + list(extra))
        e2 = dve(lambda e, o=g, r=rcolb: e.tensor_tensor(out=o, in0=o, in1=r, op=ALU.mult), waits=[e1])
        e3 = dve(lambda e, o=g, s=sigb: e.tensor_tensor(out=o, in0=o, in1=s, op=ALU.subtract), waits=[e2])
        e4 = dve(lambda e, o=m8, i=g: e.max(out=o, in_=i), waits=[e3])
        e5 = dve(lambda e, o=t1, i=g, m=m8: e.tensor_scalar(
            out=o, in0=i, scalar1=m[:, 0:1], scalar2=None, op0=ALU.is_equal), waits=[e4])
        e6 = dve(lambda e, o=t2, i=g, m=m8: e.tensor_scalar(
            out=o, in0=i, scalar1=m[:, 1:2], scalar2=None, op0=ALU.is_equal), waits=[e5])
        e7 = dve(lambda e, o=t1, i=t2: e.tensor_tensor(out=o, in0=o, in1=i, op=ALU.add), waits=[e6])
        e8 = dve(lambda e, o=t2, i=g: e.tensor_single_scalar(out=o, in_=i, scalar=0.0, op=ALU.is_gt), waits=[e7])
        e9 = dve(lambda e, o=af, m=m8: e.tensor_single_scalar(out=o, in_=m[:, 0:1], scalar=0.0, op=ALU.is_gt), waits=[e8])
        e10 = dve(lambda e, o=t2, i=t1: e.tensor_tensor(out=o, in0=o, in1=i, op=ALU.subtract), waits=[e9])
        e11 = dve(lambda e, o=t1, i=t2, a=af: e.scalar_tensor_tensor(
            out=o, in0=i, scalar=a, in1=o, op0=ALU.mult, op1=ALU.add), waits=[e10])
        e12 = dve(lambda e, o=t2, i=t1: e.tensor_single_scalar(out=o, in_=i, scalar=0.0, op=ALU.is_gt), waits=[e11])
        e13 = dve(lambda e, o=g: e.tensor_scalar_max(out=o, in0=o, scalar1=0.0), waits=[e12])
        e14 = dve(lambda e, o=g, u=t2: e.tensor_tensor(out=o, in0=o, in1=u, op=ALU.mult), waits=[e13])
        e15 = dve(lambda e, o=t1, u=t2: e.tensor_scalar(
            out=o, in0=u, scalar1=-NEG, scalar2=NEG, op0=ALU.mult, op1=ALU.add), waits=[e14])
        e16 = dve(lambda e, o=g, i=t1: e.tensor_tensor(out=o, in0=o, in1=i, op=ALU.add), waits=[e15])
        e17 = dve(lambda e, o=t1, i=g: e.tensor_reduce(out=o[:, 0:1], in_=i, axis=AX.X, op=ALU.max, negate=True), waits=[e16])
        e18 = act(rw_dst, g, AF.Exp, bias=t1[:, 0:1], accum=den, waits=[e17])
        e19 = dve(lambda e, o=den: e.reciprocal(out=o, in_=o), waits=[e18])
        e20 = dve(lambda e, o=rw_dst, r=den: e.tensor_scalar_mul(out=o, in0=o, scalar1=r), waits=[e19])
        return e20

    tiles = [("full", t) for t in range(8)] + [("own", t) for t in range(4)]
    pro_smalls = ev_sig2
    pro_row512 = ev_rc2
    xin_free = [None, None]
    st = {
        "rwb_free": ev_lnc, "r2bc_free": None, "accA_free": ev_lnc, "accB_free": None,
        "cap": None,
        "sqs_free": None, "xn1T_free": None, "vfm_free": None,
    }
    ev_k2 = []; ev_v2 = []; ev_q2 = []; ev_kaug = [ev_m3]; ev_voa = [ev_m4]
    ev_v2a = [ev_m5]; ev_rw_ownd = []; ev_xnownd = []
    last_full_proj = []

    for tix, (kind, t) in enumerate(tiles):
        buf = tix % 2
        src = x_full if kind == "full" else x_own
        base_tok = t * 512
        if kind == "own" and t == 0:
            ev_qp = dma(kp.rearrange("p (e k h) -> p e k h", k=8, h=128),
                        qp_d.rearrange("e (k p) h -> p e k h", p=128),
                        waits=last_full_proj, q="gpsimd")
        ev_xnT_blk = []
        ev_r2t = []
        for b in range(4):
            bbuf = (tix * 4 + b) % 2
            xblk = xin[bbuf]
            tok0 = base_tok + b * 128
            ev_x = dma(xblk, src[tok0:tok0 + 128, :], waits=[xin_free[bbuf]])
            cols = 48 * b
            evm, rstd, nmr, rstd2, ev_r2 = ln_stats(
                xblk, cols, [ev_x] + ([pro_smalls] if tix == 0 else []))
            if kind == "own":
                blk = t * 4 + b
                evxn = pool(lambda e, o=partial[:, blk * D:(blk + 1) * D], i=xblk,
                            a=nmr, r=rstd: e.tensor_scalar(
                    out=o, in0=i, scalar1=a, scalar2=r, op0=ALU.add, op1=ALU.mult),
                    waits=[evm])
                ev_xnownd.append(evxn)
                evc = act(xnb, partial[:, blk * D:(blk + 1) * D], AF.Copy,
                          waits=[evxn, (st.get("sqx_free") if b == 0 else None)])
            else:
                evc = pool(lambda e, o=xnb, i=xblk, a=nmr, r=rstd: e.tensor_scalar(
                    out=o, in0=i, scalar1=a, scalar2=r, op0=ALU.add, op1=ALU.mult),
                    waits=[evm, (st.get("sqx_free") if b == 0 else None)])
            evtr = []
            for k in range(8):
                evtr.append(tr(pbt[:, k * 128:(k + 1) * 128], xnb[:, k * 128:(k + 1) * 128],
                               ident_b[:], waits=[evc, state["pbt"], ev_idb]))
            evcp = act(xn1T.rearrange("p (k tt) -> p k tt", tt=512)[:, :, b * 128:(b + 1) * 128],
                       pbt[:].rearrange("p (k tt) -> p k tt", tt=128),
                       waits=[evtr[-1], st["xn1T_free"]])
            state["pbt"] = evcp
            ev_xnT_blk.append(evcp)
            ev_r2t.append(tr(pb[6][0:1, b * 128:(b + 1) * 128], rstd2, ident_f[:],
                             waits=[ev_r2, ev_pb[6], ev_idf]))
            xin_free[bbuf] = evc
        ev_row = act(row512[0:1, 0:512], pb[6][0:1, 0:512],
                     waits=[ev_r2t[-1]] + ([pro_row512] if tix == 0 else []))
        ev_pb[6] = ev_row
        ev_bb = mm(pb[5][:, 0:512], ones1_f[0:1, 0:128], row512[0:1, 0:512],
                   waits=[ev_row, ev_pb[5]])
        ev_r2b = act(r2bc, pb[5][:, 0:512], waits=[ev_bb, st["r2bc_free"], st.get("cap")])
        ev_pb[5] = ev_r2b

        if kind == "full":
            ev_kvm = []
            for k in range(8):
                ev_kvm.append(mm(pb[0][:, 0:512], wkv[:, k * 128:(k + 1) * 128],
                                 xn1T[:, k * 512:(k + 1) * 512],
                                 start=(k == 0), stop=(k == 7),
                                 waits=[ev_wkv, ev_pb[0] if k == 0 else None,
                                        ev_xnT_blk[-1]]))
            ev_k = act(kaug[0:64, base_tok:base_tok + 512], pb[0][0:64, 0:512], waits=[ev_kvm[-1]])
            ev_kaug.append(ev_k)
            ev_v = act(vfm[:], pb[0][64:128, 0:512], waits=[ev_kvm[-1], st["vfm_free"]])
            ev_pb[0] = ev_v
            evvt = []
            for b in range(4):
                evvt.append(tr(pbt[:, b * 64:(b + 1) * 64], vfm[:, b * 128:(b + 1) * 128],
                               ident_b[0:64, 0:64], waits=[ev_v, state["pbt"]]))
            ev_vo = act(v_o_aug[:, (4 * t) * 65:(4 * t + 4) * 65].rearrange(
                "p (b c) -> p b c", c=65)[:, :, 0:64],
                pbt[:, 0:256].rearrange("p (b c) -> p b c", c=64),
                waits=[evvt[-1]])
            state["pbt"] = ev_vo
            st["vfm_free"] = ev_vo
            ev_voa.append(ev_vo)
            n1dep = ev_kvm[-1]
        else:
            n1dep = None

        ev_n1 = []
        eprev = [st["accB_free"], st["accA_free"]]
        for k in range(8):
            ch = xn1T[:, k * 512:(k + 1) * 512]
            eng = pool if k % 2 == 0 else dve
            acc = accB if k % 2 == 0 else accA
            e1 = eng(lambda e, o=acc, i=ch, r=r2bc: e.tensor_tensor(out=o, in0=i, in1=r, op=ALU.mult),
                     waits=[ev_r2b, n1dep, ev_xnT_blk[-1], eprev[k % 2]])
            e2 = eng(lambda e, o=ch, i=acc, kk=k: e.tensor_scalar(
                out=o, in0=i, scalar1=ln1w_c[:, kk:kk + 1], scalar2=ln1b_c[:, kk:kk + 1],
                op0=ALU.mult, op1=ALU.add), waits=[e1] + ev_ln1)
            eprev[k % 2] = e2
            ev_n1.append(e2)
        st["accB_free"] = eprev[0]
        st["accA_free"] = eprev[1]

        if debug and tix == 0 and BUILD_PHASES < 2:
            ev_cap_n1 = act(AB2[:, 10272:14368], xn1T[:], waits=[ev_n1[-1]])
            ev_cap_r2 = act(AB2[:, 14368:14880], r2bc, waits=[ev_r2b, ev_cap_n1])
            st["cap"] = ev_cap_r2
        prev_bufev = [st["sqs_free"], st.get("sqx_free")]
        ev_sq_mm = []
        for k in range(8):
            buf = sqs if k % 2 == 0 else xnb[:, 0:512]
            esq = act(buf, xn1T[:, k * 512:(k + 1) * 512], AF.Square,
                      waits=[ev_n1[k], prev_bufev[k % 2]])
            emm = mm(pb[0][0:1, 0:512], onesc_b[:], buf, start=(k == 0), stop=(k == 7),
                     waits=[esq, ev_pb[0] if k == 0 else None, ev_m1])
            prev_bufev[k % 2] = emm
            ev_sq_mm.append(emm)
        st["sqs_free"] = prev_bufev[0]
        st["sqx_free"] = prev_bufev[1]
        ev_ssr = dve(lambda e, o=row512b[0:1, 0:512], i_=pb[0][0:1, 0:512]:
                     e.tensor_scalar_max(out=o, in0=i_, scalar1=1e-24),
                     waits=[ev_sq_mm[-1]])
        ev_pb[0] = ev_ssr
        ev_ss2 = act(row512b[0:1, 0:512], row512b[0:1, 0:512], AF.Ln, waits=[ev_ssr])
        ev_ss3 = act(row512b[0:1, 0:512], row512b[0:1, 0:512], AF.Exp, scale=-0.5, waits=[ev_ss2])
        ev_rnt = []
        for b in range(4):
            ev_rnt.append(tr(pb[6][:, b:b + 1], row512b[0:1, b * 128:(b + 1) * 128],
                             ident_f[0:1, 0:1], waits=[ev_ss3, ev_pb[6]]))
        ev_rn = act(rn_cols[:], pb[6][:, 0:4], waits=[ev_rnt[-1]])
        ev_pb[6] = ev_rn

        ev_rawm = []
        for k in range(8):
            ev_rawm.append(mm(pb[4][0:8, 0:512], sim1[:, k * 8:(k + 1) * 8],
                              xn1T[:, k * 512:(k + 1) * 512],
                              start=(k == 0), stop=(k == 7),
                              waits=[ev_n1[k], ev_pb[4] if k == 0 else None, ev_sim1]))
        ev_lgT = act(lgT[:], pb[4][0:8, 0:512], waits=[ev_rawm[-1]])
        ev_pb[4] = ev_lgT
        ev_lgt = []
        for b in range(4):
            ev_lgt.append(tr(pb[3][:, b * 8:(b + 1) * 8], lgT[:, b * 128:(b + 1) * 128],
                             ident_f[0:8, 0:8], waits=[ev_lgT, ev_pb[3]]))
        ev_lgtm = act(lg_tm[:], pb[3][:, 0:32], waits=[ev_lgt[-1]])
        ev_pb[3] = ev_lgtm
        if debug and BUILD_PHASES < 3:
            ev_dbg1 = act(d_stash[:, tix * 36:tix * 36 + 32], lg_tm[:], waits=[ev_lgtm])
            ev_dbg2 = act(d_stash[:, tix * 36 + 32:tix * 36 + 36], rn_cols[:], waits=[ev_rn, ev_dbg1])
            ev_lgtm = ev_dbg1

        ev_rw = []
        for b in range(4):
            cols = 48 * b
            g = lg_tm[:, b * 8:(b + 1) * 8]
            if kind == "own":
                blk = t * 4 + b
                rw_dst = rw_own[:, blk * 8:(blk + 1) * 8]
            else:
                rw_dst = rwtm[:, b * 8:(b + 1) * 8]
            ev = gate_block(ev_lgtm, g, rn_cols[:, b:b + 1], ev_rn,
                            sig1[:], rcol1[:], rw_dst, cols, extra=[ev_sig1, ev_rc1])
            ev_rw.append(ev)
            if kind == "own":
                ev_rw_ownd.append(ev)
        ev_rwt = []
        for b in range(4):
            srcrw = (rw_own[:, (t * 4 + b) * 8:(t * 4 + b + 1) * 8] if kind == "own"
                     else rwtm[:, b * 8:(b + 1) * 8])
            ev_rwt.append(tr(pb[3][0:8, b * 128:(b + 1) * 128], srcrw, ident_f[:],
                             waits=[ev_rw[b], ev_pb[3]]))
        ev_rwT = act(rwT[:], pb[3][0:8, 0:512], waits=[ev_rwt[-1]])
        ev_pb[3] = ev_rwT

        def proj_pass(wsb, ev_w, acc, acc_free):
            ev_acc = None
            ev_tmp_free = st["accB_free"]
            lastmm = None
            for e in range(NE):
                bank_i = 1 + (e % 2)
                bank = pb[bank_i]
                evb1 = mm(pb[5][:, 0:512], e_one[0:8, e * 128:(e + 1) * 128], rwT[:],
                          waits=[ev_rwT, ev_pb[5], ev_eone])
                evb2 = act(rwb, pb[5][:, 0:512], waits=[evb1, st["rwb_free"]])
                ev_pb[5] = evb2
                evm = None
                for k in range(8):
                    evm = mm(bank[:, 0:512], wsb[:, (e * 8 + k) * 128:(e * 8 + k + 1) * 128],
                             xn1T[:, k * 512:(k + 1) * 512],
                             start=(k == 0), stop=(k == 7),
                             waits=[ev_n1[k], ev_w,
                                    ev_pb[bank_i] if k == 0 else None])
                lastmm = evm
                if e == 0:
                    ev_acc = dve(lambda ee, o=acc, pp=bank, r=rwb: ee.tensor_tensor(
                        out=o, in0=pp[:, 0:512], in1=r, op=ALU.mult),
                        waits=[evm, evb2, acc_free])
                    ev_pb[bank_i] = ev_acc
                    st["rwb_free"] = ev_acc
                else:
                    evt_ = dve(lambda ee, pp=bank, r=rwb: ee.tensor_tensor(
                        out=accB, in0=pp[:, 0:512], in1=r, op=ALU.mult),
                        waits=[evm, evb2, ev_tmp_free])
                    ev_pb[bank_i] = evt_
                    st["rwb_free"] = evt_
                    ev_acc = dve(lambda ee, o=acc: ee.tensor_tensor(
                        out=o, in0=o, in1=accB, op=ALU.add), waits=[evt_, ev_acc])
                    ev_tmp_free = ev_acc
            st["accB_free"] = ev_tmp_free
            return ev_acc, lastmm

        if kind == "full":
            ev_acc, mmk = proj_pass(kp, ev_kp, accA, st["accA_free"])
            evk2 = act(k2_fm[:, base_tok:base_tok + 512], accA, waits=[ev_acc])
            ev_k2.append(evk2)
            ev_acc2, mmv = proj_pass(vp, ev_vp, accA, evk2)
            e1 = act(sqs, accA, AF.Copy, waits=[ev_acc2, st["sqs_free"]])
            st["sqs_free"] = None
            evvt2 = []
            for b in range(4):
                evvt2.append(tr(pbt[:, b * 128:(b + 1) * 128], sqs[:, b * 128:(b + 1) * 128],
                                ident_b[:], waits=[e1, state["pbt"]]))
            evv2 = act(v2_aug[(slice(None))].rearrange("p (bb c) -> p bb c", c=129)[:, 4 * t:4 * t + 4, 0:128],
                       pbt[:, 0:512].rearrange("p (bb c) -> p bb c", c=128),
                       waits=[evvt2[-1]])
            state["pbt"] = evv2
            st["sqs_free"] = evvt2[-1]
            ev_v2a.append(evv2)
            st["accA_free"] = e1
            last_full_proj = [mmk, mmv]
        else:
            ev_acc, mmq = proj_pass(kp, ev_qp, accA, st["accA_free"])
            evq2 = act(q2_fm[:, base_tok:base_tok + 512], accA, AF.Copy,
                       scale=Q2_SCALE, waits=[ev_acc])
            ev_q2.append(evq2)
            st["accA_free"] = evq2

        st["xn1T_free"] = ev_rawm[-1]
        if kind == "full":
            st["xn1T_free"] = last_full_proj[-1]
        else:
            st["xn1T_free"] = mmq
        st["r2bc_free"] = ev_n1[-1]

    P.flush()

    if BUILD_PHASES >= 2:
        opj = AB[:, 0:8192]
        ev_opj = dma(opj.rearrange("p (e c) -> p e c", c=1024),
                     op_d.rearrange("e h c -> h e c"), q="gpsimd")
        ptb = [AB[:, 8192:8448], AB[:, 8448:8704], AB[:, 8704:8960], AB[:, 8960:9216]]
        a2tm = AB[:, 9216:9344]
        a2T = AB[:, 9344:9472]
        sq1 = AB[:, 9472:9600]
        n2a = AFa[:, 1024:2048]
        nbcS = [AFa[:, 2048:2560], AFa[:, 512:1024]]
        nearS = AFa[:, 2560:2816]
        ev_hs = []
        ev_n2T = []
        ev_rw2 = []
        st["ptb1"] = [None, None, None, None]
        st["a2T_free"] = None
        st["nearS_free"] = None
        st["n2a_free"] = None
        st["sq1_free"] = None
        ev_k2l = ev_k2[-1]
        ev_v2l = ev_v2a[-1]
        ev_q2l = ev_q2[-1]

        from collections import deque
        bg1 = deque()

        def bg1_pop(n=1):
            for _ in range(n):
                if bg1:
                    bg1.popleft()()

        def mk_front1(i):
            nkb = 2 * i + 2
            far = nkb - 2
            nch = (far * 128 + 511) // 512
            cols = 48 * (i % 4)
            steps = []

            def s_chunk(c):
                def go():
                    qsl = q2_fm[:, i * 128:(i + 1) * 128]
                    tset = smalls[:, cols + 24:cols + 36]
                    gw = min(512, far * 128 - c * 512)
                    evm_ = mm(pb[5][:, 0:gw], qsl, k2_fm[:, c * 512:c * 512 + gw],
                              waits=[ev_q2l, ev_k2l, ev_pb[5]])
                    evr = dve(lambda e, o=tset[:, c:c + 1], i_=pb[5][:, 0:gw]:
                              e.tensor_reduce(out=o, in_=i_, axis=AX.X, op=ALU.max),
                              waits=[evm_, st.get("tset1_free_%d" % (i % 4))])
                    ev_pb[5] = evr
                    st.setdefault("evred1_%d" % i, []).append(evr)
                return go

            def s_tail():
                def go():
                    qsl = q2_fm[:, i * 128:(i + 1) * 128]
                    tset = smalls[:, cols + 24:cols + 36]
                    mcol = smalls[:, cols + 46:cols + 47]
                    evmn = mm(pb[5][:, 0:256], qsl, k2_fm[:, (nkb - 2) * 128:nkb * 128],
                              waits=[ev_q2l, ev_k2l, ev_pb[5]])
                    evadd = dve(lambda e, o=nearS, i_=pb[5][:, 0:256]: e.tensor_tensor(
                        out=o, in0=i_, in1=ca_t[:], op=ALU.add),
                        waits=[evmn, ev_ca, st["nearS_free"]])
                    ev_pb[5] = evadd
                    evrn = dve(lambda e, o=tset[:, nch:nch + 1], i_=nearS: e.tensor_reduce(
                        out=o, in_=i_, axis=AX.X, op=ALU.max), waits=[evadd])
                    st["nearS_free"] = evrn
                    evfm = dve(lambda e, o=mcol, i_=tset[:, 0:nch + 1]: e.tensor_reduce(
                        out=o, in_=i_, axis=AX.X, op=ALU.max, negate=True),
                        waits=[evrn] + st.get("evred1_%d" % i, []) + [st.get("mcol1_free_%d" % (i % 4))])
                    st["tset1_free_%d" % (i % 4)] = evfm
                    st["evfm1_%d" % i] = evfm
                return go

            def s_bcast():
                def go():
                    mcol = smalls[:, cols + 46:cols + 47]
                    nb = nbcS[i % 2]
                    evt_ = tr(pb[6][0:1, 0:128], mcol, ident_f[:],
                              waits=[st["evfm1_%d" % i], ev_pb[6]])
                    evrow = act(row512[0:1, 0:128], pb[6][0:1, 0:128],
                                waits=[evt_, st.get("row512_free")])
                    ev_pb[6] = evrow
                    st["mcol1_free_%d" % (i % 4)] = evt_
                    evbb = mm(pb[6][:, 0:128], ones1_f[0:1, 0:128], row512[0:1, 0:128],
                              waits=[evrow, ev_pb[6]])
                    st["row512_free"] = evbb
                    evnbca = act(nb[:, 0:128], pb[6][:, 0:128],
                                 waits=[evbb, st.get("nbc_free_%d" % (i % 2))])
                    evnbc = act(nb[:, 128:256], pb[6][:, 0:128], waits=[evnbca])
                    ev_pb[6] = evnbc
                    evnbc0 = dve(lambda e, o=nb[:, 256:384]: e.tensor_tensor(
                        out=o, in0=nb[:, 0:128], in1=ct_t[:, 0:128], op=ALU.add),
                        waits=[evnbc, ev_ct, st.get("nbc_free_%d" % (i % 2))])
                    evnbc1 = dve(lambda e, o=nb[:, 384:512]: e.tensor_tensor(
                        out=o, in0=nb[:, 0:128], in1=ct_t[:, 128:256], op=ALU.add),
                        waits=[evnbc, evnbc0])
                    st["nbc_done_%d" % i] = (evnbc, evnbc0, evnbc1)
                return go

            for c in range(nch):
                steps.append(s_chunk(c))
            steps.append(s_tail())
            steps.append(s_bcast())
            return steps

        bg1.extend(mk_front1(0))
        for i in range(NOWN):
            if i + 1 < NOWN:
                bg1.extend(mk_front1(i + 1))
            nkb = 2 * i + 2
            qsl = q2_fm[:, i * 128:(i + 1) * 128]
            cols = 48 * (i % 4)
            while st.get("nbc_done_%d" % i) is None and bg1:
                bg1.popleft()()
            evnbc, evnbc0, evnbc1 = st["nbc_done_%d" % i]
            nb = nbcS[i % 2]
            ev_ctx = None
            last_evs = None
            npair = nkb // 2
            pending1 = None

            def emit_av1(pav1):
                nonlocal ev_ctx
                kb0p, evep = pav1
                pt_ = ptb[(kb0p // 2) % 4]
                evav = None
                for u in range(2):
                    evav = mm(pb[4][:, 0:129], pt_[:, u * 128:(u + 1) * 128],
                              v2_aug[:, (kb0p + u) * 129:(kb0p + u + 1) * 129],
                              start=(kb0p + u == 0), stop=(kb0p + u == nkb - 1),
                              waits=[evep, ev_v2l, ev_m5,
                                     ev_pb[4] if kb0p + u == 0 else None])
                st["ptb1"][(kb0p // 2) % 4] = evav
                ev_ctx = evav

            for kp_ in range(npair):
                kb0 = 2 * kp_
                bsel = kp_ % 3
                evl = None
                for u in range(2):
                    evl = mm(pb[bsel][:, u * 128:(u + 1) * 128],
                             k2_fm[:, (kb0 + u) * 128:(kb0 + u + 1) * 128], qsl,
                             waits=[(ev_pb[bsel] if u == 0 else None), ev_k2l, ev_q2l])
                bg1_pop(1)
                if kp_ == npair - 1:
                    addt = nb[:, 256:512]; evadd_w = [evnbc1]
                else:
                    addt = nb[:, 0:256]; evadd_w = [evnbc]
                evs = dve(lambda e, o=nearS, i_=pb[bsel][:, 0:256], a=addt:
                          e.tensor_tensor(out=o, in0=i_, in1=a, op=ALU.add),
                          waits=[evl] + evadd_w + [st.get("nearS_free")])
                ev_pb[bsel] = evs
                last_evs = evs
                pt = ptb[kp_ % 4]
                eve = act(pt, nearS, AF.Exp, waits=[evs, st["ptb1"][kp_ % 4]])
                st["nearS_free"] = eve
                if pending1 is not None:
                    emit_av1(pending1)
                pending1 = (kb0, eve)
                bg1_pop(1)
            if pending1 is not None:
                emit_av1(pending1)
            st["nbc_free_%d" % (i % 2)] = last_evs
            rl = smalls[:, cols + 47:cols + 48]
            evrl = dve(lambda e, o=rl: e.reciprocal(out=o, in_=pb[4][:, 128:129]), waits=[ev_ctx])
            eva2 = dve(lambda e, o=a2tm, r=rl: e.tensor_scalar_mul(
                out=o, in0=pb[4][:, 0:128], scalar1=r), waits=[evrl, st.get("a2tm_free")])
            ev_pb[4] = eva2
            evt2 = tr(pbt[:, 0:128], a2tm, ident_b[:], waits=[eva2, state["pbt"]])
            eva2T = act(a2T, pbt[:, 0:128], waits=[evt2, st["a2T_free"]])
            state["pbt"] = eva2T
            st["a2tm_free"] = evt2
            bg1_pop(2)
            hsblk = partial[:, i * D:(i + 1) * D]
            bankrot = 0
            ev_half_read = {0: None, 1: None}
            for e_ in range(NE):
                for h_ in range(2):
                    bk = bankrot % 2
                    bankrot += 1
                    evmo = mm(pb[bk][:, 0:512], a2T,
                              opj[:, e_ * 1024 + h_ * 512:e_ * 1024 + (h_ + 1) * 512],
                              waits=[eva2T, ev_opj, ev_pb[bk]])
                    evst = dve(lambda e, o=hsblk.rearrange("p (h c) -> p h c", c=512),
                               pp=pb[bk][:, 0:512], rw=rw_own[:, i * 8 + e_:i * 8 + e_ + 1],
                               hh=h_: e.scalar_tensor_tensor(
                        out=o[:, hh, :], in0=pp, scalar=rw, in1=o[:, hh, :],
                        op0=ALU.mult, op1=ALU.add),
                        waits=[evmo, ev_rw_ownd[-1], ev_half_read[h_]])
                    ev_pb[bk] = evst
                    ev_half_read[h_] = evst
                if e_ % 3 == 2:
                    bg1_pop(1)
            ev_hs_done = ev_half_read[0]
            ev_hs_done2 = ev_half_read[1]
            ev_hs.append(ev_hs_done2)
            evm2, rstd_, nmr_, _r2, _e11 = ln_stats(hsblk, cols, [ev_hs_done, ev_hs_done2])
            evn2a = pool(lambda e, o=n2a, i_=hsblk, a=nmr_, r=rstd_: e.tensor_scalar(
                out=o, in0=i_, scalar1=a, scalar2=r, op0=ALU.add, op1=ALU.mult),
                waits=[evm2, st.get("n2a_free")])
            ev_tr2 = []
            for k in range(8):
                bsel = 6 if k % 2 == 0 else 3
                evt3 = tr(pb[bsel][:, (k // 2 % 2) * 256:(k // 2 % 2) * 256 + 128],
                          n2a[:, k * 128:(k + 1) * 128], ident_f[:],
                          waits=[evn2a, ev_pb[bsel]])
                evw = dve(lambda e, o=n2T.rearrange("p (k tt) -> p k tt", tt=2048)[:, k, i * 128:(i + 1) * 128],
                          i_=pb[bsel][:, (k // 2 % 2) * 256:(k // 2 % 2) * 256 + 128], kk=k:
                          e.tensor_scalar(out=o, in0=i_, scalar1=ln2w_c[:, kk:kk + 1],
                                          scalar2=ln2b_c[:, kk:kk + 1], op0=ALU.mult, op1=ALU.add),
                          waits=[evt3] + ev_ln2)
                ev_pb[bsel] = evw
                ev_tr2.append(evw)
            st["n2a_free"] = ev_tr2[-1]
            ev_n2T.append(ev_tr2[-1])
            bg1_pop(2)
            n2v = n2T.rearrange("p (k tt) -> p k tt", tt=2048)
            ev_sqm = []
            prev_sq1 = st.get("sq1_free")
            for k in range(8):
                esq = act(sq1, n2v[:, k, i * 128:(i + 1) * 128], AF.Square,
                          waits=[ev_tr2[k], prev_sq1])
                emm = mm(pb[3][0:1, 256:384] if False else pb[3][0:1, 384:512],
                         onesc_b[:], sq1, start=(k == 0), stop=(k == 7),
                         waits=[esq, ev_pb[3] if k == 0 else None])
                prev_sq1 = emm
                ev_sqm.append(emm)
            st["sq1_free"] = prev_sq1
            evss = dve(lambda e, o=row512b[0:1, 0:128], i_=pb[3][0:1, 384:512]:
                       e.tensor_scalar_max(out=o, in0=i_, scalar1=1e-24),
                       waits=[ev_sqm[-1], st.get("row512b_free")])
            ev_pb[3] = evss
            evs2 = act(row512b[0:1, 0:128], row512b[0:1, 0:128], AF.Ln, waits=[evss])
            evs3 = act(row512b[0:1, 0:128], row512b[0:1, 0:128], AF.Exp, scale=-0.5, waits=[evs2])
            evrt = tr(pb[6][:, 0:1], row512b[0:1, 0:128], ident_f[0:1, 0:1],
                      waits=[evs3, ev_pb[6]])
            st["row512b_free"] = evrt
            evrn2 = act(rn_cols[:, 0:1], pb[6][:, 0:1], waits=[evrt, st.get("rncols_free")])
            ev_pb[6] = evrn2
            ev_rawm2 = []
            for k in range(8):
                ev_rawm2.append(mm(pb[3][0:8, 0:128], sim2[:, k * 8:(k + 1) * 8],
                                   n2v[:, k, i * 128:(i + 1) * 128],
                                   start=(k == 0), stop=(k == 7),
                                   waits=[ev_tr2[k], ev_sim2, ev_pb[3] if k == 0 else None]))
            evlg2 = act(lgT[0:8, 0:128], pb[3][0:8, 0:128], waits=[ev_rawm2[-1], st.get("lgT_free")])
            ev_pb[3] = evlg2
            evlt2 = tr(pb[6][:, 4:12], lgT[0:8, 0:128], ident_f[0:8, 0:8],
                       waits=[evlg2, ev_pb[6]])
            evltm2 = act(lg_tm[:, 0:8], pb[6][:, 4:12], waits=[evlt2, st.get("lgtm_free")])
            ev_pb[6] = evltm2
            st["lgT_free"] = evlt2
            ev = gate_block(evltm2, lg_tm[:, 0:8], rn_cols[:, 0:1], evrn2,
                            sig2[:], rcol2[:], rw2_own[:, i * 8:(i + 1) * 8], cols,
                            extra=[ev_sig2, ev_rc2])
            ev_rw2.append(ev)
            st["rncols_free"] = ev
            st["lgtm_free"] = ev
            bg1_pop(2)
        while bg1:
            bg1.popleft()()

        P.flush()

    if BUILD_PHASES >= 3:
        slots = [AB[:, j * 4096:(j + 1) * 4096] for j in range(6)]
        _abdone = [st["xn1T_free"], ev_rw2[-1], st["sq1_free"], ev_n2T[-1],
                   st["lgT_free"], st["a2T_free"], state["pbt"]]
        slotev = [list(_abdone) for _ in range(6)]
        i0 = AB2[:, 0:4096]
        i1 = AB2[:, 4096:8192]
        ibuf = [i0, i1]
        ibuf_freeev = [None, None]
        ibuf_gelu = [[None] * 8, [None] * 8]
        n2v2 = n2T.rearrange("p (k tt) -> p k tt", tt=2048)
        ev_stt = None
        ev_w1ps = {0: None, 1: None}
        ev_p2 = []

        def w_dma(e_, what, slot_j, waits):
            semn = "LWA" if e_ % 2 == 0 else "LWB"
            if what < 2:
                srcd = w1_d[e_, what * 512:(what + 1) * 512, :]
            else:
                srcd = w2_d[e_, (what - 2) * 512:(what - 1) * 512, :]
            return dma(slots[slot_j].rearrange("p (k i) -> p k i", i=1024),
                       srcd.rearrange("(k p) i -> p k i", p=128),
                       waits=waits, q="gpsimd", sem=semn)

        def slotmap(e_):
            b = (4 * e_) % 6
            return [(b + j) % 6 for j in range(4)]

        ev_wdma = {}
        for w_ in range(4):
            ev_wdma[(0, w_)] = w_dma(0, w_, slotmap(0)[w_], slotev[slotmap(0)[w_]])

        def emit_w1_tile(e_, t_, ib, evw1a, evw1b):
            sm = slotmap(e_)
            last = None
            for j in range(8):
                bk = j % 2
                evm_ = None
                for k in range(8):
                    half = slots[sm[0]] if k < 4 else slots[sm[1]]
                    kk = k % 4
                    evm_ = mm(pb[bk][:, 0:512],
                              half[:, kk * 1024 + j * 128:kk * 1024 + (j + 1) * 128],
                              n2v2[:, k, t_ * 512:(t_ + 1) * 512],
                              start=(k == 0), stop=(k == 7),
                              waits=[evw1a if k < 4 else evw1b,
                                     ev_n2T[4 * t_ + 3],
                                     (ev_w1ps[bk] if k == 0 else None),
                                     (ev_pb[bk] if k == 0 else None)])
                slotev[sm[0]] = [evm_]
                slotev[sm[1]] = [evm_]
                evg = act(ibuf[ib][:, j * 512:(j + 1) * 512], pb[bk][:, 0:512], AF.Gelu,
                          waits=[evm_, ibuf_freeev[ib]])
                ev_w1ps[bk] = evg
                ev_pb[bk] = evg
                ibuf_gelu[ib][j] = evg
                last = evg
            return last

        def emit_w2_tile(e_, t_, ib, evw2a, evw2b):
            nonlocal ev_stt
            sm = slotmap(e_)
            lastmm = None
            for sub in range(4):
                blk = 4 * t_ + sub
                rwcol = rw2_own[:, blk * 8 + e_:blk * 8 + e_ + 1]
                for ch in range(2):
                    bk = 2 + ((2 * sub + ch) % 2)
                    evm_ = None
                    for j in range(8):
                        half = slots[sm[2]] if j < 4 else slots[sm[3]]
                        jj = j % 4
                        evm_ = mm(pb[bk][:, 0:512],
                                  ibuf[ib][:, j * 512 + sub * 128:j * 512 + (sub + 1) * 128],
                                  half[:, jj * 1024 + ch * 512:jj * 1024 + (ch + 1) * 512],
                                  start=(j == 0), stop=(j == 7),
                                  waits=[ibuf_gelu[ib][j],
                                         evw2a if j < 4 else evw2b,
                                         (ev_pb[bk] if j == 0 else None)])
                    slotev[sm[2]] = [evm_]
                    slotev[sm[3]] = [evm_]
                    lastmm = evm_
                    hv = partial[:, blk * D + ch * 512:blk * D + (ch + 1) * 512]
                    evst = dve(lambda e, o=hv, pp=pb[bk][:, 0:512], rw=rwcol:
                               e.scalar_tensor_tensor(out=o, in0=pp, scalar=rw, in1=o,
                                                      op0=ALU.mult, op1=ALU.add),
                               waits=[evm_, ev_rw2[blk], ev_stt, ev_hs[blk]])
                    ev_pb[bk] = evst
                    ev_stt = evst
            ibuf_freeev[ib] = lastmm
            return lastmm

        for e_ in range(NE):
            sm = slotmap(e_)
            if e_ + 1 < NE:
                smn = slotmap(e_ + 1)
                ev_wdma[(e_ + 1, 0)] = w_dma(e_ + 1, 0, smn[0], slotev[smn[0]])
                ev_wdma[(e_ + 1, 1)] = w_dma(e_ + 1, 1, smn[1], slotev[smn[1]])
            ea, eb = ev_wdma[(e_, 0)], ev_wdma[(e_, 1)]
            e2a, e2b = ev_wdma[(e_, 2)], ev_wdma[(e_, 3)]
            emit_w1_tile(e_, 0, 0, ea, eb)
            emit_w1_tile(e_, 1, 1, ea, eb)
            emit_w2_tile(e_, 0, 0, e2a, e2b)
            emit_w1_tile(e_, 2, 0, ea, eb)
            emit_w2_tile(e_, 1, 1, e2a, e2b)
            emit_w1_tile(e_, 3, 1, ea, eb)
            if e_ + 1 < NE:
                smn = slotmap(e_ + 1)
                ev_wdma[(e_ + 1, 2)] = w_dma(e_ + 1, 2, smn[2], slotev[smn[2]])
                ev_wdma[(e_ + 1, 3)] = w_dma(e_ + 1, 3, smn[3], slotev[smn[3]])
            emit_w2_tile(e_, 2, 0, e2a, e2b)
            emit_w2_tile(e_, 3, 1, e2a, e2b)
        ev_badd = []
        for blk in range(NOWN):
            ev_stt = dve(lambda e, o=partial[:, blk * D:(blk + 1) * D]:
                         e.tensor_tensor(out=o, in0=o, in1=bout_bc[:], op=ALU.add),
                         waits=[ev_stt, ev_bo2])
            ev_badd.append(ev_stt)
        ev_partial_done = ev_badd
        P.flush()

    if BUILD_PHASES >= 4:
        xnT_pair = AB[:, 0:2048]
        xnb3 = AB[:, 2048:3072]
        wqv = AB[:, 4096:12288]
        qaugA = AB[:, 12288:14336]
        ptw = [AB[:, 14336:15360], AB[:, 15360:16384]]
        ctxn3 = AB[:, 16384:17408]
        qp64 = AB[:, 17408:21504]
        qaugB = AB[:, 21504:23552]
        xA = AFa[:, 0:1024]
        xB = AFa[:, 1024:2048]
        scrF2 = AFa[:, 2048:3072]
        rlbF = AFa[:, 2560:3072]
        nearP3 = E(nc.sbuf_tensor("nearP3", [128, 384], F32))
        rl_t = [row512[0:1, 0:512], row512b[0:1, 0:512],
                lgT[0:1, 0:512], row512[0:1, 0:512]]
        m_all2 = m_all

        _p2done = [ev_stt, slotev[0][0], slotev[1][0], slotev[2][0], slotev[3][0],
                   slotev[4][0], slotev[5][0]]
        ev_wq = dma(wqv.rearrange("p (k d) -> p k d", d=1024),
                    Wq_d.rearrange("(k p) d -> p k d", p=128),
                    waits=_p2done, q="gpsimd", sem="LWA")
        ev_wo = dma(woutv.rearrange("p (k d) -> p k d", d=1024),
                    Wout_d.rearrange("(k p) d -> p k d", p=128),
                    waits=_p2done, q="gpsimd", sem="LWB")

        ohs_sm = AB[0:33, 23552:24320]
        ev_oh = dma(ohs_sm.rearrange("b (s k) -> b s k", k=256),
                    ohmT_d.rearrange("s b k -> b s k"),
                    waits=_p2done)
        ev_embr = [ev_emb, ev_embm]
        _dT_writes = []
        ev_pack_read = {4: ev_pb[4], 5: ev_pb[5]}
        for slot in range(3):
            for stg in range(16):
                bk = 4 + ((stg // 4) % 2)
                evm_ = None
                for cchunk in range(8):
                    s_val = stg * 8 + cchunk
                    evm_ = mm(pb[bk][:, ((stg % 4) * 8 + cchunk) * 16:((stg % 4) * 8 + cchunk + 1) * 16],
                              ohs_sm[0:33, slot * 256 + 127 - s_val:slot * 256 + 255 - s_val],
                              emb_ext[:], start=True, stop=True,
                              waits=[ev_oh] + ev_embr +
                                    ([ev_pack_read[bk]] if stg % 4 == 0 and cchunk == 0 else []))
                if stg % 4 == 3:
                    base = slot * 2048 + (stg - 3) * 128
                    epk = act(deltaT[:, base:base + 512], pb[bk][:, 0:512], waits=[evm_])
                    ev_pack_read[bk] = epk
                    _dT_writes.append(epk)
        ev_pb[4] = ev_pack_read[4]
        ev_pb[5] = ev_pack_read[5]
        ev_dT_done = _dT_writes[-1]
        ev_dA = {}
        _prev_cp = None
        for slot in range(3):
            for h in range(16):
                tT = deltaT.rearrange("p (s c h) -> p s c h", s=3, h=16)[:, slot, :, h]
                evt_ = tr(pbt[:, 0:128], tT, ident_b[:], waits=[ev_dT_done, state["pbt"]])
                evc_ = act(deltaA[:, (slot * 16 + h) * 128:(slot * 16 + h + 1) * 128],
                           pbt[:, 0:128], waits=[evt_, _prev_cp])
                state["pbt"] = evc_
                _prev_cp = evc_
                ev_dA[(slot, h)] = evc_
        st["qaugB_free"] = slotev[5][0]
        st["qaugB_free2"] = ev_oh

        from collections import deque
        bg = deque()

        def bg_pop(n=1):
            for _ in range(n):
                if not bg:
                    return
                gate, fn = bg[0]
                if gate is not None and st.get(gate) is None:
                    return
                bg.popleft()
                fn()

        st.update({
            "xbuf_free": [None, None], "xnb3_free": None, "xnT_free3": None,
            "qp_read": None, "qaug_free": [None, None], "mall_free": [None, None],
            "nearP3_free": None, "scrF_free_0": None, "scrF_free_1": None,
            "ebank": [None, None], "rlbF_free": None,
            "rlg_free": [st.get("row512_free"), st.get("row512b_free"),
                         st.get("lgT_free"), None],
            "ctxn_free": None, "p3stt": None,
        })
        ev_y = []

        def mk_front(pr):
            blks = [2 * pr, 2 * pr + 1]
            steps = []

            def s_x(bi):
                def go():
                    st["evx_%d" % blks[bi]] = dma(
                        (xA, xB)[bi], x_own[blks[bi] * 128:(blks[bi] + 1) * 128, :],
                        waits=[st["xbuf_free"][bi]] +
                              ([ev_rw2[-1], ev_n2T[-1]] if pr == 0 else []),
                        sem=("LDA" if bi == 0 else "LDB"))
                return go

            def s_ln(bi):
                def go():
                    blk = blks[bi]
                    cols = 48 * (blk % 4)
                    evm3, rstd3, nmr3, _r, _e = ln_stats((xA, xB)[bi], cols,
                                                         [st["evx_%d" % blk]])
                    st["evln_%d" % blk] = pool(
                        lambda e, o=xnb3, i_=(xA, xB)[bi], a=nmr3, r=rstd3: e.tensor_scalar(
                            out=o, in0=i_, scalar1=a, scalar2=r, op0=ALU.add, op1=ALU.mult),
                        waits=[evm3, st.get("xnb3_free")])
                return go

            def s_addx(bi):
                def go():
                    blk = blks[bi]
                    ev = dve(lambda e, o=partial[:, blk * D:(blk + 1) * D], xx=(xA, xB)[bi]:
                             e.tensor_tensor(out=o, in0=o, in1=xx, op=ALU.add),
                             waits=[st["evx_%d" % blk], ev_partial_done[blk],
                                    st["evln_%d" % blk], st["p3stt"]])
                    st["p3stt"] = ev
                    st["xbuf_free"][bi] = ev
                return go

            def s_xnT(bi):
                def go():
                    blk = blks[bi]
                    evtr3 = []
                    for k in range(8):
                        evtr3.append(tr(pbt[:, k * 128:(k + 1) * 128],
                                        xnb3[:, k * 128:(k + 1) * 128],
                                        ident_b[:],
                                        waits=[st["evln_%d" % blk], state["pbt"]]))
                    evcp3 = act(xnT_pair.rearrange("p (k tt) -> p k tt", tt=256)[:, :, bi * 128:(bi + 1) * 128],
                                pbt[:].rearrange("p (k tt) -> p k tt", tt=128),
                                waits=[evtr3[-1], st["xnT_free3"]])
                    state["pbt"] = evcp3
                    st["xnb3_free"] = evtr3[-1]
                    st["evxnT_%d_%d" % (pr, bi)] = evcp3
                return go

            def s_q(dk):
                def go():
                    evq_ = None
                    for k in range(8):
                        evq_ = mm(pb[6][:, 0:256], wqv[:, k * 1024 + dk * 128:k * 1024 + (dk + 1) * 128],
                                  xnT_pair.rearrange("p (k tt) -> p k tt", tt=256)[:, k, :],
                                  start=(k == 0), stop=(k == 7),
                                  waits=[ev_wq, st["evxnT_%d_1" % pr], (ev_pb[6] if k == 0 else None)])
                    evqe0 = act(qp64[0:64, (2 * dk) * 256:(2 * dk + 1) * 256],
                                pb[6][0:64, 0:256], AF.Copy,
                                scale=Q_SCALE, waits=[evq_, st["qp_read"]])
                    evqe = act(qp64[0:64, (2 * dk + 1) * 256:(2 * dk + 2) * 256],
                               pb[6][64:128, 0:256], AF.Copy,
                               scale=Q_SCALE, waits=[evqe0])
                    ev_pb[6] = evqe
                    st["xnT_free3"] = evq_
                    st["evq_%d_%d" % (pr, dk)] = evqe
                return go

            def s_passA_chunk(bi, h, c):
                def go():
                    blk = blks[bi]
                    i = blk
                    nkb = 2 * i + 2
                    nnear = min(3, nkb)
                    far = nkb - nnear
                    cols = 48 * (blk % 4)
                    tset = smalls[:, cols + 24 + (h % 2) * 12:cols + 36 + (h % 2) * 12]
                    qsl = qp64.rearrange("p (hh tt) -> p hh tt", tt=256)[0:64, h, bi * 128:(bi + 1) * 128]
                    gw = min(512, far * 128 - c * 512)
                    evm_ = mm(pb[6][:, 0:gw], qsl, kaug[0:64, c * 512:c * 512 + gw],
                              waits=[st["evq_%d_3" % pr], ev_kaug[-1], ev_pb[6]])
                    evr = dve(lambda e, o=tset[:, c:c + 1], i_=pb[6][:, 0:gw]:
                              e.tensor_reduce(out=o, in_=i_, axis=AX.X, op=ALU.max),
                              waits=[evm_, st.get("tset_free_%d_%d" % (blk % 4, h % 2))])
                    ev_pb[6] = evr
                    st.setdefault("evred_%d_%d" % (blk, h), []).append(evr)
                return go

            def s_passA_tail(bi, h):
                def go():
                    blk = blks[bi]
                    i = blk
                    nkb = 2 * i + 2
                    nnear = min(3, nkb)
                    far = nkb - nnear
                    cols = 48 * (blk % 4)
                    tset = smalls[:, cols + 24 + (h % 2) * 12:cols + 36 + (h % 2) * 12]
                    qsl = qp64.rearrange("p (hh tt) -> p hh tt", tt=256)[0:64, h, bi * 128:(bi + 1) * 128]
                    nch = (far * 128 + 511) // 512
                    nw = nnear * 128
                    evmn = mm(pb[6][:, 0:nw], qsl, kaug[0:64, far * 128:far * 128 + nw],
                              waits=[st["evq_%d_3" % pr], ev_kaug[-1], ev_pb[6]])
                    dsl = deltaA.rearrange("p (s h q) -> p s h q", h=16, q=128)[:, 3 - nnear:3, h, :]
                    evadd = dve(lambda e, o=nearP3[:, 0:nw], i_=pb[6][:, 0:nw], d_=dsl:
                                e.tensor_tensor(out=o, in0=i_, in1=d_, op=ALU.add),
                                waits=[evmn, ev_dA[(2, h)], st["nearP3_free"]])
                    ev_pb[6] = evadd
                    evrn3 = dve(lambda e, o=tset[:, nch:nch + 1], i_=nearP3[:, 0:nw]:
                                e.tensor_reduce(out=o, in_=i_, axis=AX.X, op=ALU.max),
                                waits=[evadd])
                    st["nearP3_free"] = evrn3
                    evfm = dve(lambda e, o=m_all[:, h:h + 1] if bi == 0 else smalls[:, 192 + h:193 + h],
                               i_=tset[:, 0:nch + 1]:
                               e.tensor_reduce(out=o, in_=i_, axis=AX.X, op=ALU.max, negate=True),
                               waits=[evrn3] + st.get("evred_%d_%d" % (blk, h), []) + [st["mall_free"][bi]])
                    st["tset_free_%d_%d" % (blk % 4, h % 2)] = evfm
                    st["evfm_%d_%d" % (blk, h)] = evfm
                return go

            def s_qaug(bi, g):
                def go():
                    blk = blks[bi]
                    qa = qaugA if bi == 0 else qaugB
                    mv = m_all if bi == 0 else smalls[:, 192:208]
                    qsrc = qp64.rearrange("p (hh tt) -> p hh tt", tt=256)[0:64, 4 * g:4 * g + 4,
                                                                          bi * 128:(bi + 1) * 128]
                    evq1 = act(qa[0:64, g * 512:(g + 1) * 512], qsrc,
                               waits=[st["evq_%d_3" % pr], st["qaug_free"][bi]] +
                                     ([st.get("qaugB_free"), st.get("qaugB_free2")] if bi == 1 else []))
                    evts = None
                    for u in range(4):
                        h = 4 * g + u
                        evts = tr(pb[6][0:1, u * 128:(u + 1) * 128], mv[:, h:h + 1],
                                  ident_f[:], waits=[st["evfm_%d_%d" % (blk, h)], ev_pb[6]])
                    evq2_ = act(qa[64:65, g * 512:(g + 1) * 512], pb[6][0:1, 0:512],
                                waits=[evts, evq1])
                    ev_pb[6] = evq2_
                    st["qaug_done_%d_%d" % (blk, g)] = evq2_
                    if g == 3:
                        st["mall_free"][bi] = evq2_
                        if bi == 1:
                            st["qp_read"] = evq1
                return go

            steps.append((None, s_x(0)))
            steps.append((None, s_ln(0)))
            steps.append((None, s_addx(0)))
            steps.append((None, s_xnT(0)))
            steps.append((None, s_x(1)))
            steps.append((None, s_ln(1)))
            steps.append((None, s_addx(1)))
            steps.append((None, s_xnT(1)))
            for dkp in range(4):
                steps.append((None, s_q(dkp)))
            for bi in range(2):
                blk = blks[bi]
                far = (2 * blk + 2) - min(3, 2 * blk + 2)
                nch = (far * 128 + 511) // 512
                for h in range(16):
                    for c in range(nch):
                        steps.append((None, s_passA_chunk(bi, h, c)))
                    steps.append((None, s_passA_tail(bi, h)))
                gate = ("passB_done_%d" % (blks[bi] - 2)) if pr >= 1 else None
                for g in range(4):
                    steps.append((gate, s_qaug(bi, g)))
            return steps


        def emit_passB(pr, bi):
            blk = 2 * pr + bi
            i = blk
            nkb = 2 * i + 2
            qa = qaugA if bi == 0 else qaugB
            ev_ptw = [[None, None], [None, None]]
            ev_avdone = [None] * 4
            ctx_free = [ev_pb[2], ev_pb[3], ev_pb[4], ev_pb[5]]
            pending_av = None

            def emit_av(pav):
                kbp, hwp, eep2 = pav
                evav = None
                for g in (2 * hwp, 2 * hwp + 1):
                    if g % 2 == 1:
                        bg_pop(1)
                    evav = mm(pb[2 + g][0:65, 0:512],
                              v_o_aug[:, kbp * 65:(kbp + 1) * 65],
                              ptw[hwp][:, (g % 2) * 512:(g % 2) * 512 + 512],
                              start=(kbp == 0), stop=(kbp == nkb - 1),
                              waits=[eep2[g % 2], ev_voa[-1], ev_m4] +
                                    ([ctx_free[g]] if kbp == 0 else []))
                    ev_avdone[g] = evav
                    ev_ptw[hwp][g % 2] = evav

            for kb in range(nkb):
                slot = kb - (nkb - 3)
                for hw_ in range(2):
                    evl_by = [None, None]
                    for g in (2 * hw_, 2 * hw_ + 1):
                        lb = g % 2
                        evl = mm(pb[lb][:, 0:512], kaug[:, kb * 128:(kb + 1) * 128],
                                 qa[0:65, g * 512:(g + 1) * 512],
                                 waits=[st["qaug_done_%d_%d" % (blk, g)],
                                        st["ebank"][lb], ev_pb[lb]])
                        ev_pb[lb] = evl
                        evl_by[lb] = evl
                    bg_pop(1)
                    ee2 = [None, None]
                    if slot >= 0:
                        dview = deltaT.rearrange("p (s q h) -> p s q h", q=128, h=16)
                        for lb in range(2):
                            dsl2 = dview[:, slot, :, 8 * hw_ + 4 * lb:8 * hw_ + 4 * lb + 4
                                         ].transpose([0, 2, 1])
                            evdd = dve(lambda e, o=scrF2[:, lb * 512:(lb + 1) * 512],
                                       i_=pall[:, lb * 512:(lb + 1) * 512], d_=dsl2:
                                       e.tensor_tensor(out=o, in0=i_, in1=d_, op=ALU.add),
                                       waits=[evl_by[lb], ev_dT_done,
                                              st["scrF_free_%d" % lb]] +
                                             ([st["rlbF_free"]] if lb == 1 else []))
                            st["ebank"][lb] = evdd
                            ee2[lb] = act(ptw[hw_][:, lb * 512:(lb + 1) * 512],
                                          scrF2[:, lb * 512:(lb + 1) * 512], AF.Exp,
                                          waits=[evdd, ev_ptw[hw_][lb]])
                            st["scrF_free_%d" % lb] = ee2[lb]
                    else:
                        for lb in range(2):
                            ee2[lb] = act(ptw[hw_][:, lb * 512:(lb + 1) * 512],
                                          pall[:, lb * 512:(lb + 1) * 512], AF.Exp,
                                          waits=[evl_by[lb], ev_ptw[hw_][lb]])
                            st["ebank"][lb] = ee2[lb]
                    if pending_av is not None:
                        emit_av(pending_av)
                    pending_av = (kb, hw_, ee2)
                    bg_pop(1)
                bg_pop(1)
            if pending_av is not None:
                emit_av(pending_av)
            st["qaug_free"][bi] = evl
            st["passB_done_%d" % blk] = evl

            ev_ctx_read = []
            evrl_g = [None] * 4
            for g in range(3):
                evrl_g[g] = act(rl_t[g], pall[64:65, 1024 + g * 512:1536 + g * 512],
                                AF.Ln, waits=[ev_avdone[g], st["rlg_free"][g]])
            bg_pop(2)
            evbb_g0 = None
            for g in range(4):
                if g == 3:
                    evrl_g[3] = act(rl_t[3], pall[64:65, 2560:3072], AF.Ln,
                                    waits=[ev_avdone[3], evbb_g0])
                evbb3 = mm(pb[6][0:64, 0:512], ones1_f[0:1, 0:64], rl_t[g],
                           waits=[evrl_g[g], ev_pb[6]])
                if g == 0:
                    evbb_g0 = evbb3
                st["rlg_free"][0 if g == 3 else g] = evbb3
                evcpb = act(rlbF[0:64, 0:512], pb[6][0:64, 0:512], AF.Exp, scale=-1.0,
                            waits=[evbb3, st["rlbF_free"], st["scrF_free_1"]] +
                                  ([ev_ctx_read[-1]] if ev_ctx_read else []))
                ev_pb[6] = evcpb
                for par in range(2):
                    ci = pb[2 + g].rearrange("p (u q) -> p u q", q=128)[0:64, par:4:2, :]
                    ri = rlbF.rearrange("p (u q) -> p u q", q=128)[0:64, par:4:2, :]
                    dst = ctxn3.rearrange("p (k q) -> p k q", q=128)[
                        par * 64:(par + 1) * 64, 2 * g:2 * g + 2, :]
                    evnrm = dve(lambda e, o=dst, i_=ci, r_=ri:
                                e.tensor_tensor(out=o, in0=i_, in1=r_, op=ALU.mult),
                                waits=[evcpb, st["ctxn_free"]])
                    ev_pb[2 + g] = evnrm
                    ev_ctx_read.append(evnrm)
                bg_pop(1)
            st["rlbF_free"] = ev_ctx_read[-1]
            ev_out_half = []
            evwm = None
            for ch in range(2):
                bk = ch
                for k in range(8):
                    evwm = mm(pb[bk][:, 0:512],
                              ctxn3.rearrange("p (k tt) -> p k tt", tt=128)[:, k, :],
                              woutv[:, k * 1024 + ch * 512:k * 1024 + (ch + 1) * 512],
                              start=(k == 0), stop=(k == 7),
                              waits=[ev_ctx_read[2 * (k // 2) + 1], ev_wo,
                                     (ev_pb[bk] if k == 0 else None)])
                hv = partial[:, i * D + ch * 512:i * D + (ch + 1) * 512]
                ev1 = dve(lambda e, o=hv, pp=pb[bk][:, 0:512]: e.tensor_tensor(
                    out=o, in0=o, in1=pp, op=ALU.add),
                    waits=[evwm, st["p3stt"]])
                ev_pb[bk] = ev1
                st["p3stt"] = ev1
                ev_out_half.append(ev1)
                bg_pop(1)
            st["ctxn_free"] = evwm
            evy = dma_out(y[i * 128:(i + 1) * 128, :], partial[:, i * D:(i + 1) * D],
                          waits=ev_out_half)
            ev_y.append(evy)

        bg.extend(mk_front(0))
        for pr in range(8):
            if pr + 1 < 8:
                bg.extend(mk_front(pr + 1))
            for bi in range(2):
                key = "qaug_done_%d_3" % (2 * pr + bi)
                while st.get(key) is None and bg:
                    gate, fn = bg.popleft()
                    fn()
                emit_passB(pr, bi)
        while bg:
            gate, fn = bg.popleft()
            fn()
        S("sync", lambda e: e.nop(), waits=[("OUT", P.cnt["OUT"])])
        P.flush()

    if debug and BUILD_PHASES < 4:
        wD = ([ev_kaug[-1], ev_voa[-1], ev_k2[-1] if ev_k2 else None,
               ev_v2a[-1], ev_q2[-1] if ev_q2 else None]
              + ev_rw_ownd[-1:] + ev_xnownd[-1:])
        d = []
        d.append(dma_out(dbg["d_kaug"], kaug[:], waits=wD))
        d.append(dma_out(dbg["d_voaug"], v_o_aug[:], waits=wD))
        d.append(dma_out(dbg["d_k2"], k2_fm, waits=wD))
        d.append(dma_out(dbg["d_v2aug"], v2_aug, waits=wD))
        d.append(dma_out(dbg["d_q2"], q2_fm, waits=wD))
        d.append(dma_out(dbg["d_rw"], rw_own[:], waits=wD))
        d.append(dma_out(dbg["d_xn"], partial[:], waits=wD))
        if debug and BUILD_PHASES < 3:
            d.append(dma_out(dbg["d_lgraw"], d_stash[:].rearrange("p (t c) -> p t c", c=36)[:, :, 0:32], waits=wD))
            d.append(dma_out(dbg["d_rn"], d_stash[:].rearrange("p (t c) -> p t c", c=36)[:, :, 32:36], waits=wD))
        if BUILD_PHASES < 2:
            d.append(dma_out(dbg["d_n1"], AB2[:, 10272:14368], waits=wD))
            d.append(dma_out(dbg["d_r2bc"], AB2[:, 14368:14880], waits=wD))
        if debug >= 2 and BUILD_PHASES >= 2:
            w2_ = [ev_hs[-1], ev_n2T[-1], ev_rw2[-1]]
            if BUILD_PHASES < 3:
                d.append(dma_out(dbg["d_hs"], partial[:], waits=w2_))
            d.append(dma_out(dbg["d_n2T"], n2T, waits=w2_))
            d.append(dma_out(dbg["d_rw2"], rw2_own[:], waits=w2_))
        if debug >= 3 and BUILD_PHASES >= 3:
            d.append(dma_out(dbg["d_partial"], partial[:], waits=[ev_partial_done[-1]]))
        d.append(dma_out(y[0:128, :], partial[:, 0:1024], waits=wD))
        S("sync", lambda e: e.nop(), waits=[d[-1]])
        P.flush()
        ctx.close()
        return nc, dbg

    ctx.close()
    return nc, dbg


_NC_CACHE = {}


def _prep_inputs(inputs, p):
    cst = _host_constants(p)
    base = {
        "Wq": inputs["L0_attn_Wq"], "Wkv": inputs["L0_attn_Wkv"],
        "Wout": inputs["L0_attn_Wout"], "bout": inputs["L0_attn_bout"],
        "rel_emb": inputs["L0_attn_rel_emb"],
        "ln1_w": inputs["L0_ffn_ln1_w"], "ln1_b": inputs["L0_ffn_ln1_b"],
        "ln2_w": inputs["L0_ffn_ln2_w"], "ln2_b": inputs["L0_ffn_ln2_b"],
        "smha_sim": inputs["L0_ffn_smha_sim"], "smha_gates": inputs["L0_ffn_smha_gates"],
        "q_proj": inputs["L0_ffn_q_proj"], "k_proj": inputs["L0_ffn_k_proj"],
        "v_proj": inputs["L0_ffn_v_proj"], "o_proj": inputs["L0_ffn_o_proj"],
        "moe_sim": inputs["L0_ffn_moe_sim"], "moe_gates": inputs["L0_ffn_moe_gates"],
        "w1": inputs["L0_ffn_w1"], "w2": inputs["L0_ffn_w2"],
        "ohm_T": cst["ohm_T"],
        "ca": cst["ca"], "ct": cst["ct"],
        "ident_f": cst["ident_f"], "ident_b": cst["ident_b"], "e_one": cst["e_one"],
    }
    return base


def make_in_maps(inputs, debug=0):
    x = np.ascontiguousarray(np.asarray(inputs["L0_attn_x"], dtype=np.float32))
    in_maps = []
    for c in range(8):
        b, p = c // 2, c % 2
        m = _prep_inputs(inputs, p)
        xb = x[b]
        m["x_full"] = np.ascontiguousarray(xb)
        m["x_own"] = np.ascontiguousarray(xb.reshape(NB, 128, D)[p::2].reshape(TOWN, D))
        in_maps.append(m)
    return in_maps


def kernel(**inputs):
    nc, _ = build_program(debug=0)
    in_maps = make_in_maps(inputs)
    res = run_bass_kernel_spmd(nc, in_maps, core_ids=list(range(8)))
    x = np.asarray(inputs["L0_attn_x"], dtype=np.float32)
    out = np.empty((4, N, D), dtype=np.float32)
    for c in range(8):
        b, p = c // 2, c % 2
        yc = np.asarray(res.results[c]["y"])
        out[b].reshape(NB, 128, D)[p::2] = yc.reshape(NOWN, 128, D)
    return out


# version 23
# speedup vs baseline: 1.0414x; 1.0414x over previous
import numpy as np
import ml_dtypes
import concourse.bass as bass
import concourse.mybir as mybir
from concourse.bass_utils import run_bass_kernel_spmd
from contextlib import ExitStack

F32 = mybir.dt.float32
BF16 = mybir.dt.bfloat16
ALU = mybir.AluOpType
AF = mybir.ActivationFunctionType
AX = mybir.AxisListType

D = 1024
N = 4096
NB = 32
NOWN = 16
TOWN = NOWN * 128
NH = 16
DH = 64
H2 = 128
NE = 8
EPS = 1e-5
NEG = -1e30
Q_SCALE = 0.125
Q2_SCALE = float(np.float32(1.0) / np.sqrt(np.float32(128.0)))

BUILD_PHASES = 4


class Prog:
    ENGS = ("sync", "gpsimd", "scalar", "vector", "tensor")

    def __init__(self, nc, ctx):
        self.nc = nc
        self.sems = {}
        self.cnt = {}
        self.last_wait = {}
        for name in ("LD", "LW", "OUT", "LDA", "LDB", "LWA", "LWB", "PE", "DV", "AC", "PL"):
            self.sems[name] = ctx.enter_context(nc.semaphore(name))
            self.cnt[name] = 0
        self.sched = []

    DMA_SEMS = ("LD", "LW", "OUT", "LDA", "LDB", "LWA", "LWB")

    def step(self, eng, fn, waits=(), inc=None, n=1):
        ev = None
        waits = [(w[0], self.cnt[w[0]]) if (w is not None and w[0] in self.DMA_SEMS)
                 else w for w in waits if w is not None]
        if inc is not None:
            self.cnt[inc] += n
            ev = (inc, self.cnt[inc], n)
        self.sched.append((eng, fn, tuple(waits), ev))
        return (ev[0], ev[1]) if ev else None

    def flush(self):
        sched = self.sched
        self.sched = []
        if not sched:
            return
        with self.nc.Block() as block:
            for engname in self.ENGS:
                if not any(s[0] == engname for s in sched):
                    continue

                def make(en):
                    def body(eng):
                        for (e2, fn, waits, ev) in sched:
                            if e2 != en:
                                continue
                            for (s, v) in waits:
                                key = (en, s)
                                if self.last_wait.get(key, -1) < v:
                                    eng.wait_ge(self.sems[s], v)
                                    self.last_wait[key] = v
                            ins = fn(eng)
                            if ev is not None:
                                ins.then_inc(self.sems[ev[0]], ev[2])
                    return body

                getattr(block, engname)(make(engname))


def _bucket(n):
    n = np.asarray(n)
    small = n < 16
    nf = np.maximum(n, 1).astype(np.float32)
    val = 16 + (np.log(nf / np.float32(16.0)) / np.float32(np.log(8.0))
                * np.float32(16.0)).astype(np.int32)
    val = np.minimum(val, 31)
    return np.where(small, n, val)


def _host_constants(p):
    q_idx = np.arange(128)[:, None]
    s_idx = np.arange(128)[None, :]
    ohm_T = np.zeros((3, 33, 256), dtype=np.float32)
    for slot in range(3):
        dblk = 1 + p - slot
        n = 128 * dblk - 127 + np.arange(256)
        masked = n < 0
        bkt = _bucket(np.maximum(n, 0))
        for b in range(31):
            sel = (bkt == b) & (~masked)
            ohm_T[slot, b][sel] = 1.0
            ohm_T[slot, 31][sel] -= 1.0
        ohm_T[slot, 32][masked] = 1.0
    ca = np.zeros((2, 128, 128), dtype=np.float32)
    caus = np.where(s_idx > q_idx, np.float32(NEG), np.float32(0.0)).astype(np.float32)
    if p == 0:
        ca[0] = caus
        ca[1] = NEG
    else:
        ca[0] = 0.0
        ca[1] = caus
    ct = np.ascontiguousarray(ca.transpose(0, 2, 1))
    ident = np.eye(128, dtype=np.float32)
    e_one = np.zeros((8, 8 * 128), dtype=np.float32)
    for e in range(8):
        e_one[e, e * 128:(e + 1) * 128] = 1.0
    bf = ml_dtypes.bfloat16
    return {
        "ohm_T": ohm_T.astype(bf),
        "ca": ca, "ct": ct,
        "ident_f": ident, "ident_b": ident.astype(bf),
        "e_one": e_one.astype(bf),
    }


def build_program(debug=0):
    nc = bass.Bass("TRN2", target_bir_lowering=False)
    ctx = ExitStack()

    dT = lambda name, shape, dt=F32: nc.dram_tensor(name, shape, dt, kind="ExternalInput").ap()
    x_full = dT("x_full", [N, D])
    x_own = dT("x_own", [TOWN, D])
    Wq_d = dT("Wq", [D, D])
    Wkv_d = dT("Wkv", [D, 2 * DH])
    Wout_d = dT("Wout", [D, D])
    bout_d = dT("bout", [D])
    rel_d = dT("rel_emb", [32, 16])
    ln1w_d = dT("ln1_w", [D]); ln1b_d = dT("ln1_b", [D])
    ln2w_d = dT("ln2_w", [D]); ln2b_d = dT("ln2_b", [D])
    sim1_d = dT("smha_sim", [D, NE]); g1_d = dT("smha_gates", [NE])
    qp_d = dT("q_proj", [NE, D, H2]); kp_d = dT("k_proj", [NE, D, H2])
    vp_d = dT("v_proj", [NE, D, H2]); op_d = dT("o_proj", [NE, H2, D])
    sim2_d = dT("moe_sim", [D, NE]); g2_d = dT("moe_gates", [NE])
    w1_d = dT("w1", [NE, D, D]); w2_d = dT("w2", [NE, D, D])
    ohmT_d = dT("ohm_T", [3, 33, 256], BF16)
    ca_d = dT("ca", [2, 128, 128]); ct_d = dT("ct", [2, 128, 128])
    identf_d = dT("ident_f", [128, 128])
    identb_d = dT("ident_b", [128, 128], BF16)
    eone_d = dT("e_one", [8, 8 * 128], BF16)

    y = nc.dram_tensor("y", [TOWN, D], F32, kind="ExternalOutput").ap()
    dbg = {}
    if debug:
        def dout(name, shape, dt=F32):
            dbg[name] = nc.dram_tensor(name, shape, dt, kind="ExternalOutput").ap()
        dout("d_kaug", [65, N], BF16)
        dout("d_voaug", [128, 32 * 65], BF16)
        dout("d_k2", [128, N], BF16)
        dout("d_v2aug", [128, 32 * 129], BF16)
        dout("d_q2", [128, TOWN], BF16)
        dout("d_rw", [128, NOWN * 8])
        dout("d_xn", [128, NOWN * D])
        dout("d_lgraw", [128, 12 * 32])
        dout("d_n1", [128, 4096], BF16)
        dout("d_r2bc", [128, 512], BF16)
        dout("d_rn", [128, 12 * 4])
        dout("d_stash2", [128, 32])
        if debug >= 2:
            dout("d_hs", [128, NOWN * D])
            dout("d_n2T", [128, NOWN * D], BF16)
            dout("d_rw2", [128, NOWN * 8])
        if debug >= 3:
            dout("d_partial", [128, NOWN * D])

    E = ctx.enter_context

    kaug = E(nc.sbuf_tensor("s_kaug", [65, N], BF16))
    v_o_aug = E(nc.sbuf_tensor("v_o_aug", [128, 32 * 65], BF16))
    partial = E(nc.sbuf_tensor("partial", [128, NOWN * D], F32))
    rw_own = E(nc.sbuf_tensor("rw_own", [128, NOWN * 8], F32))
    rw2_own = E(nc.sbuf_tensor("rw2_own", [128, NOWN * 8], F32))
    ca_t = E(nc.sbuf_tensor("ca_t", [128, 2 * 128], F32))
    ct_t = E(nc.sbuf_tensor("ct_t", [128, 2 * 128], F32))
    ident_f = E(nc.sbuf_tensor("s_ident_f", [128, 128], F32))
    ident_b = E(nc.sbuf_tensor("s_ident_b", [128, 128], BF16))
    e_one = E(nc.sbuf_tensor("s_e_one", [8, 8 * 128], BF16))
    onesc_b = E(nc.sbuf_tensor("onesc_b", [128, 1], BF16))
    ones1_f = E(nc.sbuf_tensor("ones1_f", [1, 128], F32))
    sig1 = E(nc.sbuf_tensor("sig1", [128, 8], F32))
    sig2 = E(nc.sbuf_tensor("sig2", [128, 8], F32))
    rcol1 = E(nc.sbuf_tensor("rcol1", [128, 8], F32))
    rcol2 = E(nc.sbuf_tensor("rcol2", [128, 8], F32))
    sim1 = E(nc.sbuf_tensor("sim1", [128, 8 * 8], BF16))
    sim2 = E(nc.sbuf_tensor("sim2", [128, 8 * 8], BF16))
    lnc = E(nc.sbuf_tensor("lnc", [128, 32], F32))
    ln1w_c = lnc[:, 0:8]; ln1b_c = lnc[:, 8:16]
    ln2w_c = lnc[:, 16:24]; ln2b_c = lnc[:, 24:32]
    bout_bc = E(nc.sbuf_tensor("bout_bc", [128, D], BF16))
    emb_ext = E(nc.sbuf_tensor("emb_ext", [33, 16], BF16))
    rel_sb = E(nc.sbuf_tensor("rel_sb", [32, 16], F32))
    row512 = E(nc.sbuf_tensor("row512", [1, 512], F32))
    row512b = E(nc.sbuf_tensor("row512b", [1, 512], F32))
    lgT = E(nc.sbuf_tensor("lgT", [8, 512], F32))
    rwT = E(nc.sbuf_tensor("rwT", [8, 512], BF16))
    lg_tm = E(nc.sbuf_tensor("lg_tm", [128, 32], F32))
    rn_cols = E(nc.sbuf_tensor("rn_cols", [128, 4], F32))
    rwtm = rw2_own[:, 0:32]
    smalls = E(nc.sbuf_tensor("smalls", [128, 208], F32))
    m_all = E(nc.sbuf_tensor("m_all", [128, 16], F32))
    qaug = E(nc.sbuf_tensor("qaug", [65, 128], BF16))
    vfm = E(nc.sbuf_tensor("vfm", [64, 512], BF16))

    d_stash = E(nc.sbuf_tensor("d_stash", [128, 432], F32)) if (debug and BUILD_PHASES < 3) else None
    AB = E(nc.sbuf_tensor("AB", [128, 24576], BF16))
    AB2 = E(nc.sbuf_tensor("AB2", [128, 26656], BF16))
    AFa = E(nc.sbuf_tensor("AFa", [128, 3072], F32))

    k2_fm = AB2[:, 0:4096]
    v2_aug = AB2[:, 4096:4096 + 32 * 129]
    q2_fm = AB2[:, 8224:8224 + 2048]
    n2T = AB2[:, 10272:10272 + 16384]
    deltaT = AB2[:, 0:6144]
    deltaA = AB2[:, 6144:12288]
    woutv = AB2[:, 12288:12288 + 8192]
    p3scr = AB2[:, 20480:26656]
    wkv = AB[:, 23040:24064]

    pall = E(nc.psum_tensor("pall", [128, 3584], F32))
    pb = [pall[:, i * 512:(i + 1) * 512] for i in range(7)]
    pbt = E(nc.psum_tensor("pbt", [128, 1024], BF16))

    P = Prog(nc, ctx)
    S = P.step

    def mm(out, lhsT, rhs, start=True, stop=True, waits=(), inc="PE"):
        return S("tensor", lambda e, o=out, l=lhsT, r=rhs, a=start, b=stop:
                 e.matmul(out=o, lhsT=l, rhs=r, start=a, stop=b), waits, inc)

    def tr(out, in_, ident, waits=(), inc="PE"):
        return S("tensor", lambda e, o=out, i=in_, d=ident:
                 e.transpose(out=o, in_=i, identity=d), waits, inc)

    def act(out, in_, func=AF.Copy, bias=0.0, scale=1.0, accum=None, waits=(), inc="AC"):
        return S("scalar", lambda e, o=out, i=in_, f=func, b=bias, s=scale, a=accum:
                 e.activation(out=o, in_=i, func=f, bias=b, scale=s, accum_out=a), waits, inc)

    def dve(fn, waits=(), inc="DV"):
        return S("vector", fn, waits, inc)

    def pool(fn, waits=(), inc="PL"):
        return S("gpsimd", fn, waits, inc)

    def dma(out, in_, waits=(), q="sync", sem=None):
        s = sem or ("LW" if q == "gpsimd" else "LD")
        return S(q, lambda e, o=out, i=in_: e.dma_start(out=o, in_=i),
                 list(waits) + [(s, P.cnt[s])], s, 16)

    def dma_out(out, in_, waits=()):
        return S("sync", lambda e, o=out, i=in_: e.dma_start(out=o, in_=i),
                 list(waits) + [("OUT", P.cnt["OUT"])], "OUT", 16)

    ev_pb = {i: None for i in range(7)}
    state = {"pbt": None}

    ev_idf = dma(ident_f[:], identf_d)
    ev_idb = dma(ident_b[:], identb_d)
    ev_eone = dma(e_one[:], eone_d)
    ev_ca = [dma(ca_t[:, a * 128:(a + 1) * 128], ca_d[a]) for a in range(2)][-1]
    ev_ct = [dma(ct_t[:, a * 128:(a + 1) * 128], ct_d[a]) for a in range(2)][-1]
    ev_rel = dma(rel_sb[:], rel_d)
    ev_sim1 = dma(sim1[:].rearrange("p (k e) -> p k e", e=8), sim1_d.rearrange("(k p) e -> p k e", p=128), q="gpsimd")
    ev_sim2 = dma(sim2[:].rearrange("p (k e) -> p k e", e=8), sim2_d.rearrange("(k p) e -> p k e", p=128), q="gpsimd")
    _lnstage = AFa[0:8, 2048:2560]
    _lnd = []
    for i, srcd in enumerate((ln1w_d, ln1b_d, ln2w_d, ln2b_d)):
        _lnd.append(dma(_lnstage.rearrange("k (i c) -> k i c", c=128)[:, i, :],
                        srcd.rearrange("(k p) -> k p", p=128)))
    _lnt = []
    for i in range(4):
        _lnt.append(tr(pb[6][:, 120 + i * 8:120 + (i + 1) * 8],
                       _lnstage.rearrange("k (i c) -> k i c", c=128)[:, i, :],
                       ident_f[0:8, 0:8], waits=[_lnd[i], ev_idf, ev_pb[6]]))
    ev_lnc = act(lnc[:], pb[6][:, 120:152], waits=[_lnt[-1]])
    ev_pb[6] = ev_lnc
    ev_ln1 = [ev_lnc]
    ev_ln2 = [ev_lnc]
    ev_g1 = dma(smalls[0:1, 0:8], g1_d.rearrange("(o e) -> o e", o=1))
    ev_g2 = dma(smalls[0:1, 8:16], g2_d.rearrange("(o e) -> o e", o=1))
    ev_bout = dma(AFa[0:1, 0:1024], bout_d.rearrange("(o d) -> o d", o=1))

    ev_m1 = pool(lambda e: e.memset(onesc_b[:], 1.0))
    ev_m2 = pool(lambda e: e.memset(ones1_f[:], 1.0))
    ev_m3 = pool(lambda e: e.memset(kaug[64:65, :], 1.0))
    ev_m4 = pool(lambda e: e.memset(
        v_o_aug[:].rearrange("p (b c) -> p b c", c=65)[:, :, 64:65], 1.0))
    ev_m5 = pool(lambda e: e.memset(
        v2_aug.rearrange("p (b c) -> p b c", c=129)[:, :, 128:129], 1.0))

    def bcast_row(dst, src_row, n, evs):
        evm = mm(pb[6][:, 0:n], ones1_f[0:1, 0:128], src_row,
                 waits=list(evs) + [ev_m2, ev_pb[6]])
        ev = act(dst, pb[6][:, 0:n], waits=[evm])
        ev_pb[6] = ev
        return ev

    ev_s1 = act(smalls[0:1, 16:24], smalls[0:1, 0:8], AF.Sigmoid, waits=[ev_g1])
    ev_s2 = act(smalls[0:1, 24:32], smalls[0:1, 8:16], AF.Sigmoid, waits=[ev_s1, ev_g2])
    ev_sig1 = bcast_row(sig1[:], smalls[0:1, 16:24], 8, [ev_s1])
    ev_sig2 = bcast_row(sig2[:], smalls[0:1, 24:32], 8, [ev_s2])
    ev_bo1 = bcast_row(bout_bc[:, 0:512], AFa[0:1, 0:512], 512, [ev_bout])
    ev_bo2 = bcast_row(bout_bc[:, 512:1024], AFa[0:1, 512:1024], 512, [ev_bout])

    def sim_colnorm(dst, sim_sb, ev_sim, sq_off):
        sq = AB[:, sq_off:sq_off + 64]
        e1 = act(sq, sim_sb[:], AF.Square, waits=[ev_sim])
        e2 = mm(pb[5][0:1, 0:64], onesc_b[:], sq, waits=[e1, ev_m1, ev_pb[5]])
        row = row512[0:1, 0:64]
        e3 = act(row, pb[5][0:1, 0:64], waits=[e2])
        ev_pb[5] = e3
        rowv = row.rearrange("o (k e) -> o e k", k=8)
        ssr = row512b[0:1, 0:8]
        e4 = dve(lambda e, o=ssr, i=rowv: e.tensor_reduce(out=o, in_=i, axis=AX.X, op=ALU.add), waits=[e3])
        e5 = dve(lambda e, o=ssr: e.tensor_scalar_max(out=o, in0=o, scalar1=1e-24), waits=[e4])
        e6 = act(ssr, ssr, AF.Ln, waits=[e5])
        e7 = act(ssr, ssr, AF.Exp, scale=-0.5, waits=[e6])
        return bcast_row(dst, ssr, 8, [e7])

    ev_rc1 = sim_colnorm(rcol1[:], sim1, ev_sim1, 0)
    ev_rc2 = sim_colnorm(rcol2[:], sim2, ev_sim2, 64)

    ev_emb = act(emb_ext[0:32, :], rel_sb[:], AF.Copy, scale=8.0, waits=[ev_rel])
    ev_embm = pool(lambda e: e.memset(emb_ext[32:33, :], NEG))

    P.flush()

    kp = AB[:, 0:8192]
    vp = AB[:, 8192:16384]
    xn1T = AB[:, 16384:20480]
    xnb = AB[:, 20480:21504]
    sqs = AB[:, 21504:22016]
    rwb = AB[:, 22016:22528]
    r2bc = AB[:, 22528:23040]

    ev_kp = dma(kp.rearrange("p (e k h) -> p e k h", k=8, h=128), kp_d.rearrange("e (k p) h -> p e k h", p=128), q="gpsimd")
    ev_vp = dma(vp.rearrange("p (e k h) -> p e k h", k=8, h=128), vp_d.rearrange("e (k p) h -> p e k h", p=128), q="gpsimd")
    ev_wkv = dma(wkv.rearrange("p (k d) -> p k d", d=128), Wkv_d.rearrange("(k p) d -> p k d", p=128), q="gpsimd")
    ev_qp = None

    xin = [AFa[:, 0:1024], AFa[:, 1024:2048]]
    accA = AFa[:, 2048:2560]
    accB = AFa[:, 2560:3072]

    def ln_stats(xblk_f, cols, waits):
        c = cols
        bn6 = smalls[:, c:c + 12]
        mv = smalls[:, c + 12:c + 14]
        rstd = smalls[:, c + 14:c + 15]
        nmr = smalls[:, c + 15:c + 16]
        rstd2 = smalls[:, c + 16:c + 17]
        tmp = smalls[:, c + 17:c + 18]
        ru = smalls[:, c + 18:c + 19]
        e0 = dve(lambda e, o=bn6, i=xblk_f: e.bn_stats(out=o[:, 0:6], in_=i[:, 0:512]), waits=waits)
        e1 = dve(lambda e, o=bn6, i=xblk_f: e.bn_stats(out=o[:, 6:12], in_=i[:, 512:1024]), waits=waits + [e0])
        e2 = dve(lambda e, o=mv, i=bn6: e.bn_aggr(out=o, in_=i), waits=[e1])
        e3 = dve(lambda e, o=tmp, i=mv: e.tensor_scalar_add(out=o, in0=i[:, 1:2], scalar1=EPS), waits=[e2])
        e4 = act(rstd, tmp, AF.Ln, waits=[e3])
        e5 = act(rstd, rstd, AF.Exp, scale=-0.5, waits=[e4])
        e6 = dve(lambda e, o=nmr, i=mv: e.tensor_scalar_mul(
            out=o, in0=i[:, 0:1], scalar1=-1.0), waits=[e5, e2])
        e7 = act(ru, rstd, AF.Square, waits=[e5])
        e8 = dve(lambda e, o=tmp, i=mv, r=ru: e.tensor_tensor(out=o, in0=i[:, 1:2], in1=r, op=ALU.mult), waits=[e7, e6])
        e9 = dve(lambda e, o=tmp: e.tensor_scalar_add(out=o, in0=o, scalar1=EPS), waits=[e8])
        e10 = act(rstd2, tmp, AF.Ln, waits=[e9])
        e11 = act(rstd2, rstd2, AF.Exp, scale=-0.5, waits=[e10])
        return e6, rstd, nmr, rstd2, e11

    def gate_block(ev_ready, g, rncol, ev_rn, sigb, rcolb, rw_dst, cols, extra=()):
        c = cols
        t1 = smalls[:, c + 20:c + 28]
        t2 = smalls[:, c + 28:c + 36]
        m8 = smalls[:, c + 36:c + 44]
        af = smalls[:, c + 44:c + 45]
        den = smalls[:, c + 45:c + 46]
        e1 = dve(lambda e, o=g, r=rncol: e.tensor_scalar_mul(out=o, in0=o, scalar1=r),
                 waits=[ev_ready, ev_rn] + list(extra))
        e2 = dve(lambda e, o=g, r=rcolb: e.tensor_tensor(out=o, in0=o, in1=r, op=ALU.mult), waits=[e1])
        e3 = dve(lambda e, o=g, s=sigb: e.tensor_tensor(out=o, in0=o, in1=s, op=ALU.subtract), waits=[e2])
        e4 = dve(lambda e, o=m8, i=g: e.max(out=o, in_=i), waits=[e3])
        e5 = dve(lambda e, o=t1, i=g, m=m8: e.tensor_scalar(
            out=o, in0=i, scalar1=m[:, 0:1], scalar2=None, op0=ALU.is_equal), waits=[e4])
        e6 = dve(lambda e, o=t2, i=g, m=m8: e.tensor_scalar(
            out=o, in0=i, scalar1=m[:, 1:2], scalar2=None, op0=ALU.is_equal), waits=[e5])
        e7 = dve(lambda e, o=t1, i=t2: e.tensor_tensor(out=o, in0=o, in1=i, op=ALU.add), waits=[e6])
        e8 = dve(lambda e, o=t2, i=g: e.tensor_single_scalar(out=o, in_=i, scalar=0.0, op=ALU.is_gt), waits=[e7])
        e9 = dve(lambda e, o=af, m=m8: e.tensor_single_scalar(out=o, in_=m[:, 0:1], scalar=0.0, op=ALU.is_gt), waits=[e8])
        e10 = dve(lambda e, o=t2, i=t1: e.tensor_tensor(out=o, in0=o, in1=i, op=ALU.subtract), waits=[e9])
        e11 = dve(lambda e, o=t1, i=t2, a=af: e.scalar_tensor_tensor(
            out=o, in0=i, scalar=a, in1=o, op0=ALU.mult, op1=ALU.add), waits=[e10])
        e12 = dve(lambda e, o=t2, i=t1: e.tensor_single_scalar(out=o, in_=i, scalar=0.0, op=ALU.is_gt), waits=[e11])
        e13 = dve(lambda e, o=g: e.tensor_scalar_max(out=o, in0=o, scalar1=0.0), waits=[e12])
        e14 = dve(lambda e, o=g, u=t2: e.tensor_tensor(out=o, in0=o, in1=u, op=ALU.mult), waits=[e13])
        e15 = dve(lambda e, o=t1, u=t2: e.tensor_scalar(
            out=o, in0=u, scalar1=-NEG, scalar2=NEG, op0=ALU.mult, op1=ALU.add), waits=[e14])
        e16 = dve(lambda e, o=g, i=t1: e.tensor_tensor(out=o, in0=o, in1=i, op=ALU.add), waits=[e15])
        e17 = dve(lambda e, o=t1, i=g: e.tensor_reduce(out=o[:, 0:1], in_=i, axis=AX.X, op=ALU.max, negate=True), waits=[e16])
        e18 = act(rw_dst, g, AF.Exp, bias=t1[:, 0:1], accum=den, waits=[e17])
        e19 = dve(lambda e, o=den: e.reciprocal(out=o, in_=o), waits=[e18])
        e20 = dve(lambda e, o=rw_dst, r=den: e.tensor_scalar_mul(out=o, in0=o, scalar1=r), waits=[e19])
        return e20

    tiles = [("full", t) for t in range(8)] + [("own", t) for t in range(4)]
    pro_smalls = ev_sig2
    pro_row512 = ev_rc2
    xin_free = [None, None]
    st = {
        "rwb_free": ev_lnc, "r2bc_free": None, "accA_free": ev_lnc, "accB_free": None,
        "cap": None,
        "sqs_free": None, "xn1T_free": None, "vfm_free": None,
    }
    ev_k2 = []; ev_v2 = []; ev_q2 = []; ev_kaug = [ev_m3]; ev_voa = [ev_m4]
    ev_v2a = [ev_m5]; ev_rw_ownd = []; ev_xnownd = []
    last_full_proj = []

    for tix, (kind, t) in enumerate(tiles):
        buf = tix % 2
        src = x_full if kind == "full" else x_own
        base_tok = t * 512
        if kind == "own" and t == 0:
            ev_qp = dma(kp.rearrange("p (e k h) -> p e k h", k=8, h=128),
                        qp_d.rearrange("e (k p) h -> p e k h", p=128),
                        waits=last_full_proj, q="gpsimd")
        ev_xnT_blk = []
        ev_r2t = []
        for b in range(4):
            bbuf = (tix * 4 + b) % 2
            xblk = xin[bbuf]
            tok0 = base_tok + b * 128
            ev_x = dma(xblk, src[tok0:tok0 + 128, :], waits=[xin_free[bbuf]])
            cols = 48 * b
            evm, rstd, nmr, rstd2, ev_r2 = ln_stats(
                xblk, cols, [ev_x] + ([pro_smalls] if tix == 0 else []))
            if kind == "own":
                blk = t * 4 + b
                evxn = pool(lambda e, o=partial[:, blk * D:(blk + 1) * D], i=xblk,
                            a=nmr, r=rstd: e.tensor_scalar(
                    out=o, in0=i, scalar1=a, scalar2=r, op0=ALU.add, op1=ALU.mult),
                    waits=[evm])
                ev_xnownd.append(evxn)
                evc = act(xnb, partial[:, blk * D:(blk + 1) * D], AF.Copy,
                          waits=[evxn, (st.get("sqx_free") if b == 0 else None)])
            else:
                evc = pool(lambda e, o=xnb, i=xblk, a=nmr, r=rstd: e.tensor_scalar(
                    out=o, in0=i, scalar1=a, scalar2=r, op0=ALU.add, op1=ALU.mult),
                    waits=[evm, (st.get("sqx_free") if b == 0 else None)])
            evtr = []
            for k in range(8):
                evtr.append(tr(pbt[:, k * 128:(k + 1) * 128], xnb[:, k * 128:(k + 1) * 128],
                               ident_b[:], waits=[evc, state["pbt"], ev_idb]))
            evcp = act(xn1T.rearrange("p (k tt) -> p k tt", tt=512)[:, :, b * 128:(b + 1) * 128],
                       pbt[:].rearrange("p (k tt) -> p k tt", tt=128),
                       waits=[evtr[-1], st["xn1T_free"]])
            state["pbt"] = evcp
            ev_xnT_blk.append(evcp)
            ev_r2t.append(tr(pb[6][0:1, b * 128:(b + 1) * 128], rstd2, ident_f[:],
                             waits=[ev_r2, ev_pb[6], ev_idf]))
            xin_free[bbuf] = evc
        ev_row = act(row512[0:1, 0:512], pb[6][0:1, 0:512],
                     waits=[ev_r2t[-1]] + ([pro_row512] if tix == 0 else []))
        ev_pb[6] = ev_row
        ev_bb = mm(pb[5][:, 0:512], ones1_f[0:1, 0:128], row512[0:1, 0:512],
                   waits=[ev_row, ev_pb[5]])
        ev_r2b = act(r2bc, pb[5][:, 0:512], waits=[ev_bb, st["r2bc_free"], st.get("cap")])
        ev_pb[5] = ev_r2b

        if kind == "full":
            ev_kvm = []
            for k in range(8):
                ev_kvm.append(mm(pb[0][:, 0:512], wkv[:, k * 128:(k + 1) * 128],
                                 xn1T[:, k * 512:(k + 1) * 512],
                                 start=(k == 0), stop=(k == 7),
                                 waits=[ev_wkv, ev_pb[0] if k == 0 else None,
                                        ev_xnT_blk[-1]]))
            ev_k = act(kaug[0:64, base_tok:base_tok + 512], pb[0][0:64, 0:512], waits=[ev_kvm[-1]])
            ev_kaug.append(ev_k)
            ev_v = act(vfm[:], pb[0][64:128, 0:512], waits=[ev_kvm[-1], st["vfm_free"]])
            ev_pb[0] = ev_v
            evvt = []
            for b in range(4):
                evvt.append(tr(pbt[:, b * 64:(b + 1) * 64], vfm[:, b * 128:(b + 1) * 128],
                               ident_b[0:64, 0:64], waits=[ev_v, state["pbt"]]))
            ev_vo = act(v_o_aug[:, (4 * t) * 65:(4 * t + 4) * 65].rearrange(
                "p (b c) -> p b c", c=65)[:, :, 0:64],
                pbt[:, 0:256].rearrange("p (b c) -> p b c", c=64),
                waits=[evvt[-1]])
            state["pbt"] = ev_vo
            st["vfm_free"] = ev_vo
            ev_voa.append(ev_vo)
            n1dep = ev_kvm[-1]
        else:
            n1dep = None

        ev_n1 = []
        eprev = [st["accB_free"], st["accA_free"]]
        for k in range(8):
            ch = xn1T[:, k * 512:(k + 1) * 512]
            eng = pool if k % 2 == 0 else dve
            acc = accB if k % 2 == 0 else accA
            e1 = eng(lambda e, o=acc, i=ch, r=r2bc: e.tensor_tensor(out=o, in0=i, in1=r, op=ALU.mult),
                     waits=[ev_r2b, n1dep, ev_xnT_blk[-1], eprev[k % 2]])
            e2 = eng(lambda e, o=ch, i=acc, kk=k: e.tensor_scalar(
                out=o, in0=i, scalar1=ln1w_c[:, kk:kk + 1], scalar2=ln1b_c[:, kk:kk + 1],
                op0=ALU.mult, op1=ALU.add), waits=[e1] + ev_ln1)
            eprev[k % 2] = e2
            ev_n1.append(e2)
        st["accB_free"] = eprev[0]
        st["accA_free"] = eprev[1]

        if debug and tix == 0 and BUILD_PHASES < 2:
            ev_cap_n1 = act(AB2[:, 10272:14368], xn1T[:], waits=[ev_n1[-1]])
            ev_cap_r2 = act(AB2[:, 14368:14880], r2bc, waits=[ev_r2b, ev_cap_n1])
            st["cap"] = ev_cap_r2
        prev_bufev = [st["sqs_free"], st.get("sqx_free")]
        ev_sq_mm = []
        for k in range(8):
            buf = sqs if k % 2 == 0 else xnb[:, 0:512]
            esq = act(buf, xn1T[:, k * 512:(k + 1) * 512], AF.Square,
                      waits=[ev_n1[k], prev_bufev[k % 2]])
            emm = mm(pb[0][0:1, 0:512], onesc_b[:], buf, start=(k == 0), stop=(k == 7),
                     waits=[esq, ev_pb[0] if k == 0 else None, ev_m1])
            prev_bufev[k % 2] = emm
            ev_sq_mm.append(emm)
        st["sqs_free"] = prev_bufev[0]
        st["sqx_free"] = prev_bufev[1]
        ev_ssr = dve(lambda e, o=row512b[0:1, 0:512], i_=pb[0][0:1, 0:512]:
                     e.tensor_scalar_max(out=o, in0=i_, scalar1=1e-24),
                     waits=[ev_sq_mm[-1]])
        ev_pb[0] = ev_ssr
        ev_ss2 = act(row512b[0:1, 0:512], row512b[0:1, 0:512], AF.Ln, waits=[ev_ssr])
        ev_ss3 = act(row512b[0:1, 0:512], row512b[0:1, 0:512], AF.Exp, scale=-0.5, waits=[ev_ss2])
        ev_rnt = []
        for b in range(4):
            ev_rnt.append(tr(pb[6][:, b:b + 1], row512b[0:1, b * 128:(b + 1) * 128],
                             ident_f[0:1, 0:1], waits=[ev_ss3, ev_pb[6]]))
        ev_rn = act(rn_cols[:], pb[6][:, 0:4], waits=[ev_rnt[-1]])
        ev_pb[6] = ev_rn

        ev_rawm = []
        for k in range(8):
            ev_rawm.append(mm(pb[4][0:8, 0:512], sim1[:, k * 8:(k + 1) * 8],
                              xn1T[:, k * 512:(k + 1) * 512],
                              start=(k == 0), stop=(k == 7),
                              waits=[ev_n1[k], ev_pb[4] if k == 0 else None, ev_sim1]))
        ev_lgT = act(lgT[:], pb[4][0:8, 0:512], waits=[ev_rawm[-1]])
        ev_pb[4] = ev_lgT
        ev_lgt = []
        for b in range(4):
            ev_lgt.append(tr(pb[3][:, b * 8:(b + 1) * 8], lgT[:, b * 128:(b + 1) * 128],
                             ident_f[0:8, 0:8], waits=[ev_lgT, ev_pb[3]]))
        ev_lgtm = act(lg_tm[:], pb[3][:, 0:32], waits=[ev_lgt[-1]])
        ev_pb[3] = ev_lgtm
        if debug and BUILD_PHASES < 3:
            ev_dbg1 = act(d_stash[:, tix * 36:tix * 36 + 32], lg_tm[:], waits=[ev_lgtm])
            ev_dbg2 = act(d_stash[:, tix * 36 + 32:tix * 36 + 36], rn_cols[:], waits=[ev_rn, ev_dbg1])
            ev_lgtm = ev_dbg1

        ev_rw = []
        for b in range(4):
            cols = 48 * b
            g = lg_tm[:, b * 8:(b + 1) * 8]
            if kind == "own":
                blk = t * 4 + b
                rw_dst = rw_own[:, blk * 8:(blk + 1) * 8]
            else:
                rw_dst = rwtm[:, b * 8:(b + 1) * 8]
            ev = gate_block(ev_lgtm, g, rn_cols[:, b:b + 1], ev_rn,
                            sig1[:], rcol1[:], rw_dst, cols, extra=[ev_sig1, ev_rc1])
            ev_rw.append(ev)
            if kind == "own":
                ev_rw_ownd.append(ev)
        ev_rwt = []
        for b in range(4):
            srcrw = (rw_own[:, (t * 4 + b) * 8:(t * 4 + b + 1) * 8] if kind == "own"
                     else rwtm[:, b * 8:(b + 1) * 8])
            ev_rwt.append(tr(pb[3][0:8, b * 128:(b + 1) * 128], srcrw, ident_f[:],
                             waits=[ev_rw[b], ev_pb[3]]))
        ev_rwT = act(rwT[:], pb[3][0:8, 0:512], waits=[ev_rwt[-1]])
        ev_pb[3] = ev_rwT

        def proj_pass(wsb, ev_w, acc, acc_free):
            ev_acc = None
            ev_tmp_free = st["accB_free"]
            lastmm = None
            for e in range(NE):
                bank_i = 1 + (e % 2)
                bank = pb[bank_i]
                evb1 = mm(pb[5][:, 0:512], e_one[0:8, e * 128:(e + 1) * 128], rwT[:],
                          waits=[ev_rwT, ev_pb[5], ev_eone])
                evb2 = act(rwb, pb[5][:, 0:512], waits=[evb1, st["rwb_free"]])
                ev_pb[5] = evb2
                evm = None
                for k in range(8):
                    evm = mm(bank[:, 0:512], wsb[:, (e * 8 + k) * 128:(e * 8 + k + 1) * 128],
                             xn1T[:, k * 512:(k + 1) * 512],
                             start=(k == 0), stop=(k == 7),
                             waits=[ev_n1[k], ev_w,
                                    ev_pb[bank_i] if k == 0 else None])
                lastmm = evm
                if e == 0:
                    ev_acc = dve(lambda ee, o=acc, pp=bank, r=rwb: ee.tensor_tensor(
                        out=o, in0=pp[:, 0:512], in1=r, op=ALU.mult),
                        waits=[evm, evb2, acc_free])
                    ev_pb[bank_i] = ev_acc
                    st["rwb_free"] = ev_acc
                else:
                    evt_ = dve(lambda ee, pp=bank, r=rwb: ee.tensor_tensor(
                        out=accB, in0=pp[:, 0:512], in1=r, op=ALU.mult),
                        waits=[evm, evb2, ev_tmp_free])
                    ev_pb[bank_i] = evt_
                    st["rwb_free"] = evt_
                    ev_acc = dve(lambda ee, o=acc: ee.tensor_tensor(
                        out=o, in0=o, in1=accB, op=ALU.add), waits=[evt_, ev_acc])
                    ev_tmp_free = ev_acc
            st["accB_free"] = ev_tmp_free
            return ev_acc, lastmm

        if kind == "full":
            ev_acc, mmk = proj_pass(kp, ev_kp, accA, st["accA_free"])
            evk2 = act(k2_fm[:, base_tok:base_tok + 512], accA, waits=[ev_acc])
            ev_k2.append(evk2)
            ev_acc2, mmv = proj_pass(vp, ev_vp, accA, evk2)
            e1 = act(sqs, accA, AF.Copy, waits=[ev_acc2, st["sqs_free"]])
            st["sqs_free"] = None
            evvt2 = []
            for b in range(4):
                evvt2.append(tr(pbt[:, b * 128:(b + 1) * 128], sqs[:, b * 128:(b + 1) * 128],
                                ident_b[:], waits=[e1, state["pbt"]]))
            evv2 = act(v2_aug[(slice(None))].rearrange("p (bb c) -> p bb c", c=129)[:, 4 * t:4 * t + 4, 0:128],
                       pbt[:, 0:512].rearrange("p (bb c) -> p bb c", c=128),
                       waits=[evvt2[-1]])
            state["pbt"] = evv2
            st["sqs_free"] = evvt2[-1]
            ev_v2a.append(evv2)
            st["accA_free"] = e1
            last_full_proj = [mmk, mmv]
        else:
            ev_acc, mmq = proj_pass(kp, ev_qp, accA, st["accA_free"])
            evq2 = act(q2_fm[:, base_tok:base_tok + 512], accA, AF.Copy,
                       scale=Q2_SCALE, waits=[ev_acc])
            ev_q2.append(evq2)
            st["accA_free"] = evq2

        st["xn1T_free"] = ev_rawm[-1]
        if kind == "full":
            st["xn1T_free"] = last_full_proj[-1]
        else:
            st["xn1T_free"] = mmq
        st["r2bc_free"] = ev_n1[-1]

    P.flush()

    if BUILD_PHASES >= 2:
        opj = AB[:, 0:8192]
        ev_opj = dma(opj.rearrange("p (e c) -> p e c", c=1024),
                     op_d.rearrange("e h c -> h e c"), q="gpsimd")
        ptb = [AB[:, 8192:8448], AB[:, 8448:8704], AB[:, 8704:8960], AB[:, 8960:9216]]
        a2tm = AB[:, 9216:9344]
        a2T = AB[:, 9344:9472]
        sq1 = AB[:, 9472:9600]
        n2a = AFa[:, 1024:2048]
        nbcS = [AFa[:, 2048:2560], AFa[:, 512:1024]]
        nearS = AFa[:, 2560:2816]
        ev_hs = []
        ev_n2T = []
        ev_rw2 = []
        st["ptb1"] = [None, None, None, None]
        st["a2T_free"] = None
        st["nearS_free"] = None
        st["n2a_free"] = None
        st["sq1_free"] = None
        ev_k2l = ev_k2[-1]
        ev_v2l = ev_v2a[-1]
        ev_q2l = ev_q2[-1]

        from collections import deque
        bg1 = deque()

        def bg1_pop(n=1):
            for _ in range(n):
                if bg1:
                    bg1.popleft()()

        def mk_front1(i):
            nkb = 2 * i + 2
            far = nkb - 2
            nch = (far * 128 + 511) // 512
            cols = 48 * (i % 4)
            steps = []

            def s_chunk(c):
                def go():
                    qsl = q2_fm[:, i * 128:(i + 1) * 128]
                    tset = smalls[:, cols + 24:cols + 36]
                    gw = min(512, far * 128 - c * 512)
                    evm_ = mm(pb[5][:, 0:gw], qsl, k2_fm[:, c * 512:c * 512 + gw],
                              waits=[ev_q2l, ev_k2l, ev_pb[5]])
                    evr = dve(lambda e, o=tset[:, c:c + 1], i_=pb[5][:, 0:gw]:
                              e.tensor_reduce(out=o, in_=i_, axis=AX.X, op=ALU.max),
                              waits=[evm_, st.get("tset1_free_%d" % (i % 4))])
                    ev_pb[5] = evr
                    st.setdefault("evred1_%d" % i, []).append(evr)
                return go

            def s_tail():
                def go():
                    qsl = q2_fm[:, i * 128:(i + 1) * 128]
                    tset = smalls[:, cols + 24:cols + 36]
                    mcol = smalls[:, cols + 46:cols + 47]
                    evmn = mm(pb[5][:, 0:256], qsl, k2_fm[:, (nkb - 2) * 128:nkb * 128],
                              waits=[ev_q2l, ev_k2l, ev_pb[5]])
                    evadd = dve(lambda e, o=nearS, i_=pb[5][:, 0:256]: e.tensor_tensor(
                        out=o, in0=i_, in1=ca_t[:], op=ALU.add),
                        waits=[evmn, ev_ca, st["nearS_free"]])
                    ev_pb[5] = evadd
                    evrn = dve(lambda e, o=tset[:, nch:nch + 1], i_=nearS: e.tensor_reduce(
                        out=o, in_=i_, axis=AX.X, op=ALU.max), waits=[evadd])
                    st["nearS_free"] = evrn
                    evfm = dve(lambda e, o=mcol, i_=tset[:, 0:nch + 1]: e.tensor_reduce(
                        out=o, in_=i_, axis=AX.X, op=ALU.max, negate=True),
                        waits=[evrn] + st.get("evred1_%d" % i, []) + [st.get("mcol1_free_%d" % (i % 4))])
                    st["tset1_free_%d" % (i % 4)] = evfm
                    st["evfm1_%d" % i] = evfm
                return go

            def s_bcast():
                def go():
                    mcol = smalls[:, cols + 46:cols + 47]
                    nb = nbcS[i % 2]
                    evt_ = tr(pb[6][0:1, 0:128], mcol, ident_f[:],
                              waits=[st["evfm1_%d" % i], ev_pb[6]])
                    evrow = act(row512[0:1, 0:128], pb[6][0:1, 0:128],
                                waits=[evt_, st.get("row512_free")])
                    ev_pb[6] = evrow
                    st["mcol1_free_%d" % (i % 4)] = evt_
                    evbb = mm(pb[6][:, 0:128], ones1_f[0:1, 0:128], row512[0:1, 0:128],
                              waits=[evrow, ev_pb[6]])
                    st["row512_free"] = evbb
                    evnbca = act(nb[:, 0:128], pb[6][:, 0:128],
                                 waits=[evbb, st.get("nbc_free_%d" % (i % 2))])
                    evnbc = act(nb[:, 128:256], pb[6][:, 0:128], waits=[evnbca])
                    ev_pb[6] = evnbc
                    evnbc0 = dve(lambda e, o=nb[:, 256:384]: e.tensor_tensor(
                        out=o, in0=nb[:, 0:128], in1=ct_t[:, 0:128], op=ALU.add),
                        waits=[evnbc, ev_ct, st.get("nbc_free_%d" % (i % 2))])
                    evnbc1 = dve(lambda e, o=nb[:, 384:512]: e.tensor_tensor(
                        out=o, in0=nb[:, 0:128], in1=ct_t[:, 128:256], op=ALU.add),
                        waits=[evnbc, evnbc0])
                    st["nbc_done_%d" % i] = (evnbc, evnbc0, evnbc1)
                return go

            for c in range(nch):
                steps.append(s_chunk(c))
            steps.append(s_tail())
            steps.append(s_bcast())
            return steps

        bg1.extend(mk_front1(0))
        for i in range(NOWN):
            if i + 1 < NOWN:
                bg1.extend(mk_front1(i + 1))
            nkb = 2 * i + 2
            qsl = q2_fm[:, i * 128:(i + 1) * 128]
            cols = 48 * (i % 4)
            while st.get("nbc_done_%d" % i) is None and bg1:
                bg1.popleft()()
            evnbc, evnbc0, evnbc1 = st["nbc_done_%d" % i]
            nb = nbcS[i % 2]
            ev_ctx = None
            last_evs = None
            npair = nkb // 2
            pending1 = None

            def emit_av1(pav1):
                nonlocal ev_ctx
                kb0p, evep = pav1
                pt_ = ptb[(kb0p // 2) % 4]
                evav = None
                for u in range(2):
                    evav = mm(pb[4][:, 0:129], pt_[:, u * 128:(u + 1) * 128],
                              v2_aug[:, (kb0p + u) * 129:(kb0p + u + 1) * 129],
                              start=(kb0p + u == 0), stop=(kb0p + u == nkb - 1),
                              waits=[evep, ev_v2l, ev_m5,
                                     ev_pb[4] if kb0p + u == 0 else None])
                st["ptb1"][(kb0p // 2) % 4] = evav
                ev_ctx = evav

            for kp_ in range(npair):
                kb0 = 2 * kp_
                bsel = kp_ % 3
                evl = None
                for u in range(2):
                    evl = mm(pb[bsel][:, u * 128:(u + 1) * 128],
                             k2_fm[:, (kb0 + u) * 128:(kb0 + u + 1) * 128], qsl,
                             waits=[(ev_pb[bsel] if u == 0 else None), ev_k2l, ev_q2l])
                bg1_pop(1)
                if kp_ == npair - 1:
                    addt = nb[:, 256:512]; evadd_w = [evnbc1]
                else:
                    addt = nb[:, 0:256]; evadd_w = [evnbc]
                evs = dve(lambda e, o=nearS, i_=pb[bsel][:, 0:256], a=addt:
                          e.tensor_tensor(out=o, in0=i_, in1=a, op=ALU.add),
                          waits=[evl] + evadd_w + [st.get("nearS_free")])
                ev_pb[bsel] = evs
                last_evs = evs
                pt = ptb[kp_ % 4]
                eve = act(pt, nearS, AF.Exp, waits=[evs, st["ptb1"][kp_ % 4]])
                st["nearS_free"] = eve
                if pending1 is not None:
                    emit_av1(pending1)
                pending1 = (kb0, eve)
                bg1_pop(1)
            if pending1 is not None:
                emit_av1(pending1)
            st["nbc_free_%d" % (i % 2)] = last_evs
            rl = smalls[:, cols + 47:cols + 48]
            evrl = dve(lambda e, o=rl: e.reciprocal(out=o, in_=pb[4][:, 128:129]), waits=[ev_ctx])
            eva2 = dve(lambda e, o=a2tm, r=rl: e.tensor_scalar_mul(
                out=o, in0=pb[4][:, 0:128], scalar1=r), waits=[evrl, st.get("a2tm_free")])
            ev_pb[4] = eva2
            evt2 = tr(pbt[:, 0:128], a2tm, ident_b[:], waits=[eva2, state["pbt"]])
            eva2T = act(a2T, pbt[:, 0:128], waits=[evt2, st["a2T_free"]])
            state["pbt"] = eva2T
            st["a2tm_free"] = evt2
            bg1_pop(2)
            hsblk = partial[:, i * D:(i + 1) * D]
            bankrot = 0
            ev_half_read = {0: None, 1: None}
            for e_ in range(NE):
                for h_ in range(2):
                    bk = bankrot % 2
                    bankrot += 1
                    evmo = mm(pb[bk][:, 0:512], a2T,
                              opj[:, e_ * 1024 + h_ * 512:e_ * 1024 + (h_ + 1) * 512],
                              waits=[eva2T, ev_opj, ev_pb[bk]])
                    evst = dve(lambda e, o=hsblk.rearrange("p (h c) -> p h c", c=512),
                               pp=pb[bk][:, 0:512], rw=rw_own[:, i * 8 + e_:i * 8 + e_ + 1],
                               hh=h_: e.scalar_tensor_tensor(
                        out=o[:, hh, :], in0=pp, scalar=rw, in1=o[:, hh, :],
                        op0=ALU.mult, op1=ALU.add),
                        waits=[evmo, ev_rw_ownd[-1], ev_half_read[h_]])
                    ev_pb[bk] = evst
                    ev_half_read[h_] = evst
                if e_ % 3 == 2:
                    bg1_pop(1)
            ev_hs_done = ev_half_read[0]
            ev_hs_done2 = ev_half_read[1]
            ev_hs.append(ev_hs_done2)
            evm2, rstd_, nmr_, _r2, _e11 = ln_stats(hsblk, cols, [ev_hs_done, ev_hs_done2])
            evn2a = pool(lambda e, o=n2a, i_=hsblk, a=nmr_, r=rstd_: e.tensor_scalar(
                out=o, in0=i_, scalar1=a, scalar2=r, op0=ALU.add, op1=ALU.mult),
                waits=[evm2, st.get("n2a_free")])
            ev_tr2 = []
            for k in range(8):
                bsel = 6 if k % 2 == 0 else 3
                evt3 = tr(pb[bsel][:, (k // 2 % 2) * 256:(k // 2 % 2) * 256 + 128],
                          n2a[:, k * 128:(k + 1) * 128], ident_f[:],
                          waits=[evn2a, ev_pb[bsel]])
                evw = dve(lambda e, o=n2T.rearrange("p (k tt) -> p k tt", tt=2048)[:, k, i * 128:(i + 1) * 128],
                          i_=pb[bsel][:, (k // 2 % 2) * 256:(k // 2 % 2) * 256 + 128], kk=k:
                          e.tensor_scalar(out=o, in0=i_, scalar1=ln2w_c[:, kk:kk + 1],
                                          scalar2=ln2b_c[:, kk:kk + 1], op0=ALU.mult, op1=ALU.add),
                          waits=[evt3] + ev_ln2)
                ev_pb[bsel] = evw
                ev_tr2.append(evw)
            st["n2a_free"] = ev_tr2[-1]
            ev_n2T.append(ev_tr2[-1])
            bg1_pop(2)
            n2v = n2T.rearrange("p (k tt) -> p k tt", tt=2048)
            ev_sqm = []
            prev_sq1 = st.get("sq1_free")
            for k in range(8):
                esq = act(sq1, n2v[:, k, i * 128:(i + 1) * 128], AF.Square,
                          waits=[ev_tr2[k], prev_sq1])
                emm = mm(pb[3][0:1, 256:384] if False else pb[3][0:1, 384:512],
                         onesc_b[:], sq1, start=(k == 0), stop=(k == 7),
                         waits=[esq, ev_pb[3] if k == 0 else None])
                prev_sq1 = emm
                ev_sqm.append(emm)
            st["sq1_free"] = prev_sq1
            evss = dve(lambda e, o=row512b[0:1, 0:128], i_=pb[3][0:1, 384:512]:
                       e.tensor_scalar_max(out=o, in0=i_, scalar1=1e-24),
                       waits=[ev_sqm[-1], st.get("row512b_free")])
            ev_pb[3] = evss
            evs2 = act(row512b[0:1, 0:128], row512b[0:1, 0:128], AF.Ln, waits=[evss])
            evs3 = act(row512b[0:1, 0:128], row512b[0:1, 0:128], AF.Exp, scale=-0.5, waits=[evs2])
            evrt = tr(pb[6][:, 0:1], row512b[0:1, 0:128], ident_f[0:1, 0:1],
                      waits=[evs3, ev_pb[6]])
            st["row512b_free"] = evrt
            evrn2 = act(rn_cols[:, 0:1], pb[6][:, 0:1], waits=[evrt, st.get("rncols_free")])
            ev_pb[6] = evrn2
            ev_rawm2 = []
            for k in range(8):
                ev_rawm2.append(mm(pb[3][0:8, 0:128], sim2[:, k * 8:(k + 1) * 8],
                                   n2v[:, k, i * 128:(i + 1) * 128],
                                   start=(k == 0), stop=(k == 7),
                                   waits=[ev_tr2[k], ev_sim2, ev_pb[3] if k == 0 else None]))
            evlg2 = act(lgT[0:8, 0:128], pb[3][0:8, 0:128], waits=[ev_rawm2[-1], st.get("lgT_free")])
            ev_pb[3] = evlg2
            evlt2 = tr(pb[6][:, 4:12], lgT[0:8, 0:128], ident_f[0:8, 0:8],
                       waits=[evlg2, ev_pb[6]])
            evltm2 = act(lg_tm[:, 0:8], pb[6][:, 4:12], waits=[evlt2, st.get("lgtm_free")])
            ev_pb[6] = evltm2
            st["lgT_free"] = evlt2
            ev = gate_block(evltm2, lg_tm[:, 0:8], rn_cols[:, 0:1], evrn2,
                            sig2[:], rcol2[:], rw2_own[:, i * 8:(i + 1) * 8], cols,
                            extra=[ev_sig2, ev_rc2])
            ev_rw2.append(ev)
            st["rncols_free"] = ev
            st["lgtm_free"] = ev
            bg1_pop(2)
        while bg1:
            bg1.popleft()()

        P.flush()

    if BUILD_PHASES >= 3:
        slots = [AB[:, j * 4096:(j + 1) * 4096] for j in range(6)]
        _abdone = [st["xn1T_free"], ev_rw2[-1], st["sq1_free"], ev_n2T[-1],
                   st["lgT_free"], st["a2T_free"], state["pbt"]]
        slotev = [list(_abdone) for _ in range(6)]
        i0 = AB2[:, 0:4096]
        i1 = AB2[:, 4096:8192]
        ibuf = [i0, i1]
        ibuf_freeev = [None, None]
        ibuf_gelu = [[None] * 8, [None] * 8]
        n2v2 = n2T.rearrange("p (k tt) -> p k tt", tt=2048)
        ev_stt = None
        ev_w1ps = {0: None, 1: None}
        ev_p2 = []

        def w_dma(e_, what, slot_j, waits):
            semn = "LWA" if e_ % 2 == 0 else "LWB"
            if what < 2:
                srcd = w1_d[e_, what * 512:(what + 1) * 512, :]
            else:
                srcd = w2_d[e_, (what - 2) * 512:(what - 1) * 512, :]
            return dma(slots[slot_j].rearrange("p (k i) -> p k i", i=1024),
                       srcd.rearrange("(k p) i -> p k i", p=128),
                       waits=waits, q="gpsimd", sem=semn)

        def slotmap(e_):
            b = (4 * e_) % 6
            return [(b + j) % 6 for j in range(4)]

        ev_wdma = {}
        for w_ in range(4):
            ev_wdma[(0, w_)] = w_dma(0, w_, slotmap(0)[w_], slotev[slotmap(0)[w_]])

        def emit_w1_tile(e_, t_, ib, evw1a, evw1b):
            sm = slotmap(e_)
            last = None
            for j in range(8):
                bk = j % 2
                evm_ = None
                for k in range(8):
                    half = slots[sm[0]] if k < 4 else slots[sm[1]]
                    kk = k % 4
                    evm_ = mm(pb[bk][:, 0:512],
                              half[:, kk * 1024 + j * 128:kk * 1024 + (j + 1) * 128],
                              n2v2[:, k, t_ * 512:(t_ + 1) * 512],
                              start=(k == 0), stop=(k == 7),
                              waits=[evw1a if k < 4 else evw1b,
                                     ev_n2T[4 * t_ + 3],
                                     (ev_w1ps[bk] if k == 0 else None),
                                     (ev_pb[bk] if k == 0 else None)])
                slotev[sm[0]] = [evm_]
                slotev[sm[1]] = [evm_]
                evg = act(ibuf[ib][:, j * 512:(j + 1) * 512], pb[bk][:, 0:512], AF.Gelu,
                          waits=[evm_, ibuf_freeev[ib]])
                ev_w1ps[bk] = evg
                ev_pb[bk] = evg
                ibuf_gelu[ib][j] = evg
                last = evg
            return last

        def emit_w2_tile(e_, t_, ib, evw2a, evw2b):
            nonlocal ev_stt
            sm = slotmap(e_)
            lastmm = None
            for sub in range(4):
                blk = 4 * t_ + sub
                rwcol = rw2_own[:, blk * 8 + e_:blk * 8 + e_ + 1]
                for ch in range(2):
                    bk = 2 + ((2 * sub + ch) % 2)
                    evm_ = None
                    for j in range(8):
                        half = slots[sm[2]] if j < 4 else slots[sm[3]]
                        jj = j % 4
                        evm_ = mm(pb[bk][:, 0:512],
                                  ibuf[ib][:, j * 512 + sub * 128:j * 512 + (sub + 1) * 128],
                                  half[:, jj * 1024 + ch * 512:jj * 1024 + (ch + 1) * 512],
                                  start=(j == 0), stop=(j == 7),
                                  waits=[ibuf_gelu[ib][j],
                                         evw2a if j < 4 else evw2b,
                                         (ev_pb[bk] if j == 0 else None)])
                    slotev[sm[2]] = [evm_]
                    slotev[sm[3]] = [evm_]
                    lastmm = evm_
                    hv = partial[:, blk * D + ch * 512:blk * D + (ch + 1) * 512]
                    evst = dve(lambda e, o=hv, pp=pb[bk][:, 0:512], rw=rwcol:
                               e.scalar_tensor_tensor(out=o, in0=pp, scalar=rw, in1=o,
                                                      op0=ALU.mult, op1=ALU.add),
                               waits=[evm_, ev_rw2[blk], ev_stt, ev_hs[blk]])
                    ev_pb[bk] = evst
                    ev_stt = evst
            ibuf_freeev[ib] = lastmm
            return lastmm

        for e_ in range(NE):
            sm = slotmap(e_)
            if e_ + 1 < NE:
                smn = slotmap(e_ + 1)
                ev_wdma[(e_ + 1, 0)] = w_dma(e_ + 1, 0, smn[0], slotev[smn[0]])
                ev_wdma[(e_ + 1, 1)] = w_dma(e_ + 1, 1, smn[1], slotev[smn[1]])
            ea, eb = ev_wdma[(e_, 0)], ev_wdma[(e_, 1)]
            e2a, e2b = ev_wdma[(e_, 2)], ev_wdma[(e_, 3)]
            emit_w1_tile(e_, 0, 0, ea, eb)
            emit_w1_tile(e_, 1, 1, ea, eb)
            emit_w2_tile(e_, 0, 0, e2a, e2b)
            emit_w1_tile(e_, 2, 0, ea, eb)
            emit_w2_tile(e_, 1, 1, e2a, e2b)
            emit_w1_tile(e_, 3, 1, ea, eb)
            if e_ + 1 < NE:
                smn = slotmap(e_ + 1)
                ev_wdma[(e_ + 1, 2)] = w_dma(e_ + 1, 2, smn[2], slotev[smn[2]])
                ev_wdma[(e_ + 1, 3)] = w_dma(e_ + 1, 3, smn[3], slotev[smn[3]])
            emit_w2_tile(e_, 2, 0, e2a, e2b)
            emit_w2_tile(e_, 3, 1, e2a, e2b)
        ev_badd = []
        for blk in range(NOWN):
            ev_stt = dve(lambda e, o=partial[:, blk * D:(blk + 1) * D]:
                         e.tensor_tensor(out=o, in0=o, in1=bout_bc[:], op=ALU.add),
                         waits=[ev_stt, ev_bo2])
            ev_badd.append(ev_stt)
        ev_partial_done = ev_badd
        P.flush()

    if BUILD_PHASES >= 4:
        xnT_pair = AB[:, 0:2048]
        xnb3 = AB[:, 2048:3072]
        wqv = AB[:, 4096:12288]
        qaugA = AB[:, 12288:14336]
        ptw = [AB[:, 14336:15360], AB[:, 15360:16384]]
        ctxn3 = AB[:, 16384:17408]
        qp64 = AB[:, 17408:21504]
        qaugB = AB[:, 21504:23552]
        xA = AFa[:, 0:1024]
        xB = AFa[:, 1024:2048]
        scrF2 = AFa[:, 2048:3072]
        rlbF = AFa[:, 2560:3072]
        nearP3 = E(nc.sbuf_tensor("nearP3", [128, 384], F32))
        rl_t = [row512[0:1, 0:512], row512b[0:1, 0:512],
                lgT[0:1, 0:512], row512[0:1, 0:512]]
        m_all2 = m_all

        _p2done = [ev_stt, slotev[0][0], slotev[1][0], slotev[2][0], slotev[3][0],
                   slotev[4][0], slotev[5][0]]
        ev_wq = dma(wqv.rearrange("p (k d) -> p k d", d=1024),
                    Wq_d.rearrange("(k p) d -> p k d", p=128),
                    waits=_p2done, q="gpsimd", sem="LWA")
        ev_wo = dma(woutv.rearrange("p (k d) -> p k d", d=1024),
                    Wout_d.rearrange("(k p) d -> p k d", p=128),
                    waits=_p2done, q="gpsimd", sem="LWB")

        ohs_sm = AB[0:33, 23552:24320]
        ev_oh = dma(ohs_sm.rearrange("b (s k) -> b s k", k=256),
                    ohmT_d.rearrange("s b k -> b s k"),
                    waits=_p2done)
        ev_embr = [ev_emb, ev_embm]
        _dT_writes = []
        ev_pack_read = {4: ev_pb[4], 5: ev_pb[5]}
        for slot in range(3):
            for stg in range(16):
                bk = 4 + ((stg // 4) % 2)
                evm_ = None
                for cchunk in range(8):
                    s_val = stg * 8 + cchunk
                    evm_ = mm(pb[bk][:, ((stg % 4) * 8 + cchunk) * 16:((stg % 4) * 8 + cchunk + 1) * 16],
                              ohs_sm[0:33, slot * 256 + 127 - s_val:slot * 256 + 255 - s_val],
                              emb_ext[:], start=True, stop=True,
                              waits=[ev_oh] + ev_embr +
                                    ([ev_pack_read[bk]] if stg % 4 == 0 and cchunk == 0 else []))
                if stg % 4 == 3:
                    base = slot * 2048 + (stg - 3) * 128
                    epk = act(deltaT[:, base:base + 512], pb[bk][:, 0:512], waits=[evm_])
                    ev_pack_read[bk] = epk
                    _dT_writes.append(epk)
        ev_pb[4] = ev_pack_read[4]
        ev_pb[5] = ev_pack_read[5]
        ev_dT_done = _dT_writes[-1]
        ev_dA = {}
        _prev_cp = None
        for slot in range(3):
            for h in range(16):
                tT = deltaT.rearrange("p (s c h) -> p s c h", s=3, h=16)[:, slot, :, h]
                evt_ = tr(pbt[:, 0:128], tT, ident_b[:], waits=[ev_dT_done, state["pbt"]])
                evc_ = act(deltaA[:, (slot * 16 + h) * 128:(slot * 16 + h + 1) * 128],
                           pbt[:, 0:128], waits=[evt_, _prev_cp])
                state["pbt"] = evc_
                _prev_cp = evc_
                ev_dA[(slot, h)] = evc_
        st["qaugB_free"] = slotev[5][0]
        st["qaugB_free2"] = ev_oh

        from collections import deque
        bg = deque()

        def bg_pop(n=1):
            for _ in range(n):
                if not bg:
                    return
                gate, fn = bg[0]
                if gate is not None and st.get(gate) is None:
                    return
                bg.popleft()
                fn()

        st.update({
            "xbuf_free": [None, None], "xnb3_free": None, "xnT_free3": None,
            "qp_read": None, "qaug_free": [None, None], "mall_free": [None, None],
            "nearP3_free": None, "scrF_free_0": None, "scrF_free_1": None,
            "ebank": [None, None], "rlbF_free": None,
            "rlg_free": [st.get("row512_free"), st.get("row512b_free"),
                         st.get("lgT_free"), None],
            "ctxn_free": None, "p3stt": None,
        })
        ev_y = []

        def mk_front(pr):
            blks = [2 * pr, 2 * pr + 1]
            steps = []

            def s_x(bi):
                def go():
                    st["evx_%d" % blks[bi]] = dma(
                        (xA, xB)[bi], x_own[blks[bi] * 128:(blks[bi] + 1) * 128, :],
                        waits=[st["xbuf_free"][bi]] +
                              ([ev_rw2[-1], ev_n2T[-1]] if pr == 0 else []),
                        sem=("LDA" if bi == 0 else "LDB"))
                return go

            def s_ln(bi):
                def go():
                    blk = blks[bi]
                    cols = 48 * (blk % 4)
                    evm3, rstd3, nmr3, _r, _e = ln_stats((xA, xB)[bi], cols,
                                                         [st["evx_%d" % blk]])
                    st["evln_%d" % blk] = pool(
                        lambda e, o=xnb3, i_=(xA, xB)[bi], a=nmr3, r=rstd3: e.tensor_scalar(
                            out=o, in0=i_, scalar1=a, scalar2=r, op0=ALU.add, op1=ALU.mult),
                        waits=[evm3, st.get("xnb3_free")])
                return go

            def s_addx(bi):
                def go():
                    blk = blks[bi]
                    ev = dve(lambda e, o=partial[:, blk * D:(blk + 1) * D], xx=(xA, xB)[bi]:
                             e.tensor_tensor(out=o, in0=o, in1=xx, op=ALU.add),
                             waits=[st["evx_%d" % blk], ev_partial_done[blk],
                                    st["evln_%d" % blk], st["p3stt"]])
                    st["p3stt"] = ev
                    st["xbuf_free"][bi] = ev
                return go

            def s_xnT(bi):
                def go():
                    blk = blks[bi]
                    evtr3 = []
                    for k in range(8):
                        evtr3.append(tr(pbt[:, k * 128:(k + 1) * 128],
                                        xnb3[:, k * 128:(k + 1) * 128],
                                        ident_b[:],
                                        waits=[st["evln_%d" % blk], state["pbt"]]))
                    evcp3 = act(xnT_pair.rearrange("p (k tt) -> p k tt", tt=256)[:, :, bi * 128:(bi + 1) * 128],
                                pbt[:].rearrange("p (k tt) -> p k tt", tt=128),
                                waits=[evtr3[-1], st["xnT_free3"]])
                    state["pbt"] = evcp3
                    st["xnb3_free"] = evtr3[-1]
                    st["evxnT_%d_%d" % (pr, bi)] = evcp3
                return go

            def s_q(dk):
                def go():
                    evq_ = None
                    for k in range(8):
                        evq_ = mm(pb[6][:, 0:256], wqv[:, k * 1024 + dk * 128:k * 1024 + (dk + 1) * 128],
                                  xnT_pair.rearrange("p (k tt) -> p k tt", tt=256)[:, k, :],
                                  start=(k == 0), stop=(k == 7),
                                  waits=[ev_wq, st["evxnT_%d_1" % pr], (ev_pb[6] if k == 0 else None)])
                    evqe0 = act(qp64[0:64, (2 * dk) * 256:(2 * dk + 1) * 256],
                                pb[6][0:64, 0:256], AF.Copy,
                                scale=Q_SCALE, waits=[evq_, st["qp_read"]])
                    evqe = act(qp64[0:64, (2 * dk + 1) * 256:(2 * dk + 2) * 256],
                               pb[6][64:128, 0:256], AF.Copy,
                               scale=Q_SCALE, waits=[evqe0])
                    ev_pb[6] = evqe
                    st["xnT_free3"] = evq_
                    st["evq_%d_%d" % (pr, dk)] = evqe
                return go

            def s_passA_chunk(bi, h, c):
                def go():
                    blk = blks[bi]
                    i = blk
                    nkb = 2 * i + 2
                    nnear = min(3, nkb)
                    far = nkb - nnear
                    cols = 48 * (blk % 4)
                    tset = smalls[:, cols + 24 + (h % 2) * 12:cols + 36 + (h % 2) * 12]
                    qsl = qp64.rearrange("p (hh tt) -> p hh tt", tt=256)[0:64, h, bi * 128:(bi + 1) * 128]
                    gw = min(512, far * 128 - c * 512)
                    evm_ = mm(pb[6][:, 0:gw], qsl, kaug[0:64, c * 512:c * 512 + gw],
                              waits=[st["evq_%d_3" % pr], ev_kaug[-1], ev_pb[6]])
                    evr = dve(lambda e, o=tset[:, c:c + 1], i_=pb[6][:, 0:gw]:
                              e.tensor_reduce(out=o, in_=i_, axis=AX.X, op=ALU.max),
                              waits=[evm_, st.get("tset_free_%d_%d" % (blk % 4, h % 2))])
                    ev_pb[6] = evr
                    st.setdefault("evred_%d_%d" % (blk, h), []).append(evr)
                return go

            def s_passA_tail(bi, h):
                def go():
                    blk = blks[bi]
                    i = blk
                    nkb = 2 * i + 2
                    nnear = min(3, nkb)
                    far = nkb - nnear
                    cols = 48 * (blk % 4)
                    tset = smalls[:, cols + 24 + (h % 2) * 12:cols + 36 + (h % 2) * 12]
                    qsl = qp64.rearrange("p (hh tt) -> p hh tt", tt=256)[0:64, h, bi * 128:(bi + 1) * 128]
                    nch = (far * 128 + 511) // 512
                    nw = nnear * 128
                    evmn = mm(pb[6][:, 0:nw], qsl, kaug[0:64, far * 128:far * 128 + nw],
                              waits=[st["evq_%d_3" % pr], ev_kaug[-1], ev_pb[6]])
                    dsl = deltaA.rearrange("p (s h q) -> p s h q", h=16, q=128)[:, 3 - nnear:3, h, :]
                    evadd = dve(lambda e, o=nearP3[:, 0:nw], i_=pb[6][:, 0:nw], d_=dsl:
                                e.tensor_tensor(out=o, in0=i_, in1=d_, op=ALU.add),
                                waits=[evmn, ev_dA[(2, h)], st["nearP3_free"]])
                    ev_pb[6] = evadd
                    evrn3 = dve(lambda e, o=tset[:, nch:nch + 1], i_=nearP3[:, 0:nw]:
                                e.tensor_reduce(out=o, in_=i_, axis=AX.X, op=ALU.max),
                                waits=[evadd])
                    st["nearP3_free"] = evrn3
                    evfm = dve(lambda e, o=m_all[:, h:h + 1] if bi == 0 else smalls[:, 192 + h:193 + h],
                               i_=tset[:, 0:nch + 1]:
                               e.tensor_reduce(out=o, in_=i_, axis=AX.X, op=ALU.max, negate=True),
                               waits=[evrn3] + st.get("evred_%d_%d" % (blk, h), []) + [st["mall_free"][bi]])
                    st["tset_free_%d_%d" % (blk % 4, h % 2)] = evfm
                    st["evfm_%d_%d" % (blk, h)] = evfm
                return go

            def s_qaug(bi, g):
                def go():
                    blk = blks[bi]
                    qa = qaugA if bi == 0 else qaugB
                    mv = m_all if bi == 0 else smalls[:, 192:208]
                    qsrc = qp64.rearrange("p (hh tt) -> p hh tt", tt=256)[0:64, 4 * g:4 * g + 4,
                                                                          bi * 128:(bi + 1) * 128]
                    evq1 = act(qa[0:64, g * 512:(g + 1) * 512], qsrc,
                               waits=[st["evq_%d_3" % pr], st["qaug_free"][bi]] +
                                     ([st.get("qaugB_free"), st.get("qaugB_free2")] if bi == 1 else []))
                    evts = None
                    for u in range(4):
                        h = 4 * g + u
                        evts = tr(pb[6][0:1, u * 128:(u + 1) * 128], mv[:, h:h + 1],
                                  ident_f[:], waits=[st["evfm_%d_%d" % (blk, h)], ev_pb[6]])
                    evq2_ = act(qa[64:65, g * 512:(g + 1) * 512], pb[6][0:1, 0:512],
                                waits=[evts, evq1])
                    ev_pb[6] = evq2_
                    st["qaug_done_%d_%d" % (blk, g)] = evq2_
                    if g == 3:
                        st["mall_free"][bi] = evq2_
                        if bi == 1:
                            st["qp_read"] = evq1
                return go

            steps.append((None, s_x(0)))
            steps.append((None, s_ln(0)))
            steps.append((None, s_addx(0)))
            steps.append((None, s_xnT(0)))
            steps.append((None, s_x(1)))
            steps.append((None, s_ln(1)))
            steps.append((None, s_addx(1)))
            steps.append((None, s_xnT(1)))
            for dkp in range(4):
                steps.append((None, s_q(dkp)))
            for bi in range(2):
                blk = blks[bi]
                far = (2 * blk + 2) - min(3, 2 * blk + 2)
                nch = (far * 128 + 511) // 512
                for h in range(16):
                    for c in range(nch):
                        steps.append((None, s_passA_chunk(bi, h, c)))
                    steps.append((None, s_passA_tail(bi, h)))
                gate = ("passB_done_%d" % (blks[bi] - 2)) if pr >= 1 else None
                for g in range(4):
                    steps.append((gate, s_qaug(bi, g)))
            return steps


        def emit_passB(pr, bi):
            blk = 2 * pr + bi
            i = blk
            nkb = 2 * i + 2
            qa = qaugA if bi == 0 else qaugB
            ev_ptw = [[None, None], [None, None]]
            ev_avdone = [None] * 4
            ctx_free = [ev_pb[2], ev_pb[3], ev_pb[4], ev_pb[5]]
            pending_av = None

            def emit_av(pav):
                kbp, hwp, eep2 = pav
                evav = None
                for g in (2 * hwp, 2 * hwp + 1):
                    evav = mm(pb[2 + g][0:65, 0:512],
                              v_o_aug[:, kbp * 65:(kbp + 1) * 65],
                              ptw[hwp][:, (g % 2) * 512:(g % 2) * 512 + 512],
                              start=(kbp == 0), stop=(kbp == nkb - 1),
                              waits=[eep2[g % 2], ev_voa[-1], ev_m4] +
                                    ([ctx_free[g]] if kbp == 0 else []))
                    ev_avdone[g] = evav
                    ev_ptw[hwp][g % 2] = evav

            for kb in range(nkb):
                slot = kb - (nkb - 3)
                for hw_ in range(2):
                    evl_by = [None, None]
                    for g in (2 * hw_, 2 * hw_ + 1):
                        lb = g % 2
                        evl = mm(pb[lb][:, 0:512], kaug[:, kb * 128:(kb + 1) * 128],
                                 qa[0:65, g * 512:(g + 1) * 512],
                                 waits=[st["qaug_done_%d_%d" % (blk, g)],
                                        st["ebank"][lb], ev_pb[lb]])
                        ev_pb[lb] = evl
                        evl_by[lb] = evl
                    bg_pop(1)
                    ee2 = [None, None]
                    if slot >= 0:
                        dview = deltaT.rearrange("p (s q h) -> p s q h", q=128, h=16)
                        for lb in range(2):
                            dsl2 = dview[:, slot, :, 8 * hw_ + 4 * lb:8 * hw_ + 4 * lb + 4
                                         ].transpose([0, 2, 1])
                            evdd = dve(lambda e, o=scrF2[:, lb * 512:(lb + 1) * 512],
                                       i_=pall[:, lb * 512:(lb + 1) * 512], d_=dsl2:
                                       e.tensor_tensor(out=o, in0=i_, in1=d_, op=ALU.add),
                                       waits=[evl_by[lb], ev_dT_done,
                                              st["scrF_free_%d" % lb]] +
                                             ([st["rlbF_free"]] if lb == 1 else []))
                            st["ebank"][lb] = evdd
                            ee2[lb] = act(ptw[hw_][:, lb * 512:(lb + 1) * 512],
                                          scrF2[:, lb * 512:(lb + 1) * 512], AF.Exp,
                                          waits=[evdd, ev_ptw[hw_][lb]])
                            st["scrF_free_%d" % lb] = ee2[lb]
                    else:
                        for lb in range(2):
                            ee2[lb] = act(ptw[hw_][:, lb * 512:(lb + 1) * 512],
                                          pall[:, lb * 512:(lb + 1) * 512], AF.Exp,
                                          waits=[evl_by[lb], ev_ptw[hw_][lb]])
                            st["ebank"][lb] = ee2[lb]
                    if pending_av is not None:
                        emit_av(pending_av)
                    pending_av = (kb, hw_, ee2)
                    bg_pop(1)
                bg_pop(1)
            if pending_av is not None:
                emit_av(pending_av)
            st["qaug_free"][bi] = evl
            st["passB_done_%d" % blk] = evl

            ev_ctx_read = []
            evrl_g = [None] * 4
            for g in range(3):
                evrl_g[g] = act(rl_t[g], pall[64:65, 1024 + g * 512:1536 + g * 512],
                                AF.Ln, waits=[ev_avdone[g], st["rlg_free"][g]])
            bg_pop(2)
            evbb_g0 = None
            for g in range(4):
                if g == 3:
                    evrl_g[3] = act(rl_t[3], pall[64:65, 2560:3072], AF.Ln,
                                    waits=[ev_avdone[3], evbb_g0])
                evbb3 = mm(pb[6][0:64, 0:512], ones1_f[0:1, 0:64], rl_t[g],
                           waits=[evrl_g[g], ev_pb[6]])
                if g == 0:
                    evbb_g0 = evbb3
                st["rlg_free"][0 if g == 3 else g] = evbb3
                evcpb = act(rlbF[0:64, 0:512], pb[6][0:64, 0:512], AF.Exp, scale=-1.0,
                            waits=[evbb3, st["rlbF_free"], st["scrF_free_1"]] +
                                  ([ev_ctx_read[-1]] if ev_ctx_read else []))
                ev_pb[6] = evcpb
                for par in range(2):
                    ci = pb[2 + g].rearrange("p (u q) -> p u q", q=128)[0:64, par:4:2, :]
                    ri = rlbF.rearrange("p (u q) -> p u q", q=128)[0:64, par:4:2, :]
                    dst = ctxn3.rearrange("p (k q) -> p k q", q=128)[
                        par * 64:(par + 1) * 64, 2 * g:2 * g + 2, :]
                    evnrm = dve(lambda e, o=dst, i_=ci, r_=ri:
                                e.tensor_tensor(out=o, in0=i_, in1=r_, op=ALU.mult),
                                waits=[evcpb, st["ctxn_free"]])
                    ev_pb[2 + g] = evnrm
                    ev_ctx_read.append(evnrm)
                bg_pop(1)
            st["rlbF_free"] = ev_ctx_read[-1]
            ev_out_half = []
            evwm = None
            for ch in range(2):
                bk = ch
                for k in range(8):
                    evwm = mm(pb[bk][:, 0:512],
                              ctxn3.rearrange("p (k tt) -> p k tt", tt=128)[:, k, :],
                              woutv[:, k * 1024 + ch * 512:k * 1024 + (ch + 1) * 512],
                              start=(k == 0), stop=(k == 7),
                              waits=[ev_ctx_read[2 * (k // 2) + 1], ev_wo,
                                     (ev_pb[bk] if k == 0 else None)])
                hv = partial[:, i * D + ch * 512:i * D + (ch + 1) * 512]
                ev1 = dve(lambda e, o=hv, pp=pb[bk][:, 0:512]: e.tensor_tensor(
                    out=o, in0=o, in1=pp, op=ALU.add),
                    waits=[evwm, st["p3stt"]])
                ev_pb[bk] = ev1
                st["p3stt"] = ev1
                ev_out_half.append(ev1)
                bg_pop(1)
            st["ctxn_free"] = evwm
            evy = dma_out(y[i * 128:(i + 1) * 128, :], partial[:, i * D:(i + 1) * D],
                          waits=ev_out_half)
            ev_y.append(evy)

        bg.extend(mk_front(0))
        for pr in range(8):
            if pr + 1 < 8:
                bg.extend(mk_front(pr + 1))
            for bi in range(2):
                key = "qaug_done_%d_3" % (2 * pr + bi)
                while st.get(key) is None and bg:
                    gate, fn = bg.popleft()
                    fn()
                emit_passB(pr, bi)
        while bg:
            gate, fn = bg.popleft()
            fn()
        S("sync", lambda e: e.nop(), waits=[("OUT", P.cnt["OUT"])])
        P.flush()

    if debug and BUILD_PHASES < 4:
        wD = ([ev_kaug[-1], ev_voa[-1], ev_k2[-1] if ev_k2 else None,
               ev_v2a[-1], ev_q2[-1] if ev_q2 else None]
              + ev_rw_ownd[-1:] + ev_xnownd[-1:])
        d = []
        d.append(dma_out(dbg["d_kaug"], kaug[:], waits=wD))
        d.append(dma_out(dbg["d_voaug"], v_o_aug[:], waits=wD))
        d.append(dma_out(dbg["d_k2"], k2_fm, waits=wD))
        d.append(dma_out(dbg["d_v2aug"], v2_aug, waits=wD))
        d.append(dma_out(dbg["d_q2"], q2_fm, waits=wD))
        d.append(dma_out(dbg["d_rw"], rw_own[:], waits=wD))
        d.append(dma_out(dbg["d_xn"], partial[:], waits=wD))
        if debug and BUILD_PHASES < 3:
            d.append(dma_out(dbg["d_lgraw"], d_stash[:].rearrange("p (t c) -> p t c", c=36)[:, :, 0:32], waits=wD))
            d.append(dma_out(dbg["d_rn"], d_stash[:].rearrange("p (t c) -> p t c", c=36)[:, :, 32:36], waits=wD))
        if BUILD_PHASES < 2:
            d.append(dma_out(dbg["d_n1"], AB2[:, 10272:14368], waits=wD))
            d.append(dma_out(dbg["d_r2bc"], AB2[:, 14368:14880], waits=wD))
        if debug >= 2 and BUILD_PHASES >= 2:
            w2_ = [ev_hs[-1], ev_n2T[-1], ev_rw2[-1]]
            if BUILD_PHASES < 3:
                d.append(dma_out(dbg["d_hs"], partial[:], waits=w2_))
            d.append(dma_out(dbg["d_n2T"], n2T, waits=w2_))
            d.append(dma_out(dbg["d_rw2"], rw2_own[:], waits=w2_))
        if debug >= 3 and BUILD_PHASES >= 3:
            d.append(dma_out(dbg["d_partial"], partial[:], waits=[ev_partial_done[-1]]))
        d.append(dma_out(y[0:128, :], partial[:, 0:1024], waits=wD))
        S("sync", lambda e: e.nop(), waits=[d[-1]])
        P.flush()
        ctx.close()
        return nc, dbg

    ctx.close()
    return nc, dbg


_NC_CACHE = {}


def _prep_inputs(inputs, p):
    cst = _host_constants(p)
    base = {
        "Wq": inputs["L0_attn_Wq"], "Wkv": inputs["L0_attn_Wkv"],
        "Wout": inputs["L0_attn_Wout"], "bout": inputs["L0_attn_bout"],
        "rel_emb": inputs["L0_attn_rel_emb"],
        "ln1_w": inputs["L0_ffn_ln1_w"], "ln1_b": inputs["L0_ffn_ln1_b"],
        "ln2_w": inputs["L0_ffn_ln2_w"], "ln2_b": inputs["L0_ffn_ln2_b"],
        "smha_sim": inputs["L0_ffn_smha_sim"], "smha_gates": inputs["L0_ffn_smha_gates"],
        "q_proj": inputs["L0_ffn_q_proj"], "k_proj": inputs["L0_ffn_k_proj"],
        "v_proj": inputs["L0_ffn_v_proj"], "o_proj": inputs["L0_ffn_o_proj"],
        "moe_sim": inputs["L0_ffn_moe_sim"], "moe_gates": inputs["L0_ffn_moe_gates"],
        "w1": inputs["L0_ffn_w1"], "w2": inputs["L0_ffn_w2"],
        "ohm_T": cst["ohm_T"],
        "ca": cst["ca"], "ct": cst["ct"],
        "ident_f": cst["ident_f"], "ident_b": cst["ident_b"], "e_one": cst["e_one"],
    }
    return base


def make_in_maps(inputs, debug=0):
    x = np.ascontiguousarray(np.asarray(inputs["L0_attn_x"], dtype=np.float32))
    in_maps = []
    for c in range(8):
        b, p = c // 2, c % 2
        m = _prep_inputs(inputs, p)
        xb = x[b]
        m["x_full"] = np.ascontiguousarray(xb)
        m["x_own"] = np.ascontiguousarray(xb.reshape(NB, 128, D)[p::2].reshape(TOWN, D))
        in_maps.append(m)
    return in_maps


def kernel(**inputs):
    nc, _ = build_program(debug=0)
    in_maps = make_in_maps(inputs)
    res = run_bass_kernel_spmd(nc, in_maps, core_ids=list(range(8)))
    x = np.asarray(inputs["L0_attn_x"], dtype=np.float32)
    out = np.empty((4, N, D), dtype=np.float32)
    for c in range(8):
        b, p = c // 2, c % 2
        yc = np.asarray(res.results[c]["y"])
        out[b].reshape(NB, 128, D)[p::2] = yc.reshape(NOWN, 128, D)
    return out


# version 24
# speedup vs baseline: 1.0417x; 1.0002x over previous
import numpy as np
import ml_dtypes
import concourse.bass as bass
import concourse.mybir as mybir
from concourse.bass_utils import run_bass_kernel_spmd
from contextlib import ExitStack

F32 = mybir.dt.float32
BF16 = mybir.dt.bfloat16
ALU = mybir.AluOpType
AF = mybir.ActivationFunctionType
AX = mybir.AxisListType

D = 1024
N = 4096
NB = 32
NOWN = 16
TOWN = NOWN * 128
NH = 16
DH = 64
H2 = 128
NE = 8
EPS = 1e-5
NEG = -1e30
Q_SCALE = 0.125
Q2_SCALE = float(np.float32(1.0) / np.sqrt(np.float32(128.0)))

BUILD_PHASES = 4


class Prog:
    ENGS = ("sync", "gpsimd", "scalar", "vector", "tensor")

    def __init__(self, nc, ctx):
        self.nc = nc
        self.sems = {}
        self.cnt = {}
        self.last_wait = {}
        for name in ("LD", "LW", "OUT", "LDA", "LDB", "LWA", "LWB", "PE", "DV", "AC", "PL"):
            self.sems[name] = ctx.enter_context(nc.semaphore(name))
            self.cnt[name] = 0
        self.sched = []

    DMA_SEMS = ("LD", "LW", "OUT", "LDA", "LDB", "LWA", "LWB")

    def step(self, eng, fn, waits=(), inc=None, n=1):
        ev = None
        waits = [(w[0], self.cnt[w[0]]) if (w is not None and w[0] in self.DMA_SEMS)
                 else w for w in waits if w is not None]
        if inc is not None:
            self.cnt[inc] += n
            ev = (inc, self.cnt[inc], n)
        self.sched.append((eng, fn, tuple(waits), ev))
        return (ev[0], ev[1]) if ev else None

    def flush(self):
        sched = self.sched
        self.sched = []
        if not sched:
            return
        with self.nc.Block() as block:
            for engname in self.ENGS:
                if not any(s[0] == engname for s in sched):
                    continue

                def make(en):
                    def body(eng):
                        for (e2, fn, waits, ev) in sched:
                            if e2 != en:
                                continue
                            for (s, v) in waits:
                                key = (en, s)
                                if self.last_wait.get(key, -1) < v:
                                    eng.wait_ge(self.sems[s], v)
                                    self.last_wait[key] = v
                            ins = fn(eng)
                            if ev is not None:
                                ins.then_inc(self.sems[ev[0]], ev[2])
                    return body

                getattr(block, engname)(make(engname))


def _bucket(n):
    n = np.asarray(n)
    small = n < 16
    nf = np.maximum(n, 1).astype(np.float32)
    val = 16 + (np.log(nf / np.float32(16.0)) / np.float32(np.log(8.0))
                * np.float32(16.0)).astype(np.int32)
    val = np.minimum(val, 31)
    return np.where(small, n, val)


def _host_constants(p):
    q_idx = np.arange(128)[:, None]
    s_idx = np.arange(128)[None, :]
    ohm_T = np.zeros((3, 33, 256), dtype=np.float32)
    for slot in range(3):
        dblk = 1 + p - slot
        n = 128 * dblk - 127 + np.arange(256)
        masked = n < 0
        bkt = _bucket(np.maximum(n, 0))
        for b in range(31):
            sel = (bkt == b) & (~masked)
            ohm_T[slot, b][sel] = 1.0
            ohm_T[slot, 31][sel] -= 1.0
        ohm_T[slot, 32][masked] = 1.0
    ca = np.zeros((2, 128, 128), dtype=np.float32)
    caus = np.where(s_idx > q_idx, np.float32(NEG), np.float32(0.0)).astype(np.float32)
    if p == 0:
        ca[0] = caus
        ca[1] = NEG
    else:
        ca[0] = 0.0
        ca[1] = caus
    ct = np.ascontiguousarray(ca.transpose(0, 2, 1))
    ident = np.eye(128, dtype=np.float32)
    e_one = np.zeros((8, 8 * 128), dtype=np.float32)
    for e in range(8):
        e_one[e, e * 128:(e + 1) * 128] = 1.0
    bf = ml_dtypes.bfloat16
    return {
        "ohm_T": ohm_T.astype(bf),
        "ca": ca, "ct": ct,
        "ident_f": ident, "ident_b": ident.astype(bf),
        "e_one": e_one.astype(bf),
    }


def build_program(debug=0):
    nc = bass.Bass("TRN2", target_bir_lowering=False)
    ctx = ExitStack()

    dT = lambda name, shape, dt=F32: nc.dram_tensor(name, shape, dt, kind="ExternalInput").ap()
    x_full = dT("x_full", [N, D])
    x_own = dT("x_own", [TOWN, D])
    Wq_d = dT("Wq", [D, D])
    Wkv_d = dT("Wkv", [D, 2 * DH])
    Wout_d = dT("Wout", [D, D])
    bout_d = dT("bout", [D])
    rel_d = dT("rel_emb", [32, 16])
    ln1w_d = dT("ln1_w", [D]); ln1b_d = dT("ln1_b", [D])
    ln2w_d = dT("ln2_w", [D]); ln2b_d = dT("ln2_b", [D])
    sim1_d = dT("smha_sim", [D, NE]); g1_d = dT("smha_gates", [NE])
    qp_d = dT("q_proj", [NE, D, H2]); kp_d = dT("k_proj", [NE, D, H2])
    vp_d = dT("v_proj", [NE, D, H2]); op_d = dT("o_proj", [NE, H2, D])
    sim2_d = dT("moe_sim", [D, NE]); g2_d = dT("moe_gates", [NE])
    w1_d = dT("w1", [NE, D, D]); w2_d = dT("w2", [NE, D, D])
    ohmT_d = dT("ohm_T", [3, 33, 256], BF16)
    ca_d = dT("ca", [2, 128, 128]); ct_d = dT("ct", [2, 128, 128])
    identf_d = dT("ident_f", [128, 128])
    identb_d = dT("ident_b", [128, 128], BF16)
    eone_d = dT("e_one", [8, 8 * 128], BF16)

    y = nc.dram_tensor("y", [TOWN, D], F32, kind="ExternalOutput").ap()
    dbg = {}
    if debug:
        def dout(name, shape, dt=F32):
            dbg[name] = nc.dram_tensor(name, shape, dt, kind="ExternalOutput").ap()
        dout("d_kaug", [65, N], BF16)
        dout("d_voaug", [128, 32 * 65], BF16)
        dout("d_k2", [128, N], BF16)
        dout("d_v2aug", [128, 32 * 129], BF16)
        dout("d_q2", [128, TOWN], BF16)
        dout("d_rw", [128, NOWN * 8])
        dout("d_xn", [128, NOWN * D])
        dout("d_lgraw", [128, 12 * 32])
        dout("d_n1", [128, 4096], BF16)
        dout("d_r2bc", [128, 512], BF16)
        dout("d_rn", [128, 12 * 4])
        dout("d_stash2", [128, 32])
        if debug >= 2:
            dout("d_hs", [128, NOWN * D])
            dout("d_n2T", [128, NOWN * D], BF16)
            dout("d_rw2", [128, NOWN * 8])
        if debug >= 3:
            dout("d_partial", [128, NOWN * D])

    E = ctx.enter_context

    kaug = E(nc.sbuf_tensor("s_kaug", [65, N], BF16))
    v_o_aug = E(nc.sbuf_tensor("v_o_aug", [128, 32 * 65], BF16))
    partial = E(nc.sbuf_tensor("partial", [128, NOWN * D], F32))
    rw_own = E(nc.sbuf_tensor("rw_own", [128, NOWN * 8], F32))
    rw2_own = E(nc.sbuf_tensor("rw2_own", [128, NOWN * 8], F32))
    ca_t = E(nc.sbuf_tensor("ca_t", [128, 2 * 128], F32))
    ct_t = E(nc.sbuf_tensor("ct_t", [128, 2 * 128], F32))
    ident_f = E(nc.sbuf_tensor("s_ident_f", [128, 128], F32))
    ident_b = E(nc.sbuf_tensor("s_ident_b", [128, 128], BF16))
    e_one = E(nc.sbuf_tensor("s_e_one", [8, 8 * 128], BF16))
    onesc_b = E(nc.sbuf_tensor("onesc_b", [128, 1], BF16))
    ones1_f = E(nc.sbuf_tensor("ones1_f", [1, 128], F32))
    sig1 = E(nc.sbuf_tensor("sig1", [128, 8], F32))
    sig2 = E(nc.sbuf_tensor("sig2", [128, 8], F32))
    rcol1 = E(nc.sbuf_tensor("rcol1", [128, 8], F32))
    rcol2 = E(nc.sbuf_tensor("rcol2", [128, 8], F32))
    sim1 = E(nc.sbuf_tensor("sim1", [128, 8 * 8], BF16))
    sim2 = E(nc.sbuf_tensor("sim2", [128, 8 * 8], BF16))
    lnc = E(nc.sbuf_tensor("lnc", [128, 32], F32))
    ln1w_c = lnc[:, 0:8]; ln1b_c = lnc[:, 8:16]
    ln2w_c = lnc[:, 16:24]; ln2b_c = lnc[:, 24:32]
    bout_bc = E(nc.sbuf_tensor("bout_bc", [128, D], BF16))
    emb_ext = E(nc.sbuf_tensor("emb_ext", [33, 16], BF16))
    rel_sb = E(nc.sbuf_tensor("rel_sb", [32, 16], F32))
    row512 = E(nc.sbuf_tensor("row512", [1, 512], F32))
    row512b = E(nc.sbuf_tensor("row512b", [1, 512], F32))
    lgT = E(nc.sbuf_tensor("lgT", [8, 512], F32))
    rwT = E(nc.sbuf_tensor("rwT", [8, 512], BF16))
    lg_tm = E(nc.sbuf_tensor("lg_tm", [128, 32], F32))
    rn_cols = E(nc.sbuf_tensor("rn_cols", [128, 4], F32))
    rwtm = rw2_own[:, 0:32]
    smalls = E(nc.sbuf_tensor("smalls", [128, 208], F32))
    m_all = E(nc.sbuf_tensor("m_all", [128, 16], F32))
    qaug = E(nc.sbuf_tensor("qaug", [65, 128], BF16))
    vfm = E(nc.sbuf_tensor("vfm", [64, 512], BF16))

    d_stash = E(nc.sbuf_tensor("d_stash", [128, 432], F32)) if (debug and BUILD_PHASES < 3) else None
    AB = E(nc.sbuf_tensor("AB", [128, 24576], BF16))
    AB2 = E(nc.sbuf_tensor("AB2", [128, 26656], BF16))
    AFa = E(nc.sbuf_tensor("AFa", [128, 3072], F32))

    k2_fm = AB2[:, 0:4096]
    v2_aug = AB2[:, 4096:4096 + 32 * 129]
    q2_fm = AB2[:, 8224:8224 + 2048]
    n2T = AB2[:, 10272:10272 + 16384]
    deltaT = AB2[:, 0:6144]
    deltaA = AB2[:, 6144:12288]
    woutv = AB2[:, 12288:12288 + 8192]
    p3scr = AB2[:, 20480:26656]
    wkv = AB[:, 23040:24064]

    pall = E(nc.psum_tensor("pall", [128, 3584], F32))
    pb = [pall[:, i * 512:(i + 1) * 512] for i in range(7)]
    pbt = E(nc.psum_tensor("pbt", [128, 1024], BF16))

    P = Prog(nc, ctx)
    S = P.step

    def mm(out, lhsT, rhs, start=True, stop=True, waits=(), inc="PE"):
        return S("tensor", lambda e, o=out, l=lhsT, r=rhs, a=start, b=stop:
                 e.matmul(out=o, lhsT=l, rhs=r, start=a, stop=b), waits, inc)

    def tr(out, in_, ident, waits=(), inc="PE"):
        return S("tensor", lambda e, o=out, i=in_, d=ident:
                 e.transpose(out=o, in_=i, identity=d), waits, inc)

    def act(out, in_, func=AF.Copy, bias=0.0, scale=1.0, accum=None, waits=(), inc="AC"):
        return S("scalar", lambda e, o=out, i=in_, f=func, b=bias, s=scale, a=accum:
                 e.activation(out=o, in_=i, func=f, bias=b, scale=s, accum_out=a), waits, inc)

    def dve(fn, waits=(), inc="DV"):
        return S("vector", fn, waits, inc)

    def pool(fn, waits=(), inc="PL"):
        return S("gpsimd", fn, waits, inc)

    def dma(out, in_, waits=(), q="sync", sem=None):
        s = sem or ("LW" if q == "gpsimd" else "LD")
        return S(q, lambda e, o=out, i=in_: e.dma_start(out=o, in_=i),
                 list(waits) + [(s, P.cnt[s])], s, 16)

    def dma_out(out, in_, waits=()):
        return S("sync", lambda e, o=out, i=in_: e.dma_start(out=o, in_=i),
                 list(waits) + [("OUT", P.cnt["OUT"])], "OUT", 16)

    ev_pb = {i: None for i in range(7)}
    state = {"pbt": None}

    ev_idf = dma(ident_f[:], identf_d)
    ev_idb = dma(ident_b[:], identb_d)
    ev_eone = dma(e_one[:], eone_d)
    ev_ca = [dma(ca_t[:, a * 128:(a + 1) * 128], ca_d[a]) for a in range(2)][-1]
    ev_ct = [dma(ct_t[:, a * 128:(a + 1) * 128], ct_d[a]) for a in range(2)][-1]
    ev_rel = dma(rel_sb[:], rel_d)
    ev_sim1 = dma(sim1[:].rearrange("p (k e) -> p k e", e=8), sim1_d.rearrange("(k p) e -> p k e", p=128), q="gpsimd")
    ev_sim2 = dma(sim2[:].rearrange("p (k e) -> p k e", e=8), sim2_d.rearrange("(k p) e -> p k e", p=128), q="gpsimd")
    _lnstage = AFa[0:8, 2048:2560]
    _lnd = []
    for i, srcd in enumerate((ln1w_d, ln1b_d, ln2w_d, ln2b_d)):
        _lnd.append(dma(_lnstage.rearrange("k (i c) -> k i c", c=128)[:, i, :],
                        srcd.rearrange("(k p) -> k p", p=128)))
    _lnt = []
    for i in range(4):
        _lnt.append(tr(pb[6][:, 120 + i * 8:120 + (i + 1) * 8],
                       _lnstage.rearrange("k (i c) -> k i c", c=128)[:, i, :],
                       ident_f[0:8, 0:8], waits=[_lnd[i], ev_idf, ev_pb[6]]))
    ev_lnc = act(lnc[:], pb[6][:, 120:152], waits=[_lnt[-1]])
    ev_pb[6] = ev_lnc
    ev_ln1 = [ev_lnc]
    ev_ln2 = [ev_lnc]
    ev_g1 = dma(smalls[0:1, 0:8], g1_d.rearrange("(o e) -> o e", o=1))
    ev_g2 = dma(smalls[0:1, 8:16], g2_d.rearrange("(o e) -> o e", o=1))
    ev_bout = dma(AFa[0:1, 0:1024], bout_d.rearrange("(o d) -> o d", o=1))

    ev_m1 = pool(lambda e: e.memset(onesc_b[:], 1.0))
    ev_m2 = pool(lambda e: e.memset(ones1_f[:], 1.0))
    ev_m3 = pool(lambda e: e.memset(kaug[64:65, :], 1.0))
    ev_m4 = pool(lambda e: e.memset(
        v_o_aug[:].rearrange("p (b c) -> p b c", c=65)[:, :, 64:65], 1.0))
    ev_m5 = pool(lambda e: e.memset(
        v2_aug.rearrange("p (b c) -> p b c", c=129)[:, :, 128:129], 1.0))

    def bcast_row(dst, src_row, n, evs):
        evm = mm(pb[6][:, 0:n], ones1_f[0:1, 0:128], src_row,
                 waits=list(evs) + [ev_m2, ev_pb[6]])
        ev = act(dst, pb[6][:, 0:n], waits=[evm])
        ev_pb[6] = ev
        return ev

    ev_s1 = act(smalls[0:1, 16:24], smalls[0:1, 0:8], AF.Sigmoid, waits=[ev_g1])
    ev_s2 = act(smalls[0:1, 24:32], smalls[0:1, 8:16], AF.Sigmoid, waits=[ev_s1, ev_g2])
    ev_sig1 = bcast_row(sig1[:], smalls[0:1, 16:24], 8, [ev_s1])
    ev_sig2 = bcast_row(sig2[:], smalls[0:1, 24:32], 8, [ev_s2])
    ev_bo1 = bcast_row(bout_bc[:, 0:512], AFa[0:1, 0:512], 512, [ev_bout])
    ev_bo2 = bcast_row(bout_bc[:, 512:1024], AFa[0:1, 512:1024], 512, [ev_bout])

    def sim_colnorm(dst, sim_sb, ev_sim, sq_off):
        sq = AB[:, sq_off:sq_off + 64]
        e1 = act(sq, sim_sb[:], AF.Square, waits=[ev_sim])
        e2 = mm(pb[5][0:1, 0:64], onesc_b[:], sq, waits=[e1, ev_m1, ev_pb[5]])
        row = row512[0:1, 0:64]
        e3 = act(row, pb[5][0:1, 0:64], waits=[e2])
        ev_pb[5] = e3
        rowv = row.rearrange("o (k e) -> o e k", k=8)
        ssr = row512b[0:1, 0:8]
        e4 = dve(lambda e, o=ssr, i=rowv: e.tensor_reduce(out=o, in_=i, axis=AX.X, op=ALU.add), waits=[e3])
        e5 = dve(lambda e, o=ssr: e.tensor_scalar_max(out=o, in0=o, scalar1=1e-24), waits=[e4])
        e6 = act(ssr, ssr, AF.Ln, waits=[e5])
        e7 = act(ssr, ssr, AF.Exp, scale=-0.5, waits=[e6])
        return bcast_row(dst, ssr, 8, [e7])

    ev_rc1 = sim_colnorm(rcol1[:], sim1, ev_sim1, 0)
    ev_rc2 = sim_colnorm(rcol2[:], sim2, ev_sim2, 64)

    ev_emb = act(emb_ext[0:32, :], rel_sb[:], AF.Copy, scale=8.0, waits=[ev_rel])
    ev_embm = pool(lambda e: e.memset(emb_ext[32:33, :], NEG))

    P.flush()

    kp = AB[:, 0:8192]
    vp = AB[:, 8192:16384]
    xn1T = AB[:, 16384:20480]
    xnb = AB[:, 20480:21504]
    sqs = AB[:, 21504:22016]
    rwb = AB[:, 22016:22528]
    r2bc = AB[:, 22528:23040]

    ev_kp = dma(kp.rearrange("p (e k h) -> p e k h", k=8, h=128), kp_d.rearrange("e (k p) h -> p e k h", p=128), q="gpsimd")
    ev_vp = dma(vp.rearrange("p (e k h) -> p e k h", k=8, h=128), vp_d.rearrange("e (k p) h -> p e k h", p=128), q="gpsimd")
    ev_wkv = dma(wkv.rearrange("p (k d) -> p k d", d=128), Wkv_d.rearrange("(k p) d -> p k d", p=128), q="gpsimd")
    ev_qp = None

    xin = [AFa[:, 0:1024], AFa[:, 1024:2048]]
    accA = AFa[:, 2048:2560]
    accB = AFa[:, 2560:3072]

    def ln_stats(xblk_f, cols, waits):
        c = cols
        bn6 = smalls[:, c:c + 12]
        mv = smalls[:, c + 12:c + 14]
        rstd = smalls[:, c + 14:c + 15]
        nmr = smalls[:, c + 15:c + 16]
        rstd2 = smalls[:, c + 16:c + 17]
        tmp = smalls[:, c + 17:c + 18]
        ru = smalls[:, c + 18:c + 19]
        e0 = dve(lambda e, o=bn6, i=xblk_f: e.bn_stats(out=o[:, 0:6], in_=i[:, 0:512]), waits=waits)
        e1 = dve(lambda e, o=bn6, i=xblk_f: e.bn_stats(out=o[:, 6:12], in_=i[:, 512:1024]), waits=waits + [e0])
        e2 = dve(lambda e, o=mv, i=bn6: e.bn_aggr(out=o, in_=i), waits=[e1])
        e3 = dve(lambda e, o=tmp, i=mv: e.tensor_scalar_add(out=o, in0=i[:, 1:2], scalar1=EPS), waits=[e2])
        e4 = act(rstd, tmp, AF.Ln, waits=[e3])
        e5 = act(rstd, rstd, AF.Exp, scale=-0.5, waits=[e4])
        e6 = dve(lambda e, o=nmr, i=mv: e.tensor_scalar_mul(
            out=o, in0=i[:, 0:1], scalar1=-1.0), waits=[e5, e2])
        e7 = act(ru, rstd, AF.Square, waits=[e5])
        e8 = dve(lambda e, o=tmp, i=mv, r=ru: e.tensor_tensor(out=o, in0=i[:, 1:2], in1=r, op=ALU.mult), waits=[e7, e6])
        e9 = dve(lambda e, o=tmp: e.tensor_scalar_add(out=o, in0=o, scalar1=EPS), waits=[e8])
        e10 = act(rstd2, tmp, AF.Ln, waits=[e9])
        e11 = act(rstd2, rstd2, AF.Exp, scale=-0.5, waits=[e10])
        return e6, rstd, nmr, rstd2, e11

    def gate_block(ev_ready, g, rncol, ev_rn, sigb, rcolb, rw_dst, cols, extra=()):
        c = cols
        t1 = smalls[:, c + 20:c + 28]
        t2 = smalls[:, c + 28:c + 36]
        m8 = smalls[:, c + 36:c + 44]
        af = smalls[:, c + 44:c + 45]
        den = smalls[:, c + 45:c + 46]
        e1 = dve(lambda e, o=g, r=rncol: e.tensor_scalar_mul(out=o, in0=o, scalar1=r),
                 waits=[ev_ready, ev_rn] + list(extra))
        e2 = dve(lambda e, o=g, r=rcolb: e.tensor_tensor(out=o, in0=o, in1=r, op=ALU.mult), waits=[e1])
        e3 = dve(lambda e, o=g, s=sigb: e.tensor_tensor(out=o, in0=o, in1=s, op=ALU.subtract), waits=[e2])
        e4 = dve(lambda e, o=m8, i=g: e.max(out=o, in_=i), waits=[e3])
        e5 = dve(lambda e, o=t1, i=g, m=m8: e.tensor_scalar(
            out=o, in0=i, scalar1=m[:, 0:1], scalar2=None, op0=ALU.is_equal), waits=[e4])
        e6 = dve(lambda e, o=t2, i=g, m=m8: e.tensor_scalar(
            out=o, in0=i, scalar1=m[:, 1:2], scalar2=None, op0=ALU.is_equal), waits=[e5])
        e7 = dve(lambda e, o=t1, i=t2: e.tensor_tensor(out=o, in0=o, in1=i, op=ALU.add), waits=[e6])
        e8 = dve(lambda e, o=t2, i=g: e.tensor_single_scalar(out=o, in_=i, scalar=0.0, op=ALU.is_gt), waits=[e7])
        e9 = dve(lambda e, o=af, m=m8: e.tensor_single_scalar(out=o, in_=m[:, 0:1], scalar=0.0, op=ALU.is_gt), waits=[e8])
        e10 = dve(lambda e, o=t2, i=t1: e.tensor_tensor(out=o, in0=o, in1=i, op=ALU.subtract), waits=[e9])
        e11 = dve(lambda e, o=t1, i=t2, a=af: e.scalar_tensor_tensor(
            out=o, in0=i, scalar=a, in1=o, op0=ALU.mult, op1=ALU.add), waits=[e10])
        e12 = dve(lambda e, o=t2, i=t1: e.tensor_single_scalar(out=o, in_=i, scalar=0.0, op=ALU.is_gt), waits=[e11])
        e13 = dve(lambda e, o=g: e.tensor_scalar_max(out=o, in0=o, scalar1=0.0), waits=[e12])
        e14 = dve(lambda e, o=g, u=t2: e.tensor_tensor(out=o, in0=o, in1=u, op=ALU.mult), waits=[e13])
        e15 = dve(lambda e, o=t1, u=t2: e.tensor_scalar(
            out=o, in0=u, scalar1=-NEG, scalar2=NEG, op0=ALU.mult, op1=ALU.add), waits=[e14])
        e16 = dve(lambda e, o=g, i=t1: e.tensor_tensor(out=o, in0=o, in1=i, op=ALU.add), waits=[e15])
        e17 = dve(lambda e, o=t1, i=g: e.tensor_reduce(out=o[:, 0:1], in_=i, axis=AX.X, op=ALU.max, negate=True), waits=[e16])
        e18 = act(rw_dst, g, AF.Exp, bias=t1[:, 0:1], accum=den, waits=[e17])
        e19 = dve(lambda e, o=den: e.reciprocal(out=o, in_=o), waits=[e18])
        e20 = dve(lambda e, o=rw_dst, r=den: e.tensor_scalar_mul(out=o, in0=o, scalar1=r), waits=[e19])
        return e20

    tiles = [("full", t) for t in range(8)] + [("own", t) for t in range(4)]
    pro_smalls = ev_sig2
    pro_row512 = ev_rc2
    xin_free = [None, None]
    st = {
        "rwb_free": ev_lnc, "r2bc_free": None, "accA_free": ev_lnc, "accB_free": None,
        "cap": None,
        "sqs_free": None, "xn1T_free": None, "vfm_free": None,
    }
    ev_k2 = []; ev_v2 = []; ev_q2 = []; ev_kaug = [ev_m3]; ev_voa = [ev_m4]
    ev_v2a = [ev_m5]; ev_rw_ownd = []; ev_xnownd = []
    last_full_proj = []

    for tix, (kind, t) in enumerate(tiles):
        buf = tix % 2
        src = x_full if kind == "full" else x_own
        base_tok = t * 512
        if kind == "own" and t == 0:
            ev_qp = dma(kp.rearrange("p (e k h) -> p e k h", k=8, h=128),
                        qp_d.rearrange("e (k p) h -> p e k h", p=128),
                        waits=last_full_proj, q="gpsimd")
        ev_xnT_blk = []
        ev_r2t = []
        for b in range(4):
            bbuf = (tix * 4 + b) % 2
            xblk = xin[bbuf]
            tok0 = base_tok + b * 128
            ev_x = dma(xblk, src[tok0:tok0 + 128, :], waits=[xin_free[bbuf]])
            cols = 48 * b
            evm, rstd, nmr, rstd2, ev_r2 = ln_stats(
                xblk, cols, [ev_x] + ([pro_smalls] if tix == 0 else []))
            if kind == "own":
                blk = t * 4 + b
                evxn = pool(lambda e, o=partial[:, blk * D:(blk + 1) * D], i=xblk,
                            a=nmr, r=rstd: e.tensor_scalar(
                    out=o, in0=i, scalar1=a, scalar2=r, op0=ALU.add, op1=ALU.mult),
                    waits=[evm])
                ev_xnownd.append(evxn)
                evc = act(xnb, partial[:, blk * D:(blk + 1) * D], AF.Copy,
                          waits=[evxn, (st.get("sqx_free") if b == 0 else None)])
            else:
                evc = pool(lambda e, o=xnb, i=xblk, a=nmr, r=rstd: e.tensor_scalar(
                    out=o, in0=i, scalar1=a, scalar2=r, op0=ALU.add, op1=ALU.mult),
                    waits=[evm, (st.get("sqx_free") if b == 0 else None)])
            evtr = []
            for k in range(8):
                evtr.append(tr(pbt[:, k * 128:(k + 1) * 128], xnb[:, k * 128:(k + 1) * 128],
                               ident_b[:], waits=[evc, state["pbt"], ev_idb]))
            evcp = act(xn1T.rearrange("p (k tt) -> p k tt", tt=512)[:, :, b * 128:(b + 1) * 128],
                       pbt[:].rearrange("p (k tt) -> p k tt", tt=128),
                       waits=[evtr[-1], st["xn1T_free"]])
            state["pbt"] = evcp
            ev_xnT_blk.append(evcp)
            ev_r2t.append(tr(pb[6][0:1, b * 128:(b + 1) * 128], rstd2, ident_f[:],
                             waits=[ev_r2, ev_pb[6], ev_idf]))
            xin_free[bbuf] = evc
        ev_row = act(row512[0:1, 0:512], pb[6][0:1, 0:512],
                     waits=[ev_r2t[-1]] + ([pro_row512] if tix == 0 else []))
        ev_pb[6] = ev_row
        ev_bb = mm(pb[5][:, 0:512], ones1_f[0:1, 0:128], row512[0:1, 0:512],
                   waits=[ev_row, ev_pb[5]])
        ev_r2b = act(r2bc, pb[5][:, 0:512], waits=[ev_bb, st["r2bc_free"], st.get("cap")])
        ev_pb[5] = ev_r2b

        if kind == "full":
            ev_kvm = []
            for k in range(8):
                ev_kvm.append(mm(pb[0][:, 0:512], wkv[:, k * 128:(k + 1) * 128],
                                 xn1T[:, k * 512:(k + 1) * 512],
                                 start=(k == 0), stop=(k == 7),
                                 waits=[ev_wkv, ev_pb[0] if k == 0 else None,
                                        ev_xnT_blk[-1]]))
            ev_k = act(kaug[0:64, base_tok:base_tok + 512], pb[0][0:64, 0:512], waits=[ev_kvm[-1]])
            ev_kaug.append(ev_k)
            ev_v = act(vfm[:], pb[0][64:128, 0:512], waits=[ev_kvm[-1], st["vfm_free"]])
            ev_pb[0] = ev_v
            evvt = []
            for b in range(4):
                evvt.append(tr(pbt[:, b * 64:(b + 1) * 64], vfm[:, b * 128:(b + 1) * 128],
                               ident_b[0:64, 0:64], waits=[ev_v, state["pbt"]]))
            ev_vo = act(v_o_aug[:, (4 * t) * 65:(4 * t + 4) * 65].rearrange(
                "p (b c) -> p b c", c=65)[:, :, 0:64],
                pbt[:, 0:256].rearrange("p (b c) -> p b c", c=64),
                waits=[evvt[-1]])
            state["pbt"] = ev_vo
            st["vfm_free"] = ev_vo
            ev_voa.append(ev_vo)
            n1dep = ev_kvm[-1]
        else:
            n1dep = None

        ev_n1 = []
        eprev = [st["accB_free"], st["accA_free"]]
        for k in range(8):
            ch = xn1T[:, k * 512:(k + 1) * 512]
            eng = pool if k % 2 == 0 else dve
            acc = accB if k % 2 == 0 else accA
            e1 = eng(lambda e, o=acc, i=ch, r=r2bc: e.tensor_tensor(out=o, in0=i, in1=r, op=ALU.mult),
                     waits=[ev_r2b, n1dep, ev_xnT_blk[-1], eprev[k % 2]])
            e2 = eng(lambda e, o=ch, i=acc, kk=k: e.tensor_scalar(
                out=o, in0=i, scalar1=ln1w_c[:, kk:kk + 1], scalar2=ln1b_c[:, kk:kk + 1],
                op0=ALU.mult, op1=ALU.add), waits=[e1] + ev_ln1)
            eprev[k % 2] = e2
            ev_n1.append(e2)
        st["accB_free"] = eprev[0]
        st["accA_free"] = eprev[1]

        if debug and tix == 0 and BUILD_PHASES < 2:
            ev_cap_n1 = act(AB2[:, 10272:14368], xn1T[:], waits=[ev_n1[-1]])
            ev_cap_r2 = act(AB2[:, 14368:14880], r2bc, waits=[ev_r2b, ev_cap_n1])
            st["cap"] = ev_cap_r2
        prev_bufev = [st["sqs_free"], st.get("sqx_free")]
        ev_sq_mm = []
        for k in range(8):
            buf = sqs if k % 2 == 0 else xnb[:, 0:512]
            esq = act(buf, xn1T[:, k * 512:(k + 1) * 512], AF.Square,
                      waits=[ev_n1[k], prev_bufev[k % 2]])
            emm = mm(pb[0][0:1, 0:512], onesc_b[:], buf, start=(k == 0), stop=(k == 7),
                     waits=[esq, ev_pb[0] if k == 0 else None, ev_m1])
            prev_bufev[k % 2] = emm
            ev_sq_mm.append(emm)
        st["sqs_free"] = prev_bufev[0]
        st["sqx_free"] = prev_bufev[1]
        ev_ssr = dve(lambda e, o=row512b[0:1, 0:512], i_=pb[0][0:1, 0:512]:
                     e.tensor_scalar_max(out=o, in0=i_, scalar1=1e-24),
                     waits=[ev_sq_mm[-1]])
        ev_pb[0] = ev_ssr
        ev_ss2 = act(row512b[0:1, 0:512], row512b[0:1, 0:512], AF.Ln, waits=[ev_ssr])
        ev_ss3 = act(row512b[0:1, 0:512], row512b[0:1, 0:512], AF.Exp, scale=-0.5, waits=[ev_ss2])
        ev_rnt = []
        for b in range(4):
            ev_rnt.append(tr(pb[6][:, b:b + 1], row512b[0:1, b * 128:(b + 1) * 128],
                             ident_f[0:1, 0:1], waits=[ev_ss3, ev_pb[6]]))
        ev_rn = act(rn_cols[:], pb[6][:, 0:4], waits=[ev_rnt[-1]])
        ev_pb[6] = ev_rn

        ev_rawm = []
        for k in range(8):
            ev_rawm.append(mm(pb[4][0:8, 0:512], sim1[:, k * 8:(k + 1) * 8],
                              xn1T[:, k * 512:(k + 1) * 512],
                              start=(k == 0), stop=(k == 7),
                              waits=[ev_n1[k], ev_pb[4] if k == 0 else None, ev_sim1]))
        ev_lgT = act(lgT[:], pb[4][0:8, 0:512], waits=[ev_rawm[-1]])
        ev_pb[4] = ev_lgT
        ev_lgt = []
        for b in range(4):
            ev_lgt.append(tr(pb[3][:, b * 8:(b + 1) * 8], lgT[:, b * 128:(b + 1) * 128],
                             ident_f[0:8, 0:8], waits=[ev_lgT, ev_pb[3]]))
        ev_lgtm = act(lg_tm[:], pb[3][:, 0:32], waits=[ev_lgt[-1]])
        ev_pb[3] = ev_lgtm
        if debug and BUILD_PHASES < 3:
            ev_dbg1 = act(d_stash[:, tix * 36:tix * 36 + 32], lg_tm[:], waits=[ev_lgtm])
            ev_dbg2 = act(d_stash[:, tix * 36 + 32:tix * 36 + 36], rn_cols[:], waits=[ev_rn, ev_dbg1])
            ev_lgtm = ev_dbg1

        ev_rw = []
        for b in range(4):
            cols = 48 * b
            g = lg_tm[:, b * 8:(b + 1) * 8]
            if kind == "own":
                blk = t * 4 + b
                rw_dst = rw_own[:, blk * 8:(blk + 1) * 8]
            else:
                rw_dst = rwtm[:, b * 8:(b + 1) * 8]
            ev = gate_block(ev_lgtm, g, rn_cols[:, b:b + 1], ev_rn,
                            sig1[:], rcol1[:], rw_dst, cols, extra=[ev_sig1, ev_rc1])
            ev_rw.append(ev)
            if kind == "own":
                ev_rw_ownd.append(ev)
        ev_rwt = []
        for b in range(4):
            srcrw = (rw_own[:, (t * 4 + b) * 8:(t * 4 + b + 1) * 8] if kind == "own"
                     else rwtm[:, b * 8:(b + 1) * 8])
            ev_rwt.append(tr(pb[3][0:8, b * 128:(b + 1) * 128], srcrw, ident_f[:],
                             waits=[ev_rw[b], ev_pb[3]]))
        ev_rwT = act(rwT[:], pb[3][0:8, 0:512], waits=[ev_rwt[-1]])
        ev_pb[3] = ev_rwT

        def proj_pass(wsb, ev_w, acc, acc_free):
            ev_acc = None
            ev_tmp_free = st["accB_free"]
            lastmm = None
            for e in range(NE):
                bank_i = 1 + (e % 2)
                bank = pb[bank_i]
                evb1 = mm(pb[5][:, 0:512], e_one[0:8, e * 128:(e + 1) * 128], rwT[:],
                          waits=[ev_rwT, ev_pb[5], ev_eone])
                evb2 = act(rwb, pb[5][:, 0:512], waits=[evb1, st["rwb_free"]])
                ev_pb[5] = evb2
                evm = None
                for k in range(8):
                    evm = mm(bank[:, 0:512], wsb[:, (e * 8 + k) * 128:(e * 8 + k + 1) * 128],
                             xn1T[:, k * 512:(k + 1) * 512],
                             start=(k == 0), stop=(k == 7),
                             waits=[ev_n1[k], ev_w,
                                    ev_pb[bank_i] if k == 0 else None])
                lastmm = evm
                if e == 0:
                    ev_acc = dve(lambda ee, o=acc, pp=bank, r=rwb: ee.tensor_tensor(
                        out=o, in0=pp[:, 0:512], in1=r, op=ALU.mult),
                        waits=[evm, evb2, acc_free])
                    ev_pb[bank_i] = ev_acc
                    st["rwb_free"] = ev_acc
                else:
                    evt_ = dve(lambda ee, pp=bank, r=rwb: ee.tensor_tensor(
                        out=accB, in0=pp[:, 0:512], in1=r, op=ALU.mult),
                        waits=[evm, evb2, ev_tmp_free])
                    ev_pb[bank_i] = evt_
                    st["rwb_free"] = evt_
                    ev_acc = dve(lambda ee, o=acc: ee.tensor_tensor(
                        out=o, in0=o, in1=accB, op=ALU.add), waits=[evt_, ev_acc])
                    ev_tmp_free = ev_acc
            st["accB_free"] = ev_tmp_free
            return ev_acc, lastmm

        if kind == "full":
            ev_acc, mmk = proj_pass(kp, ev_kp, accA, st["accA_free"])
            evk2 = act(k2_fm[:, base_tok:base_tok + 512], accA, waits=[ev_acc])
            ev_k2.append(evk2)
            ev_acc2, mmv = proj_pass(vp, ev_vp, accA, evk2)
            e1 = act(sqs, accA, AF.Copy, waits=[ev_acc2, st["sqs_free"]])
            st["sqs_free"] = None
            evvt2 = []
            for b in range(4):
                evvt2.append(tr(pbt[:, b * 128:(b + 1) * 128], sqs[:, b * 128:(b + 1) * 128],
                                ident_b[:], waits=[e1, state["pbt"]]))
            evv2 = act(v2_aug[(slice(None))].rearrange("p (bb c) -> p bb c", c=129)[:, 4 * t:4 * t + 4, 0:128],
                       pbt[:, 0:512].rearrange("p (bb c) -> p bb c", c=128),
                       waits=[evvt2[-1]])
            state["pbt"] = evv2
            st["sqs_free"] = evvt2[-1]
            ev_v2a.append(evv2)
            st["accA_free"] = e1
            last_full_proj = [mmk, mmv]
        else:
            ev_acc, mmq = proj_pass(kp, ev_qp, accA, st["accA_free"])
            evq2 = act(q2_fm[:, base_tok:base_tok + 512], accA, AF.Copy,
                       scale=Q2_SCALE, waits=[ev_acc])
            ev_q2.append(evq2)
            st["accA_free"] = evq2

        st["xn1T_free"] = ev_rawm[-1]
        if kind == "full":
            st["xn1T_free"] = last_full_proj[-1]
        else:
            st["xn1T_free"] = mmq
        st["r2bc_free"] = ev_n1[-1]

    P.flush()

    if BUILD_PHASES >= 2:
        opj = AB[:, 0:8192]
        ev_opj = dma(opj.rearrange("p (e c) -> p e c", c=1024),
                     op_d.rearrange("e h c -> h e c"), q="gpsimd")
        ptb = [AB[:, 8192:8448], AB[:, 8448:8704], AB[:, 8704:8960], AB[:, 8960:9216]]
        a2tm = AB[:, 9216:9344]
        a2T = AB[:, 9344:9472]
        sq1 = AB[:, 9472:9600]
        n2a = AFa[:, 1024:2048]
        nbcS = [AFa[:, 2048:2560], AFa[:, 512:1024]]
        nearS = AFa[:, 2560:2816]
        ev_hs = []
        ev_n2T = []
        ev_rw2 = []
        st["ptb1"] = [None, None, None, None]
        st["a2T_free"] = None
        st["nearS_free"] = None
        st["n2a_free"] = None
        st["sq1_free"] = None
        ev_k2l = ev_k2[-1]
        ev_v2l = ev_v2a[-1]
        ev_q2l = ev_q2[-1]

        from collections import deque
        bg1 = deque()

        def bg1_pop(n=1):
            for _ in range(n):
                if bg1:
                    bg1.popleft()()

        def mk_front1(i):
            nkb = 2 * i + 2
            far = nkb - 2
            nch = (far * 128 + 511) // 512
            cols = 48 * (i % 4)
            steps = []

            def s_chunk(c):
                def go():
                    qsl = q2_fm[:, i * 128:(i + 1) * 128]
                    tset = smalls[:, cols + 24:cols + 36]
                    gw = min(512, far * 128 - c * 512)
                    evm_ = mm(pb[5][:, 0:gw], qsl, k2_fm[:, c * 512:c * 512 + gw],
                              waits=[ev_q2l, ev_k2l, ev_pb[5]])
                    evr = dve(lambda e, o=tset[:, c:c + 1], i_=pb[5][:, 0:gw]:
                              e.tensor_reduce(out=o, in_=i_, axis=AX.X, op=ALU.max),
                              waits=[evm_, st.get("tset1_free_%d" % (i % 4))])
                    ev_pb[5] = evr
                    st.setdefault("evred1_%d" % i, []).append(evr)
                return go

            def s_tail():
                def go():
                    qsl = q2_fm[:, i * 128:(i + 1) * 128]
                    tset = smalls[:, cols + 24:cols + 36]
                    mcol = smalls[:, cols + 46:cols + 47]
                    evmn = mm(pb[5][:, 0:256], qsl, k2_fm[:, (nkb - 2) * 128:nkb * 128],
                              waits=[ev_q2l, ev_k2l, ev_pb[5]])
                    evadd = dve(lambda e, o=nearS, i_=pb[5][:, 0:256]: e.tensor_tensor(
                        out=o, in0=i_, in1=ca_t[:], op=ALU.add),
                        waits=[evmn, ev_ca, st["nearS_free"]])
                    ev_pb[5] = evadd
                    evrn = dve(lambda e, o=tset[:, nch:nch + 1], i_=nearS: e.tensor_reduce(
                        out=o, in_=i_, axis=AX.X, op=ALU.max), waits=[evadd])
                    st["nearS_free"] = evrn
                    evfm = dve(lambda e, o=mcol, i_=tset[:, 0:nch + 1]: e.tensor_reduce(
                        out=o, in_=i_, axis=AX.X, op=ALU.max, negate=True),
                        waits=[evrn] + st.get("evred1_%d" % i, []) + [st.get("mcol1_free_%d" % (i % 4))])
                    st["tset1_free_%d" % (i % 4)] = evfm
                    st["evfm1_%d" % i] = evfm
                return go

            def s_bcast():
                def go():
                    mcol = smalls[:, cols + 46:cols + 47]
                    nb = nbcS[i % 2]
                    evt_ = tr(pb[6][0:1, 0:128], mcol, ident_f[:],
                              waits=[st["evfm1_%d" % i], ev_pb[6]])
                    evrow = act(row512[0:1, 0:128], pb[6][0:1, 0:128],
                                waits=[evt_, st.get("row512_free")])
                    ev_pb[6] = evrow
                    st["mcol1_free_%d" % (i % 4)] = evt_
                    evbb = mm(pb[6][:, 0:128], ones1_f[0:1, 0:128], row512[0:1, 0:128],
                              waits=[evrow, ev_pb[6]])
                    st["row512_free"] = evbb
                    evnbca = act(nb[:, 0:128], pb[6][:, 0:128],
                                 waits=[evbb, st.get("nbc_free_%d" % (i % 2))])
                    evnbc = act(nb[:, 128:256], pb[6][:, 0:128], waits=[evnbca])
                    ev_pb[6] = evnbc
                    evnbc0 = dve(lambda e, o=nb[:, 256:384]: e.tensor_tensor(
                        out=o, in0=nb[:, 0:128], in1=ct_t[:, 0:128], op=ALU.add),
                        waits=[evnbc, ev_ct, st.get("nbc_free_%d" % (i % 2))])
                    evnbc1 = dve(lambda e, o=nb[:, 384:512]: e.tensor_tensor(
                        out=o, in0=nb[:, 0:128], in1=ct_t[:, 128:256], op=ALU.add),
                        waits=[evnbc, evnbc0])
                    st["nbc_done_%d" % i] = (evnbc, evnbc0, evnbc1)
                return go

            for c in range(nch):
                steps.append(s_chunk(c))
            steps.append(s_tail())
            steps.append(s_bcast())
            return steps

        bg1.extend(mk_front1(0))
        for i in range(NOWN):
            if i + 1 < NOWN:
                bg1.extend(mk_front1(i + 1))
            nkb = 2 * i + 2
            qsl = q2_fm[:, i * 128:(i + 1) * 128]
            cols = 48 * (i % 4)
            while st.get("nbc_done_%d" % i) is None and bg1:
                bg1.popleft()()
            evnbc, evnbc0, evnbc1 = st["nbc_done_%d" % i]
            nb = nbcS[i % 2]
            ev_ctx = None
            last_evs = None
            npair = nkb // 2
            pending1 = None

            def emit_av1(pav1):
                nonlocal ev_ctx
                kb0p, evep = pav1
                pt_ = ptb[(kb0p // 2) % 4]
                evav = None
                for u in range(2):
                    evav = mm(pb[4][:, 0:129], pt_[:, u * 128:(u + 1) * 128],
                              v2_aug[:, (kb0p + u) * 129:(kb0p + u + 1) * 129],
                              start=(kb0p + u == 0), stop=(kb0p + u == nkb - 1),
                              waits=[evep, ev_v2l, ev_m5,
                                     ev_pb[4] if kb0p + u == 0 else None])
                st["ptb1"][(kb0p // 2) % 4] = evav
                ev_ctx = evav

            for kp_ in range(npair):
                kb0 = 2 * kp_
                bsel = kp_ % 3
                evl = None
                for u in range(2):
                    evl = mm(pb[bsel][:, u * 128:(u + 1) * 128],
                             k2_fm[:, (kb0 + u) * 128:(kb0 + u + 1) * 128], qsl,
                             waits=[(ev_pb[bsel] if u == 0 else None), ev_k2l, ev_q2l])
                bg1_pop(1)
                if kp_ == npair - 1:
                    addt = nb[:, 256:512]; evadd_w = [evnbc1]
                else:
                    addt = nb[:, 0:256]; evadd_w = [evnbc]
                evs = dve(lambda e, o=nearS, i_=pb[bsel][:, 0:256], a=addt:
                          e.tensor_tensor(out=o, in0=i_, in1=a, op=ALU.add),
                          waits=[evl] + evadd_w + [st.get("nearS_free")])
                ev_pb[bsel] = evs
                last_evs = evs
                pt = ptb[kp_ % 4]
                eve = act(pt, nearS, AF.Exp, waits=[evs, st["ptb1"][kp_ % 4]])
                st["nearS_free"] = eve
                if pending1 is not None:
                    emit_av1(pending1)
                pending1 = (kb0, eve)
                bg1_pop(1)
            if pending1 is not None:
                emit_av1(pending1)
            bg1_pop(1)
            st["nbc_free_%d" % (i % 2)] = last_evs
            rl = smalls[:, cols + 47:cols + 48]
            evrl = dve(lambda e, o=rl: e.reciprocal(out=o, in_=pb[4][:, 128:129]), waits=[ev_ctx])
            eva2 = dve(lambda e, o=a2tm, r=rl: e.tensor_scalar_mul(
                out=o, in0=pb[4][:, 0:128], scalar1=r), waits=[evrl, st.get("a2tm_free")])
            ev_pb[4] = eva2
            evt2 = tr(pbt[:, 0:128], a2tm, ident_b[:], waits=[eva2, state["pbt"]])
            eva2T = act(a2T, pbt[:, 0:128], waits=[evt2, st["a2T_free"]])
            state["pbt"] = eva2T
            st["a2tm_free"] = evt2
            bg1_pop(2)
            hsblk = partial[:, i * D:(i + 1) * D]
            bankrot = 0
            ev_half_read = {0: None, 1: None}
            for e_ in range(NE):
                for h_ in range(2):
                    bk = bankrot % 2
                    bankrot += 1
                    evmo = mm(pb[bk][:, 0:512], a2T,
                              opj[:, e_ * 1024 + h_ * 512:e_ * 1024 + (h_ + 1) * 512],
                              waits=[eva2T, ev_opj, ev_pb[bk]])
                    evst = dve(lambda e, o=hsblk.rearrange("p (h c) -> p h c", c=512),
                               pp=pb[bk][:, 0:512], rw=rw_own[:, i * 8 + e_:i * 8 + e_ + 1],
                               hh=h_: e.scalar_tensor_tensor(
                        out=o[:, hh, :], in0=pp, scalar=rw, in1=o[:, hh, :],
                        op0=ALU.mult, op1=ALU.add),
                        waits=[evmo, ev_rw_ownd[-1], ev_half_read[h_]])
                    ev_pb[bk] = evst
                    ev_half_read[h_] = evst
                bg1_pop(1)
            ev_hs_done = ev_half_read[0]
            ev_hs_done2 = ev_half_read[1]
            ev_hs.append(ev_hs_done2)
            evm2, rstd_, nmr_, _r2, _e11 = ln_stats(hsblk, cols, [ev_hs_done, ev_hs_done2])
            evn2a = pool(lambda e, o=n2a, i_=hsblk, a=nmr_, r=rstd_: e.tensor_scalar(
                out=o, in0=i_, scalar1=a, scalar2=r, op0=ALU.add, op1=ALU.mult),
                waits=[evm2, st.get("n2a_free")])
            ev_tr2 = []
            for k in range(8):
                bsel = 6 if k % 2 == 0 else 3
                evt3 = tr(pb[bsel][:, (k // 2 % 2) * 256:(k // 2 % 2) * 256 + 128],
                          n2a[:, k * 128:(k + 1) * 128], ident_f[:],
                          waits=[evn2a, ev_pb[bsel]])
                evw = dve(lambda e, o=n2T.rearrange("p (k tt) -> p k tt", tt=2048)[:, k, i * 128:(i + 1) * 128],
                          i_=pb[bsel][:, (k // 2 % 2) * 256:(k // 2 % 2) * 256 + 128], kk=k:
                          e.tensor_scalar(out=o, in0=i_, scalar1=ln2w_c[:, kk:kk + 1],
                                          scalar2=ln2b_c[:, kk:kk + 1], op0=ALU.mult, op1=ALU.add),
                          waits=[evt3] + ev_ln2)
                ev_pb[bsel] = evw
                ev_tr2.append(evw)
            st["n2a_free"] = ev_tr2[-1]
            ev_n2T.append(ev_tr2[-1])
            bg1_pop(2)
            n2v = n2T.rearrange("p (k tt) -> p k tt", tt=2048)
            ev_sqm = []
            prev_sq1 = st.get("sq1_free")
            for k in range(8):
                esq = act(sq1, n2v[:, k, i * 128:(i + 1) * 128], AF.Square,
                          waits=[ev_tr2[k], prev_sq1])
                emm = mm(pb[3][0:1, 256:384] if False else pb[3][0:1, 384:512],
                         onesc_b[:], sq1, start=(k == 0), stop=(k == 7),
                         waits=[esq, ev_pb[3] if k == 0 else None])
                prev_sq1 = emm
                ev_sqm.append(emm)
            st["sq1_free"] = prev_sq1
            evss = dve(lambda e, o=row512b[0:1, 0:128], i_=pb[3][0:1, 384:512]:
                       e.tensor_scalar_max(out=o, in0=i_, scalar1=1e-24),
                       waits=[ev_sqm[-1], st.get("row512b_free")])
            ev_pb[3] = evss
            evs2 = act(row512b[0:1, 0:128], row512b[0:1, 0:128], AF.Ln, waits=[evss])
            evs3 = act(row512b[0:1, 0:128], row512b[0:1, 0:128], AF.Exp, scale=-0.5, waits=[evs2])
            evrt = tr(pb[6][:, 0:1], row512b[0:1, 0:128], ident_f[0:1, 0:1],
                      waits=[evs3, ev_pb[6]])
            st["row512b_free"] = evrt
            evrn2 = act(rn_cols[:, 0:1], pb[6][:, 0:1], waits=[evrt, st.get("rncols_free")])
            ev_pb[6] = evrn2
            ev_rawm2 = []
            for k in range(8):
                ev_rawm2.append(mm(pb[3][0:8, 0:128], sim2[:, k * 8:(k + 1) * 8],
                                   n2v[:, k, i * 128:(i + 1) * 128],
                                   start=(k == 0), stop=(k == 7),
                                   waits=[ev_tr2[k], ev_sim2, ev_pb[3] if k == 0 else None]))
            evlg2 = act(lgT[0:8, 0:128], pb[3][0:8, 0:128], waits=[ev_rawm2[-1], st.get("lgT_free")])
            ev_pb[3] = evlg2
            evlt2 = tr(pb[6][:, 4:12], lgT[0:8, 0:128], ident_f[0:8, 0:8],
                       waits=[evlg2, ev_pb[6]])
            evltm2 = act(lg_tm[:, 0:8], pb[6][:, 4:12], waits=[evlt2, st.get("lgtm_free")])
            ev_pb[6] = evltm2
            st["lgT_free"] = evlt2
            ev = gate_block(evltm2, lg_tm[:, 0:8], rn_cols[:, 0:1], evrn2,
                            sig2[:], rcol2[:], rw2_own[:, i * 8:(i + 1) * 8], cols,
                            extra=[ev_sig2, ev_rc2])
            ev_rw2.append(ev)
            st["rncols_free"] = ev
            st["lgtm_free"] = ev
            bg1_pop(2)
        while bg1:
            bg1.popleft()()

        P.flush()

    if BUILD_PHASES >= 3:
        slots = [AB[:, j * 4096:(j + 1) * 4096] for j in range(6)]
        _abdone = [st["xn1T_free"], ev_rw2[-1], st["sq1_free"], ev_n2T[-1],
                   st["lgT_free"], st["a2T_free"], state["pbt"]]
        slotev = [list(_abdone) for _ in range(6)]
        i0 = AB2[:, 0:4096]
        i1 = AB2[:, 4096:8192]
        ibuf = [i0, i1]
        ibuf_freeev = [None, None]
        ibuf_gelu = [[None] * 8, [None] * 8]
        n2v2 = n2T.rearrange("p (k tt) -> p k tt", tt=2048)
        ev_stt = None
        ev_w1ps = {0: None, 1: None}
        ev_p2 = []

        def w_dma(e_, what, slot_j, waits):
            semn = "LWA" if e_ % 2 == 0 else "LWB"
            if what < 2:
                srcd = w1_d[e_, what * 512:(what + 1) * 512, :]
            else:
                srcd = w2_d[e_, (what - 2) * 512:(what - 1) * 512, :]
            return dma(slots[slot_j].rearrange("p (k i) -> p k i", i=1024),
                       srcd.rearrange("(k p) i -> p k i", p=128),
                       waits=waits, q="gpsimd", sem=semn)

        def slotmap(e_):
            b = (4 * e_) % 6
            return [(b + j) % 6 for j in range(4)]

        ev_wdma = {}
        for w_ in range(4):
            ev_wdma[(0, w_)] = w_dma(0, w_, slotmap(0)[w_], slotev[slotmap(0)[w_]])

        def emit_w1_tile(e_, t_, ib, evw1a, evw1b):
            sm = slotmap(e_)
            last = None
            for j in range(8):
                bk = j % 2
                evm_ = None
                for k in range(8):
                    half = slots[sm[0]] if k < 4 else slots[sm[1]]
                    kk = k % 4
                    evm_ = mm(pb[bk][:, 0:512],
                              half[:, kk * 1024 + j * 128:kk * 1024 + (j + 1) * 128],
                              n2v2[:, k, t_ * 512:(t_ + 1) * 512],
                              start=(k == 0), stop=(k == 7),
                              waits=[evw1a if k < 4 else evw1b,
                                     ev_n2T[4 * t_ + 3],
                                     (ev_w1ps[bk] if k == 0 else None),
                                     (ev_pb[bk] if k == 0 else None)])
                slotev[sm[0]] = [evm_]
                slotev[sm[1]] = [evm_]
                evg = act(ibuf[ib][:, j * 512:(j + 1) * 512], pb[bk][:, 0:512], AF.Gelu,
                          waits=[evm_, ibuf_freeev[ib]])
                ev_w1ps[bk] = evg
                ev_pb[bk] = evg
                ibuf_gelu[ib][j] = evg
                last = evg
            return last

        def emit_w2_tile(e_, t_, ib, evw2a, evw2b):
            nonlocal ev_stt
            sm = slotmap(e_)
            lastmm = None
            for sub in range(4):
                blk = 4 * t_ + sub
                rwcol = rw2_own[:, blk * 8 + e_:blk * 8 + e_ + 1]
                for ch in range(2):
                    bk = 2 + ((2 * sub + ch) % 2)
                    evm_ = None
                    for j in range(8):
                        half = slots[sm[2]] if j < 4 else slots[sm[3]]
                        jj = j % 4
                        evm_ = mm(pb[bk][:, 0:512],
                                  ibuf[ib][:, j * 512 + sub * 128:j * 512 + (sub + 1) * 128],
                                  half[:, jj * 1024 + ch * 512:jj * 1024 + (ch + 1) * 512],
                                  start=(j == 0), stop=(j == 7),
                                  waits=[ibuf_gelu[ib][j],
                                         evw2a if j < 4 else evw2b,
                                         (ev_pb[bk] if j == 0 else None)])
                    slotev[sm[2]] = [evm_]
                    slotev[sm[3]] = [evm_]
                    lastmm = evm_
                    hv = partial[:, blk * D + ch * 512:blk * D + (ch + 1) * 512]
                    evst = dve(lambda e, o=hv, pp=pb[bk][:, 0:512], rw=rwcol:
                               e.scalar_tensor_tensor(out=o, in0=pp, scalar=rw, in1=o,
                                                      op0=ALU.mult, op1=ALU.add),
                               waits=[evm_, ev_rw2[blk], ev_stt, ev_hs[blk]])
                    ev_pb[bk] = evst
                    ev_stt = evst
            ibuf_freeev[ib] = lastmm
            return lastmm

        for e_ in range(NE):
            sm = slotmap(e_)
            if e_ + 1 < NE:
                smn = slotmap(e_ + 1)
                ev_wdma[(e_ + 1, 0)] = w_dma(e_ + 1, 0, smn[0], slotev[smn[0]])
                ev_wdma[(e_ + 1, 1)] = w_dma(e_ + 1, 1, smn[1], slotev[smn[1]])
            ea, eb = ev_wdma[(e_, 0)], ev_wdma[(e_, 1)]
            e2a, e2b = ev_wdma[(e_, 2)], ev_wdma[(e_, 3)]
            emit_w1_tile(e_, 0, 0, ea, eb)
            emit_w1_tile(e_, 1, 1, ea, eb)
            emit_w2_tile(e_, 0, 0, e2a, e2b)
            emit_w1_tile(e_, 2, 0, ea, eb)
            emit_w2_tile(e_, 1, 1, e2a, e2b)
            emit_w1_tile(e_, 3, 1, ea, eb)
            if e_ + 1 < NE:
                smn = slotmap(e_ + 1)
                ev_wdma[(e_ + 1, 2)] = w_dma(e_ + 1, 2, smn[2], slotev[smn[2]])
                ev_wdma[(e_ + 1, 3)] = w_dma(e_ + 1, 3, smn[3], slotev[smn[3]])
            emit_w2_tile(e_, 2, 0, e2a, e2b)
            emit_w2_tile(e_, 3, 1, e2a, e2b)
        ev_badd = []
        for blk in range(NOWN):
            ev_stt = dve(lambda e, o=partial[:, blk * D:(blk + 1) * D]:
                         e.tensor_tensor(out=o, in0=o, in1=bout_bc[:], op=ALU.add),
                         waits=[ev_stt, ev_bo2])
            ev_badd.append(ev_stt)
        ev_partial_done = ev_badd
        P.flush()

    if BUILD_PHASES >= 4:
        xnT_pair = AB[:, 0:2048]
        xnb3 = AB[:, 2048:3072]
        wqv = AB[:, 4096:12288]
        qaugA = AB[:, 12288:14336]
        ptw = [AB[:, 14336:15360], AB[:, 15360:16384]]
        ctxn3 = AB[:, 16384:17408]
        qp64 = AB[:, 17408:21504]
        qaugB = AB[:, 21504:23552]
        xA = AFa[:, 0:1024]
        xB = AFa[:, 1024:2048]
        scrF2 = AFa[:, 2048:3072]
        rlbF = AFa[:, 2560:3072]
        nearP3 = E(nc.sbuf_tensor("nearP3", [128, 384], F32))
        rl_t = [row512[0:1, 0:512], row512b[0:1, 0:512],
                lgT[0:1, 0:512], row512[0:1, 0:512]]
        m_all2 = m_all

        _p2done = [ev_stt, slotev[0][0], slotev[1][0], slotev[2][0], slotev[3][0],
                   slotev[4][0], slotev[5][0]]
        ev_wq = dma(wqv.rearrange("p (k d) -> p k d", d=1024),
                    Wq_d.rearrange("(k p) d -> p k d", p=128),
                    waits=_p2done, q="gpsimd", sem="LWA")
        ev_wo = dma(woutv.rearrange("p (k d) -> p k d", d=1024),
                    Wout_d.rearrange("(k p) d -> p k d", p=128),
                    waits=_p2done, q="gpsimd", sem="LWB")

        ohs_sm = AB[0:33, 23552:24320]
        ev_oh = dma(ohs_sm.rearrange("b (s k) -> b s k", k=256),
                    ohmT_d.rearrange("s b k -> b s k"),
                    waits=_p2done)
        ev_embr = [ev_emb, ev_embm]
        _dT_writes = []
        ev_pack_read = {4: ev_pb[4], 5: ev_pb[5]}
        for slot in range(3):
            for stg in range(16):
                bk = 4 + ((stg // 4) % 2)
                evm_ = None
                for cchunk in range(8):
                    s_val = stg * 8 + cchunk
                    evm_ = mm(pb[bk][:, ((stg % 4) * 8 + cchunk) * 16:((stg % 4) * 8 + cchunk + 1) * 16],
                              ohs_sm[0:33, slot * 256 + 127 - s_val:slot * 256 + 255 - s_val],
                              emb_ext[:], start=True, stop=True,
                              waits=[ev_oh] + ev_embr +
                                    ([ev_pack_read[bk]] if stg % 4 == 0 and cchunk == 0 else []))
                if stg % 4 == 3:
                    base = slot * 2048 + (stg - 3) * 128
                    epk = act(deltaT[:, base:base + 512], pb[bk][:, 0:512], waits=[evm_])
                    ev_pack_read[bk] = epk
                    _dT_writes.append(epk)
        ev_pb[4] = ev_pack_read[4]
        ev_pb[5] = ev_pack_read[5]
        ev_dT_done = _dT_writes[-1]
        ev_dA = {}
        _prev_cp = None
        for slot in range(3):
            for h in range(16):
                tT = deltaT.rearrange("p (s c h) -> p s c h", s=3, h=16)[:, slot, :, h]
                evt_ = tr(pbt[:, 0:128], tT, ident_b[:], waits=[ev_dT_done, state["pbt"]])
                evc_ = act(deltaA[:, (slot * 16 + h) * 128:(slot * 16 + h + 1) * 128],
                           pbt[:, 0:128], waits=[evt_, _prev_cp])
                state["pbt"] = evc_
                _prev_cp = evc_
                ev_dA[(slot, h)] = evc_
        st["qaugB_free"] = slotev[5][0]
        st["qaugB_free2"] = ev_oh

        from collections import deque
        bg = deque()

        def bg_pop(n=1):
            for _ in range(n):
                if not bg:
                    return
                gate, fn = bg[0]
                if gate is not None and st.get(gate) is None:
                    return
                bg.popleft()
                fn()

        st.update({
            "xbuf_free": [None, None], "xnb3_free": None, "xnT_free3": None,
            "qp_read": None, "qaug_free": [None, None], "mall_free": [None, None],
            "nearP3_free": None, "scrF_free_0": None, "scrF_free_1": None,
            "ebank": [None, None], "rlbF_free": None,
            "rlg_free": [st.get("row512_free"), st.get("row512b_free"),
                         st.get("lgT_free"), None],
            "ctxn_free": None, "p3stt": None,
        })
        ev_y = []

        def mk_front(pr):
            blks = [2 * pr, 2 * pr + 1]
            steps = []

            def s_x(bi):
                def go():
                    st["evx_%d" % blks[bi]] = dma(
                        (xA, xB)[bi], x_own[blks[bi] * 128:(blks[bi] + 1) * 128, :],
                        waits=[st["xbuf_free"][bi]] +
                              ([ev_rw2[-1], ev_n2T[-1]] if pr == 0 else []),
                        sem=("LDA" if bi == 0 else "LDB"))
                return go

            def s_ln(bi):
                def go():
                    blk = blks[bi]
                    cols = 48 * (blk % 4)
                    evm3, rstd3, nmr3, _r, _e = ln_stats((xA, xB)[bi], cols,
                                                         [st["evx_%d" % blk]])
                    st["evln_%d" % blk] = pool(
                        lambda e, o=xnb3, i_=(xA, xB)[bi], a=nmr3, r=rstd3: e.tensor_scalar(
                            out=o, in0=i_, scalar1=a, scalar2=r, op0=ALU.add, op1=ALU.mult),
                        waits=[evm3, st.get("xnb3_free")])
                return go

            def s_addx(bi):
                def go():
                    blk = blks[bi]
                    ev = dve(lambda e, o=partial[:, blk * D:(blk + 1) * D], xx=(xA, xB)[bi]:
                             e.tensor_tensor(out=o, in0=o, in1=xx, op=ALU.add),
                             waits=[st["evx_%d" % blk], ev_partial_done[blk],
                                    st["evln_%d" % blk], st["p3stt"]])
                    st["p3stt"] = ev
                    st["xbuf_free"][bi] = ev
                return go

            def s_xnT(bi):
                def go():
                    blk = blks[bi]
                    evtr3 = []
                    for k in range(8):
                        evtr3.append(tr(pbt[:, k * 128:(k + 1) * 128],
                                        xnb3[:, k * 128:(k + 1) * 128],
                                        ident_b[:],
                                        waits=[st["evln_%d" % blk], state["pbt"]]))
                    evcp3 = act(xnT_pair.rearrange("p (k tt) -> p k tt", tt=256)[:, :, bi * 128:(bi + 1) * 128],
                                pbt[:].rearrange("p (k tt) -> p k tt", tt=128),
                                waits=[evtr3[-1], st["xnT_free3"]])
                    state["pbt"] = evcp3
                    st["xnb3_free"] = evtr3[-1]
                    st["evxnT_%d_%d" % (pr, bi)] = evcp3
                return go

            def s_q(dk):
                def go():
                    evq_ = None
                    for k in range(8):
                        evq_ = mm(pb[6][:, 0:256], wqv[:, k * 1024 + dk * 128:k * 1024 + (dk + 1) * 128],
                                  xnT_pair.rearrange("p (k tt) -> p k tt", tt=256)[:, k, :],
                                  start=(k == 0), stop=(k == 7),
                                  waits=[ev_wq, st["evxnT_%d_1" % pr], (ev_pb[6] if k == 0 else None)])
                    evqe0 = act(qp64[0:64, (2 * dk) * 256:(2 * dk + 1) * 256],
                                pb[6][0:64, 0:256], AF.Copy,
                                scale=Q_SCALE, waits=[evq_, st["qp_read"]])
                    evqe = act(qp64[0:64, (2 * dk + 1) * 256:(2 * dk + 2) * 256],
                               pb[6][64:128, 0:256], AF.Copy,
                               scale=Q_SCALE, waits=[evqe0])
                    ev_pb[6] = evqe
                    st["xnT_free3"] = evq_
                    st["evq_%d_%d" % (pr, dk)] = evqe
                return go

            def s_passA_chunk(bi, h, c):
                def go():
                    blk = blks[bi]
                    i = blk
                    nkb = 2 * i + 2
                    nnear = min(3, nkb)
                    far = nkb - nnear
                    cols = 48 * (blk % 4)
                    tset = smalls[:, cols + 24 + (h % 2) * 12:cols + 36 + (h % 2) * 12]
                    qsl = qp64.rearrange("p (hh tt) -> p hh tt", tt=256)[0:64, h, bi * 128:(bi + 1) * 128]
                    gw = min(512, far * 128 - c * 512)
                    evm_ = mm(pb[6][:, 0:gw], qsl, kaug[0:64, c * 512:c * 512 + gw],
                              waits=[st["evq_%d_3" % pr], ev_kaug[-1], ev_pb[6]])
                    evr = dve(lambda e, o=tset[:, c:c + 1], i_=pb[6][:, 0:gw]:
                              e.tensor_reduce(out=o, in_=i_, axis=AX.X, op=ALU.max),
                              waits=[evm_, st.get("tset_free_%d_%d" % (blk % 4, h % 2))])
                    ev_pb[6] = evr
                    st.setdefault("evred_%d_%d" % (blk, h), []).append(evr)
                return go

            def s_passA_tail(bi, h):
                def go():
                    blk = blks[bi]
                    i = blk
                    nkb = 2 * i + 2
                    nnear = min(3, nkb)
                    far = nkb - nnear
                    cols = 48 * (blk % 4)
                    tset = smalls[:, cols + 24 + (h % 2) * 12:cols + 36 + (h % 2) * 12]
                    qsl = qp64.rearrange("p (hh tt) -> p hh tt", tt=256)[0:64, h, bi * 128:(bi + 1) * 128]
                    nch = (far * 128 + 511) // 512
                    nw = nnear * 128
                    evmn = mm(pb[6][:, 0:nw], qsl, kaug[0:64, far * 128:far * 128 + nw],
                              waits=[st["evq_%d_3" % pr], ev_kaug[-1], ev_pb[6]])
                    dsl = deltaA.rearrange("p (s h q) -> p s h q", h=16, q=128)[:, 3 - nnear:3, h, :]
                    evadd = dve(lambda e, o=nearP3[:, 0:nw], i_=pb[6][:, 0:nw], d_=dsl:
                                e.tensor_tensor(out=o, in0=i_, in1=d_, op=ALU.add),
                                waits=[evmn, ev_dA[(2, h)], st["nearP3_free"]])
                    ev_pb[6] = evadd
                    evrn3 = dve(lambda e, o=tset[:, nch:nch + 1], i_=nearP3[:, 0:nw]:
                                e.tensor_reduce(out=o, in_=i_, axis=AX.X, op=ALU.max),
                                waits=[evadd])
                    st["nearP3_free"] = evrn3
                    evfm = dve(lambda e, o=m_all[:, h:h + 1] if bi == 0 else smalls[:, 192 + h:193 + h],
                               i_=tset[:, 0:nch + 1]:
                               e.tensor_reduce(out=o, in_=i_, axis=AX.X, op=ALU.max, negate=True),
                               waits=[evrn3] + st.get("evred_%d_%d" % (blk, h), []) + [st["mall_free"][bi]])
                    st["tset_free_%d_%d" % (blk % 4, h % 2)] = evfm
                    st["evfm_%d_%d" % (blk, h)] = evfm
                return go

            def s_qaug(bi, g):
                def go():
                    blk = blks[bi]
                    qa = qaugA if bi == 0 else qaugB
                    mv = m_all if bi == 0 else smalls[:, 192:208]
                    qsrc = qp64.rearrange("p (hh tt) -> p hh tt", tt=256)[0:64, 4 * g:4 * g + 4,
                                                                          bi * 128:(bi + 1) * 128]
                    evq1 = act(qa[0:64, g * 512:(g + 1) * 512], qsrc,
                               waits=[st["evq_%d_3" % pr], st["qaug_free"][bi]] +
                                     ([st.get("qaugB_free"), st.get("qaugB_free2")] if bi == 1 else []))
                    evts = None
                    for u in range(4):
                        h = 4 * g + u
                        evts = tr(pb[6][0:1, u * 128:(u + 1) * 128], mv[:, h:h + 1],
                                  ident_f[:], waits=[st["evfm_%d_%d" % (blk, h)], ev_pb[6]])
                    evq2_ = act(qa[64:65, g * 512:(g + 1) * 512], pb[6][0:1, 0:512],
                                waits=[evts, evq1])
                    ev_pb[6] = evq2_
                    st["qaug_done_%d_%d" % (blk, g)] = evq2_
                    if g == 3:
                        st["mall_free"][bi] = evq2_
                        if bi == 1:
                            st["qp_read"] = evq1
                return go

            steps.append((None, s_x(0)))
            steps.append((None, s_ln(0)))
            steps.append((None, s_addx(0)))
            steps.append((None, s_xnT(0)))
            steps.append((None, s_x(1)))
            steps.append((None, s_ln(1)))
            steps.append((None, s_addx(1)))
            steps.append((None, s_xnT(1)))
            for dkp in range(4):
                steps.append((None, s_q(dkp)))
            for bi in range(2):
                blk = blks[bi]
                far = (2 * blk + 2) - min(3, 2 * blk + 2)
                nch = (far * 128 + 511) // 512
                for h in range(16):
                    for c in range(nch):
                        steps.append((None, s_passA_chunk(bi, h, c)))
                    steps.append((None, s_passA_tail(bi, h)))
                gate = ("passB_done_%d" % (blks[bi] - 2)) if pr >= 1 else None
                for g in range(4):
                    steps.append((gate, s_qaug(bi, g)))
            return steps


        def emit_passB(pr, bi):
            blk = 2 * pr + bi
            i = blk
            nkb = 2 * i + 2
            qa = qaugA if bi == 0 else qaugB
            ev_ptw = [[None, None], [None, None]]
            ev_avdone = [None] * 4
            ctx_free = [ev_pb[2], ev_pb[3], ev_pb[4], ev_pb[5]]
            pending_av = None

            def emit_av(pav):
                kbp, hwp, eep2 = pav
                evav = None
                for g in (2 * hwp, 2 * hwp + 1):
                    evav = mm(pb[2 + g][0:65, 0:512],
                              v_o_aug[:, kbp * 65:(kbp + 1) * 65],
                              ptw[hwp][:, (g % 2) * 512:(g % 2) * 512 + 512],
                              start=(kbp == 0), stop=(kbp == nkb - 1),
                              waits=[eep2[g % 2], ev_voa[-1], ev_m4] +
                                    ([ctx_free[g]] if kbp == 0 else []))
                    ev_avdone[g] = evav
                    ev_ptw[hwp][g % 2] = evav

            for kb in range(nkb):
                slot = kb - (nkb - 3)
                for hw_ in range(2):
                    evl_by = [None, None]
                    for g in (2 * hw_, 2 * hw_ + 1):
                        lb = g % 2
                        evl = mm(pb[lb][:, 0:512], kaug[:, kb * 128:(kb + 1) * 128],
                                 qa[0:65, g * 512:(g + 1) * 512],
                                 waits=[st["qaug_done_%d_%d" % (blk, g)],
                                        st["ebank"][lb], ev_pb[lb]])
                        ev_pb[lb] = evl
                        evl_by[lb] = evl
                    bg_pop(1)
                    ee2 = [None, None]
                    if slot >= 0:
                        dview = deltaT.rearrange("p (s q h) -> p s q h", q=128, h=16)
                        for lb in range(2):
                            dsl2 = dview[:, slot, :, 8 * hw_ + 4 * lb:8 * hw_ + 4 * lb + 4
                                         ].transpose([0, 2, 1])
                            evdd = dve(lambda e, o=scrF2[:, lb * 512:(lb + 1) * 512],
                                       i_=pall[:, lb * 512:(lb + 1) * 512], d_=dsl2:
                                       e.tensor_tensor(out=o, in0=i_, in1=d_, op=ALU.add),
                                       waits=[evl_by[lb], ev_dT_done,
                                              st["scrF_free_%d" % lb]] +
                                             ([st["rlbF_free"]] if lb == 1 else []))
                            st["ebank"][lb] = evdd
                            ee2[lb] = act(ptw[hw_][:, lb * 512:(lb + 1) * 512],
                                          scrF2[:, lb * 512:(lb + 1) * 512], AF.Exp,
                                          waits=[evdd, ev_ptw[hw_][lb]])
                            st["scrF_free_%d" % lb] = ee2[lb]
                    else:
                        for lb in range(2):
                            ee2[lb] = act(ptw[hw_][:, lb * 512:(lb + 1) * 512],
                                          pall[:, lb * 512:(lb + 1) * 512], AF.Exp,
                                          waits=[evl_by[lb], ev_ptw[hw_][lb]])
                            st["ebank"][lb] = ee2[lb]
                    if pending_av is not None:
                        emit_av(pending_av)
                    pending_av = (kb, hw_, ee2)
                    bg_pop(1)
                bg_pop(1)
            if pending_av is not None:
                emit_av(pending_av)
            st["qaug_free"][bi] = evl
            st["passB_done_%d" % blk] = evl

            ev_ctx_read = []
            evrl_g = [None] * 4
            for g in range(3):
                evrl_g[g] = act(rl_t[g], pall[64:65, 1024 + g * 512:1536 + g * 512],
                                AF.Ln, waits=[ev_avdone[g], st["rlg_free"][g]])
            bg_pop(2)
            evbb_g0 = None
            for g in range(4):
                if g == 3:
                    evrl_g[3] = act(rl_t[3], pall[64:65, 2560:3072], AF.Ln,
                                    waits=[ev_avdone[3], evbb_g0])
                evbb3 = mm(pb[6][0:64, 0:512], ones1_f[0:1, 0:64], rl_t[g],
                           waits=[evrl_g[g], ev_pb[6]])
                if g == 0:
                    evbb_g0 = evbb3
                st["rlg_free"][0 if g == 3 else g] = evbb3
                evcpb = act(rlbF[0:64, 0:512], pb[6][0:64, 0:512], AF.Exp, scale=-1.0,
                            waits=[evbb3, st["rlbF_free"], st["scrF_free_1"]] +
                                  ([ev_ctx_read[-1]] if ev_ctx_read else []))
                ev_pb[6] = evcpb
                for par in range(2):
                    ci = pb[2 + g].rearrange("p (u q) -> p u q", q=128)[0:64, par:4:2, :]
                    ri = rlbF.rearrange("p (u q) -> p u q", q=128)[0:64, par:4:2, :]
                    dst = ctxn3.rearrange("p (k q) -> p k q", q=128)[
                        par * 64:(par + 1) * 64, 2 * g:2 * g + 2, :]
                    evnrm = dve(lambda e, o=dst, i_=ci, r_=ri:
                                e.tensor_tensor(out=o, in0=i_, in1=r_, op=ALU.mult),
                                waits=[evcpb, st["ctxn_free"]])
                    ev_pb[2 + g] = evnrm
                    ev_ctx_read.append(evnrm)
                bg_pop(1)
            st["rlbF_free"] = ev_ctx_read[-1]
            ev_out_half = []
            evwm = None
            for ch in range(2):
                bk = ch
                for k in range(8):
                    evwm = mm(pb[bk][:, 0:512],
                              ctxn3.rearrange("p (k tt) -> p k tt", tt=128)[:, k, :],
                              woutv[:, k * 1024 + ch * 512:k * 1024 + (ch + 1) * 512],
                              start=(k == 0), stop=(k == 7),
                              waits=[ev_ctx_read[2 * (k // 2) + 1], ev_wo,
                                     (ev_pb[bk] if k == 0 else None)])
                hv = partial[:, i * D + ch * 512:i * D + (ch + 1) * 512]
                ev1 = dve(lambda e, o=hv, pp=pb[bk][:, 0:512]: e.tensor_tensor(
                    out=o, in0=o, in1=pp, op=ALU.add),
                    waits=[evwm, st["p3stt"]])
                ev_pb[bk] = ev1
                st["p3stt"] = ev1
                ev_out_half.append(ev1)
                bg_pop(1)
            st["ctxn_free"] = evwm
            evy = dma_out(y[i * 128:(i + 1) * 128, :], partial[:, i * D:(i + 1) * D],
                          waits=ev_out_half)
            ev_y.append(evy)

        bg.extend(mk_front(0))
        for pr in range(8):
            if pr + 1 < 8:
                bg.extend(mk_front(pr + 1))
            for bi in range(2):
                key = "qaug_done_%d_3" % (2 * pr + bi)
                while st.get(key) is None and bg:
                    gate, fn = bg.popleft()
                    fn()
                emit_passB(pr, bi)
        while bg:
            gate, fn = bg.popleft()
            fn()
        S("sync", lambda e: e.nop(), waits=[("OUT", P.cnt["OUT"])])
        P.flush()

    if debug and BUILD_PHASES < 4:
        wD = ([ev_kaug[-1], ev_voa[-1], ev_k2[-1] if ev_k2 else None,
               ev_v2a[-1], ev_q2[-1] if ev_q2 else None]
              + ev_rw_ownd[-1:] + ev_xnownd[-1:])
        d = []
        d.append(dma_out(dbg["d_kaug"], kaug[:], waits=wD))
        d.append(dma_out(dbg["d_voaug"], v_o_aug[:], waits=wD))
        d.append(dma_out(dbg["d_k2"], k2_fm, waits=wD))
        d.append(dma_out(dbg["d_v2aug"], v2_aug, waits=wD))
        d.append(dma_out(dbg["d_q2"], q2_fm, waits=wD))
        d.append(dma_out(dbg["d_rw"], rw_own[:], waits=wD))
        d.append(dma_out(dbg["d_xn"], partial[:], waits=wD))
        if debug and BUILD_PHASES < 3:
            d.append(dma_out(dbg["d_lgraw"], d_stash[:].rearrange("p (t c) -> p t c", c=36)[:, :, 0:32], waits=wD))
            d.append(dma_out(dbg["d_rn"], d_stash[:].rearrange("p (t c) -> p t c", c=36)[:, :, 32:36], waits=wD))
        if BUILD_PHASES < 2:
            d.append(dma_out(dbg["d_n1"], AB2[:, 10272:14368], waits=wD))
            d.append(dma_out(dbg["d_r2bc"], AB2[:, 14368:14880], waits=wD))
        if debug >= 2 and BUILD_PHASES >= 2:
            w2_ = [ev_hs[-1], ev_n2T[-1], ev_rw2[-1]]
            if BUILD_PHASES < 3:
                d.append(dma_out(dbg["d_hs"], partial[:], waits=w2_))
            d.append(dma_out(dbg["d_n2T"], n2T, waits=w2_))
            d.append(dma_out(dbg["d_rw2"], rw2_own[:], waits=w2_))
        if debug >= 3 and BUILD_PHASES >= 3:
            d.append(dma_out(dbg["d_partial"], partial[:], waits=[ev_partial_done[-1]]))
        d.append(dma_out(y[0:128, :], partial[:, 0:1024], waits=wD))
        S("sync", lambda e: e.nop(), waits=[d[-1]])
        P.flush()
        ctx.close()
        return nc, dbg

    ctx.close()
    return nc, dbg


_NC_CACHE = {}


def _prep_inputs(inputs, p):
    cst = _host_constants(p)
    base = {
        "Wq": inputs["L0_attn_Wq"], "Wkv": inputs["L0_attn_Wkv"],
        "Wout": inputs["L0_attn_Wout"], "bout": inputs["L0_attn_bout"],
        "rel_emb": inputs["L0_attn_rel_emb"],
        "ln1_w": inputs["L0_ffn_ln1_w"], "ln1_b": inputs["L0_ffn_ln1_b"],
        "ln2_w": inputs["L0_ffn_ln2_w"], "ln2_b": inputs["L0_ffn_ln2_b"],
        "smha_sim": inputs["L0_ffn_smha_sim"], "smha_gates": inputs["L0_ffn_smha_gates"],
        "q_proj": inputs["L0_ffn_q_proj"], "k_proj": inputs["L0_ffn_k_proj"],
        "v_proj": inputs["L0_ffn_v_proj"], "o_proj": inputs["L0_ffn_o_proj"],
        "moe_sim": inputs["L0_ffn_moe_sim"], "moe_gates": inputs["L0_ffn_moe_gates"],
        "w1": inputs["L0_ffn_w1"], "w2": inputs["L0_ffn_w2"],
        "ohm_T": cst["ohm_T"],
        "ca": cst["ca"], "ct": cst["ct"],
        "ident_f": cst["ident_f"], "ident_b": cst["ident_b"], "e_one": cst["e_one"],
    }
    return base


def make_in_maps(inputs, debug=0):
    x = np.ascontiguousarray(np.asarray(inputs["L0_attn_x"], dtype=np.float32))
    in_maps = []
    for c in range(8):
        b, p = c // 2, c % 2
        m = _prep_inputs(inputs, p)
        xb = x[b]
        m["x_full"] = np.ascontiguousarray(xb)
        m["x_own"] = np.ascontiguousarray(xb.reshape(NB, 128, D)[p::2].reshape(TOWN, D))
        in_maps.append(m)
    return in_maps


def kernel(**inputs):
    nc, _ = build_program(debug=0)
    in_maps = make_in_maps(inputs)
    res = run_bass_kernel_spmd(nc, in_maps, core_ids=list(range(8)))
    x = np.asarray(inputs["L0_attn_x"], dtype=np.float32)
    out = np.empty((4, N, D), dtype=np.float32)
    for c in range(8):
        b, p = c // 2, c % 2
        yc = np.asarray(res.results[c]["y"])
        out[b].reshape(NB, 128, D)[p::2] = yc.reshape(NOWN, 128, D)
    return out
